# version 35
# baseline (speedup 1.0000x reference)
import contextlib
import numpy as np
import ml_dtypes
import concourse.bass as bass
import concourse.mybir as mybir
from concourse.bass_utils import run_bass_kernel_spmd
from concourse.bass_core_rust import IndirectOffsetOnAxis

F32 = mybir.dt.float32
BF16 = mybir.dt.bfloat16
U8 = mybir.dt.uint8
I32 = mybir.dt.int32
AF = mybir.ActivationFunctionType
ALU = mybir.AluOpType
AX = mybir.AxisListType
bfloat16 = ml_dtypes.bfloat16

ENGINES = ("tensor", "vector", "scalar", "gpsimd", "sync")
DMA_QUEUES = ("gpsimd", "sync", "scalar")
NSEM_PER_QUEUE = 8


class _Op:
    __slots__ = ("idx", "engine", "fn", "reads", "writes", "queue", "tag",
                 "deps", "milestone", "mnum", "waits", "qsem", "self_inc")

    def __init__(self, idx, engine, fn, reads, writes, queue, tag):
        self.idx = idx
        self.engine = engine
        self.fn = fn
        self.reads = tuple(reads)
        self.writes = tuple(writes)
        self.queue = queue
        self.tag = tag
        self.deps = None
        self.milestone = False
        self.mnum = None
        self.waits = None
        self.qsem = None
        self.self_inc = False


class Prog:
    def __init__(self):
        self.ops = []

    def add(self, engine, fn, reads=(), writes=(), queue=None, tag=""):
        op = _Op(len(self.ops), engine, fn, reads, writes, queue, tag)
        self.ops.append(op)
        return op

    def dma(self, queue, fn, reads=(), writes=(), tag=""):
        return self.add(queue, fn, reads, writes, queue=queue, tag=tag)

    def dma_selfinc(self, queue, fn, reads=(), writes=(), tag=""):
        op = self.add(queue, fn, reads, writes, queue=queue, tag=tag)
        op.self_inc = True
        return op

    def _analyze(self):
        regions = {}
        for op in self.ops:
            deps = set()
            for r in op.reads:
                st = regions.get(r)
                if st and st[0] is not None:
                    deps.add(st[0])
            for w in op.writes:
                st = regions.get(w)
                if st:
                    if st[0] is not None:
                        deps.add(st[0])
                    deps.update(st[1])
            deps.discard(op)
            real = []
            for d in deps:
                if (d.queue is None and op.queue is None
                        and d.engine == "tensor" and op.engine == "tensor"):
                    continue
                real.append(d)
                d.milestone = True
            op.deps = real
            for w in op.writes:
                regions[w] = [op, []]
            for r in op.reads:
                st = regions.setdefault(r, [None, []])
                st[1].append(op)

    def _domain(self, op):
        if op.queue is not None:
            return ("d", op.queue, op.qsem)
        return ("e", op.engine)

    def _schedule(self):
        qcount = {q: 0 for q in DMA_QUEUES}
        for op in self.ops:
            if op.queue is not None:
                op.qsem = qcount[op.queue] % NSEM_PER_QUEUE
                qcount[op.queue] += 1
                op.milestone = True
        mcount = {}
        for op in self.ops:
            if op.milestone:
                dom = self._domain(op)
                mcount[dom] = mcount.get(dom, 0) + 1
                op.mnum = mcount[dom]
        know = {e: {} for e in ENGINES}
        snap = {}
        for op in self.ops:
            k = know[op.engine]
            need = {}
            for d in op.deps:
                dom = self._domain(d)
                if need.get(dom, (0, -1))[0] < d.mnum:
                    need[dom] = (d.mnum, d.idx)
            waits = []
            if op.queue is not None and op.mnum > 1:
                dom = self._domain(op)
                if k.get(dom, 0) < op.mnum - 1:
                    waits.append((dom, op.mnum - 1))
                    k[dom] = op.mnum - 1
            for dom, (num, didx) in sorted(need.items(), key=lambda x: x[1][1]):
                if k.get(dom, 0) >= num:
                    continue
                waits.append((dom, num))
                s = snap.get(didx)
                if s:
                    for d2, n2 in s.items():
                        if k.get(d2, 0) < n2:
                            k[d2] = n2
                if k.get(dom, 0) < num:
                    k[dom] = num
            op.waits = waits
            if op.milestone:
                snap[op.idx] = dict(k)
        self._final_counts = mcount

    def emit(self, blk, sems):
        self._analyze()
        self._schedule()
        per_engine = {e: [] for e in ENGINES}
        for op in self.ops:
            per_engine[op.engine].append(op)

        def run_stream(engine_name):
            def body(eng):
                for op in per_engine[engine_name]:
                    for dom, num in op.waits:
                        val = num * 16 if dom[0] == "d" else num
                        eng.wait_ge(sems[dom], val)
                    if op.self_inc:
                        op.fn(eng, sems[self._domain(op)])
                    else:
                        inst = op.fn(eng)
                        if op.milestone:
                            dom = self._domain(op)
                            inst.then_inc(sems[dom], 16 if dom[0] == "d" else 1)
                for dom, total in self._final_counts.items():
                    if dom[0] == "d" and dom[1] == engine_name:
                        eng.wait_ge(sems[dom], total * 16)
            return body

        blk.tensor(run_stream("tensor"))
        blk.vector(run_stream("vector"))
        blk.scalar(run_stream("scalar"))
        blk.gpsimd(run_stream("gpsimd"))
        blk.sync(run_stream("sync"))

    def all_domains(self):
        doms = [("e", e) for e in ENGINES]
        for q in DMA_QUEUES:
            for i in range(NSEM_PER_QUEUE):
                doms.append(("d", q, i))
        return doms


class Buf:
    __slots__ = ("name", "off", "nbytes", "ghosts")

    def __init__(self, name, off, nbytes, ghosts):
        self.name = name
        self.off = off
        self.nbytes = nbytes
        self.ghosts = ghosts


class Arena:
    def __init__(self, nbytes):
        self.cap = nbytes
        self.free_list = [(0, nbytes, [])]
        self.live = {}

    def alloc(self, name, nbytes, align=128):
        nbytes = (nbytes + align - 1) // align * align
        for i, (off, size, ghosts) in enumerate(self.free_list):
            aoff = (off + align - 1) // align * align
            pad = aoff - off
            if size - pad >= nbytes:
                newfree = []
                if pad:
                    newfree.append((off, pad, list(ghosts)))
                rem = size - pad - nbytes
                if rem:
                    newfree.append((aoff + nbytes, rem, list(ghosts)))
                self.free_list[i:i + 1] = newfree
                b = Buf(name, aoff, nbytes, list(ghosts))
                self.live[name] = b
                return b
        raise MemoryError(f"arena OOM for {name} ({nbytes}B); free={[(o, s) for o, s, _ in self.free_list]}; live={sorted((b.off, b.nbytes, n) for n, b in self.live.items())}")

    def free(self, name, ghost_regions):
        b = self.live.pop(name)
        self.free_list.append((b.off, b.nbytes, list(ghost_regions)))
        self.free_list.sort(key=lambda t: t[0])
        merged = []
        for off, size, g in self.free_list:
            if merged and merged[-1][0] + merged[-1][1] == off:
                merged[-1] = (merged[-1][0], merged[-1][1] + size,
                              merged[-1][2] + g)
            else:
                merged.append((off, size, list(g)))
        self.free_list = merged


def build_program(S=2048, D=1024, NH=16, HD=64, HF=2048, NE=8, dbg=False,
                  ret_prog=False):
    nst = S // 128
    ndt = D // 128
    nic = S // 512
    ndc = D // 512
    nht = HF // 128
    njt = nst
    nunits = NE + 1
    EPS = 1e-5

    nc = bass.Bass("TRN2", target_bir_lowering=False, debug=False)

    d_x = nc.dram_tensor("x", [S, D], F32, kind="ExternalInput").ap()
    d_wqkvT = nc.dram_tensor("wqkvT", [D, 3 * D], F32, kind="ExternalInput").ap()
    d_woutT = nc.dram_tensor("woutT", [D, D], F32, kind="ExternalInput").ap()
    d_lng = nc.dram_tensor("lng_rep", [128, D], F32, kind="ExternalInput").ap()
    d_lnb = nc.dram_tensor("lnb_rep", [128, D], F32, kind="ExternalInput").ap()
    d_bout = nc.dram_tensor("b_out", [1, D], F32, kind="ExternalInput").ap()
    d_cosq = nc.dram_tensor("cosq", [S, 32], F32, kind="ExternalInput").ap()
    d_sinq = nc.dram_tensor("sinq", [S, 32], F32, kind="ExternalInput").ap()
    d_cosk = nc.dram_tensor("cosk", [S, 32], F32, kind="ExternalInput").ap()
    d_sink = nc.dram_tensor("sink", [S, 32], F32, kind="ExternalInput").ap()
    d_ident = nc.dram_tensor("ident", [128, 128], F32, kind="ExternalInput").ap()
    d_rwq = nc.dram_tensor("rwq_pad", [D, 128], F32, kind="ExternalInput").ap()
    d_rbq = nc.dram_tensor("rbq_rep", [128, NE], F32, kind="ExternalInput").ap()
    d_w1 = nc.dram_tensor("w1_all", [nunits, D, HF], F32, kind="ExternalInput").ap()
    d_b1 = nc.dram_tensor("b1_all", [nunits, 128, nht], F32, kind="ExternalInput").ap()
    d_w2 = nc.dram_tensor("w2_all", [nunits, HF, D], F32, kind="ExternalInput").ap()
    d_b2 = nc.dram_tensor("b2_all", [nunits, D], F32, kind="ExternalInput").ap()
    d_y = nc.dram_tensor("y", [S, D], F32, kind="ExternalOutput").ap()
    d_xsp = nc.dram_tensor("xsp", [S, D], F32, kind="Internal").ap()
    d_tri = nc.dram_tensor("tri_u", [128, 128], F32, kind="ExternalInput").ap()
    d_itok = nc.dram_tensor("iota_tok", [128, S // 128], F32, kind="ExternalInput").ap()
    d_eidx = nc.dram_tensor("eidx", [1, NE], F32, kind="ExternalInput").ap()
    d_etag = nc.dram_tensor("etag", [1, NE], F32, kind="ExternalInput").ap()
    d_irep = nc.dram_tensor("iota_rep", [128, 128], F32, kind="ExternalInput").ap()
    CAP = [-(-min(S, (2 * S) // (k + 1)) // 128) * 128 + 128 for k in range(NE)]
    NSLOT = sum(CAP)
    NTILE = NSLOT // 128
    UOFF = [sum(CAP[:k]) for k in range(NE)]
    TK = [c // 128 for c in CAP]
    d_flatb = nc.dram_tensor("flatb", [S + 128, D], BF16, kind="Internal").ap()
    d_tokw = nc.dram_tensor("tokw", [NSLOT + 128, 2], F32, kind="Internal").ap()
    d_expout = nc.dram_tensor("expout", [NSLOT, D], BF16, kind="Internal").ap()
    d_tokw8 = d_tokw.bitcast(U8)
    d_expout8 = d_expout.bitcast(U8)
    d_w1pre = nc.dram_tensor("w1pre", [nunits, D, HF], BF16, kind="Internal").ap()
    d_w2pre = nc.dram_tensor("w2pre", [nunits, HF, D], BF16, kind="Internal").ap()
    d_w1pre8 = d_w1pre.bitcast(U8)
    d_w2pre8 = d_w2pre.bitcast(U8)
    d_woutpre = nc.dram_tensor("woutpre", [D, D], BF16, kind="Internal").ap()
    d_mixh = nc.dram_tensor("mixh", [S, D], F32, kind="Internal").ap()
    nst_ = S // 128
    ndt_ = D // 128
    if dbg:
        d_dbg = {
            "dbg_hT": nc.dram_tensor("dbg_hT", [128, ndt_ * S], BF16, kind="Internal").ap(),
            "dbg_qn": nc.dram_tensor("dbg_qn", [128, nst_ * 1024], BF16, kind="Internal").ap(),
            "dbg_kn": nc.dram_tensor("dbg_kn", [128, nst_ * 1024], BF16, kind="Internal").ap(),
            "dbg_vn": nc.dram_tensor("dbg_vn", [128, nst_ * 1040], BF16, kind="Internal").ap(),
            "dbg_negM": nc.dram_tensor("dbg_negM", [128, 1], F32, kind="Internal").ap(),
            "dbg_slab": nc.dram_tensor("dbg_slab", [128, nst_ * 512], BF16, kind="Internal").ap(),
            "dbg_oT": nc.dram_tensor("dbg_oT", [128, ndt_ * S], BF16, kind="Internal").ap(),
            "dbg_flatT": nc.dram_tensor("dbg_flatT", [128, ndt_ * S], BF16, kind="Internal").ap(),
            "dbg_w": nc.dram_tensor("dbg_w", [128, nst_ * NE], F32, kind="Internal").ap(),
            "dbg_mixed": nc.dram_tensor("dbg_mixed", [128, nst_ * D], F32, kind="Internal").ap(),
        }

    def dump(key, src_ap, reads):
        if dbg:
            P.dma("scalar", lambda e, o=d_dbg[key], s=src_ap: e.dma_start(o, s),
                  reads=reads, tag=key)

    P = Prog()
    ARENA_BYTES = 206 * 1024
    es = contextlib.ExitStack()
    arena_t = es.enter_context(nc.sbuf_tensor("arena", [128, ARENA_BYTES], U8))
    ar = arena_t.ap()
    arena = Arena(ARENA_BYTES)
    pbanks = [es.enter_context(nc.psum_tensor(f"pbk{i}", [128, 512], F32))
              for i in range(8)]
    pb = [t.ap() for t in pbanks]

    def pbf16(i, n=128):
        return pb[i].bitcast(BF16)[:, 0:n]

    sems = {}
    for dom in P.all_domains():
        sems[dom] = es.enter_context(
            nc.semaphore(name="s_" + "_".join(map(str, dom))))

    bufs = {}

    def alloc(name, nbytes, regs=()):
        b = arena.alloc(name, nbytes)
        bufs[name] = b
        if b.ghosts:
            g = list(b.ghosts)
            b.ghosts = []
            P.add("vector", lambda e: e.engine_nop(),
                  writes=g + [f"B:{name}"] + list(regs), tag=f"fence_{name}")
        return b

    def free(name, regs):
        bufs.pop(name)
        arena.free(name, regs)

    def view(name, dt=F32, cols=None):
        b = bufs[name]
        esz = 2 if dt == BF16 else 4
        n = b.nbytes // esz if cols is None else cols
        return ar[:, b.off:b.off + n * esz].bitcast(dt)

    def wreg(name, extra=()):
        b = bufs[name]
        g = b.ghosts
        b.ghosts = []
        return [f"B:{name}"] + list(extra) + g

    def mm(psum_ap, lhsT, rhs, start, stop, reads, writes, tp=None, tag=""):
        P.add("tensor",
              lambda e, o=psum_ap, a=lhsT, b=rhs, s1=start, s2=stop, t=tp:
              e.matmul(o, a, b, start=s1, stop=s2, tile_position=t),
              reads=reads, writes=writes, tag=tag)

    def transpose(psum_ap, in_ap, id_ap, reads, writes, tag=""):
        P.add("tensor",
              lambda e, o=psum_ap, i=in_ap, d=id_ap: e.transpose(o, i, d),
              reads=reads, writes=writes, tag=tag)

    def copy_on(engine, dst, src, reads, writes, tag=""):
        if engine == "scalar":
            P.add("scalar", lambda e, d=dst, s=src: e.copy(d, s),
                  reads=reads, writes=writes, tag=tag)
        else:
            P.add(engine, lambda e, d=dst, s=src: e.tensor_copy(d, s),
                  reads=reads, writes=writes, tag=tag)

    alloc("ones_f", 128 * 4)
    alloc("ones_bf", 128)
    alloc("id_bf", 128 * 2)
    alloc("id_f", 128 * 4)
    alloc("negM", 128)
    alloc("cols", 1024)
    _COLS = {}

    def col(name):
        if name not in _COLS:
            _COLS[name] = len(_COLS)
            assert len(_COLS) <= 256
        return view("cols")[:, _COLS[name]:_COLS[name] + 1]

    onesf_ap = None

    P.add("vector", lambda e, d=None: e.memset(view("ones_f"), 1.0),
          writes=wreg("ones_f"))
    P.add("vector", lambda e: e.memset(view("ones_bf", BF16, cols=1), 1.0),
          writes=wreg("ones_bf"))
    id_bf_ap = view("id_bf", BF16, cols=128)
    id_f_ap = view("id_f", cols=128)
    P.dma("gpsimd", lambda e, o=id_bf_ap: e.dma_start(o, d_ident),
          writes=wreg("id_bf"))
    P.dma("sync", lambda e, o=id_f_ap: e.dma_start(o, d_ident),
          writes=wreg("id_f"))
    onesf = view("ones_f")
    onesbf = view("ones_bf", BF16, cols=1)
    negM_ap = view("negM")[:, 0:1]

    alloc("bufA0", 4096); alloc("bufA1", 4096)
    alloc("bufA2", 4096); alloc("bufA3", 4096)
    alloc("bufB0", 4096); alloc("bufB1", 4096)
    alloc("bufB2", 4096); alloc("bufB3", 4096)
    alloc("sq0", 4096)
    alloc("htile0", 2048); alloc("htile1", 2048)
    alloc("htile2", 2048); alloc("htile3", 2048)
    alloc("hT", ndt * S * 2, [f"hT_{dt}_{st}" for dt in range(ndt) for st in range(nst)])
    alloc("q_nat", nst * 1024 * 2, [f"q_nat_{st}" for st in range(nst)])
    alloc("k_nat", nst * 1024 * 2, [f"k_nat_{st}" for st in range(nst)])
    alloc("rwq", ndt * 128 * 2)
    alloc("v_nat", nst * 1040 * 2, [f"v_nat_{st}" for st in range(nst)])
    alloc("lng", 4096); alloc("lnb", 4096)

    lng_ap = view("lng"); lnb_ap = view("lnb")
    P.dma("sync", lambda e, o=lng_ap: e.dma_start(o, d_lng), writes=wreg("lng"))
    P.dma("sync", lambda e, o=lnb_ap: e.dma_start(o, d_lnb), writes=wreg("lnb"))

    alloc("ctq", nst * 32 * 4); alloc("stq", nst * 32 * 4)
    alloc("ctk", nst * 32 * 4); alloc("stk", nst * 32 * 4)
    for nm, src in (("ctq", d_cosq), ("stq", d_sinq), ("ctk", d_cosk),
                    ("stk", d_sink)):
        dst = view(nm).rearrange("p (t f) -> p t f", f=32)
        P.dma("sync",
              lambda e, o=dst, s=src: e.dma_start(
                  o, s.rearrange("(t p) f -> p t f", p=128)),
              writes=wreg(nm))

    def wq_dma(fc):
        nm = f"wq{fc % 2}"
        dst = view(nm, BF16).rearrange("p (d f) -> p d f", f=512)
        src = d_wqkvT[:, fc * 512:(fc + 1) * 512].rearrange("(d p) f -> p d f", p=128)
        P.dma("gpsimd", lambda e, o=dst, s=src: e.dma_start(o, s),
              writes=wreg(nm, [f"wqbuf{fc % 2}"]), tag=f"wqdma{fc}")

    rwq_dst = view("rwq", BF16).rearrange("p (d f) -> p d f", f=128)
    P.dma("gpsimd",
          lambda e, o=rwq_dst: e.dma_start(
              o, d_rwq.rearrange("(d p) f -> p d f", p=128)),
          writes=wreg("rwq"))

    def ln_stats(src_ap, src_regions, pfx, sqn):
        c_s1, c_s2 = col(pfx + "s1"), col(pfx + "s2")
        c_m, c_msq = col(pfx + "m"), col(pfx + "msq")
        c_var, c_sqv, c_r = col(pfx + "var"), col(pfx + "sqv"), col(pfx + "r")
        sqv = view(sqn)
        P.add("vector", lambda e, s=src_ap, o=c_s1: e.reduce_sum(o, s, axis=AX.X),
              reads=src_regions, writes=[pfx + "s1"])
        P.add("scalar",
              lambda e, s=src_ap, o=sqv, a=c_s2: e.activation(
                  o, s, AF.Square, accum_out=a),
              reads=src_regions, writes=[f"B:{sqn}", pfx + "s2"])
        P.add("vector",
              lambda e, i=c_s1, o=c_m: e.tensor_scalar(o, i, 1.0 / D, None,
                                                       op0=ALU.mult),
              reads=[pfx + "s1"], writes=[pfx + "m"])
        P.add("vector", lambda e, i=c_m, o=c_msq: e.tensor_mul(o, i, i),
              reads=[pfx + "m"], writes=[pfx + "msq"])
        P.add("vector",
              lambda e, i=c_s2, j=c_msq, o=c_var: e.scalar_tensor_tensor(
                  o, i, 1.0 / D, j, op0=ALU.mult, op1=ALU.subtract),
              reads=[pfx + "s2", pfx + "msq"], writes=[pfx + "var"])
        P.add("vector",
              lambda e, o=c_var: e.tensor_scalar(o, o, EPS, None, op0=ALU.add),
              reads=[pfx + "var"], writes=[pfx + "var"])
        P.add("scalar",
              lambda e, i=c_var, o=c_sqv: e.activation(o, i, AF.Sqrt),
              reads=[pfx + "var"], writes=[pfx + "sqv"])
        P.add("vector", lambda e, i=c_sqv, o=c_r: e.reciprocal(o, i),
              reads=[pfx + "sqv"], writes=[pfx + "r"])
        return c_m, c_r, [pfx + "m", pfx + "r"]

    vone_ap = view("v_nat", BF16).rearrange("p (t c) -> p t c", c=65)[:, :, 64:65]
    P.add("vector", lambda e, o=vone_ap: e.memset(o, 1.0),
          writes=[f"v_nat_{st}" for st in range(nst)], tag="vones")
    hT_bf = view("hT", BF16)
    for st in range(nst):
        par = st % 4
        a, bB, ht, sqn = f"bufA{par}", f"bufB{par}", f"htile{par}", "sq0"
        a_ap, b_ap, h_ap = view(a), view(bB), view(ht, BF16)
        P.dma("sync",
              lambda e, o=a_ap, s=st: e.dma_start(o, d_x[s * 128:(s + 1) * 128, :]),
              writes=wreg(a), tag=f"xin{st}")
        cm, cr, mr = ln_stats(a_ap, [f"B:{a}"], f"p1{par}_", sqn)
        P.add("vector",
              lambda e, i=a_ap, o=b_ap, m=cm, r=cr: e.tensor_scalar(
                  o, i, m, r, op0=ALU.subtract, op1=ALU.mult),
              reads=[f"B:{a}"] + mr, writes=wreg(bB))
        cm2, cr2, mr2 = ln_stats(b_ap, [f"B:{bB}"], f"p2{par}_", sqn)
        P.add("vector",
              lambda e, o=b_ap, m=cm2, r=cr2: e.tensor_scalar(
                  o, o, m, r, op0=ALU.subtract, op1=ALU.mult),
              reads=[f"B:{bB}"] + mr2, writes=[f"B:{bB}"])
        P.add("vector", lambda e, o=b_ap, g=lng_ap: e.tensor_mul(o, o, g),
              reads=[f"B:{bB}", "B:lng"], writes=[f"B:{bB}"])
        P.add("gpsimd", lambda e, i=b_ap, o=h_ap, bb=lnb_ap: e.tensor_add(o, i, bb),
              reads=[f"B:{bB}", "B:lnb"], writes=wreg(ht))
        for dt in range(ndt):
            bank = dt % 4
            dst = hT_bf[:, dt * S + st * 128: dt * S + st * 128 + 128]
            transpose(pbf16(bank), h_ap[:, dt * 128:(dt + 1) * 128], id_bf_ap,
                      reads=[f"B:{ht}", "B:id_bf"], writes=[f"pbk{bank}"])
            copy_on("scalar", dst, pbf16(bank),
                    reads=[f"pbk{bank}"], writes=[f"hT_{dt}_{st}"])

    dump("dbg_hT", hT_bf, [f"hT_{dt}_{st}" for dt in range(ndt) for st in range(nst)])
    free("lng", ["B:lng"]); free("lnb", ["B:lnb"])
    free("bufA2", ["B:bufA2"]); free("bufA3", ["B:bufA3"])
    free("bufB2", ["B:bufB2"]); free("bufB3", ["B:bufB3"])
    free("htile2", ["B:htile2"]); free("htile3", ["B:htile3"])
    free("bufA0", ["B:bufA0"]); free("bufA1", ["B:bufA1"])
    alloc("wq0", ndt * 512 * 2); alloc("wq1", ndt * 512 * 2)
    wq_dma(0); wq_dma(1)

    alloc("rb0", 1024); alloc("rb1", 1024); alloc("rb2", 1024); alloc("rb3", 1024)
    rbv = [view(f"rb{i}").rearrange("p (h d) -> p h d", h=8) for i in range(4)]

    def rope_evac(bank, dst_name, st, half, ctn, stn):
        psap = pb[bank]
        q1 = psap.rearrange("p (h d) -> p h d", h=8)[:, :, 0:32]
        q2 = psap.rearrange("p (h d) -> p h d", h=8)[:, :, 32:64]
        cosb = view(ctn)[:, st * 32:(st + 1) * 32].unsqueeze(1).broadcast_to([128, 8, 32])
        sinb = view(stn)[:, st * 32:(st + 1) * 32].unsqueeze(1).broadcast_to([128, 8, 32])
        dst = view(dst_name, BF16)[:, st * 1024 + half * 512:
                                   st * 1024 + (half + 1) * 512]
        d1 = dst.rearrange("p (h d) -> p h d", h=8)[:, :, 0:32]
        d2 = dst.rearrange("p (h d) -> p h d", h=8)[:, :, 32:64]
        preg = [f"pbk{bank}"]
        treg = ["B:" + ctn, "B:" + stn]
        dreg = [f"{dst_name}_{st}"]
        P.add("vector", lambda e, o=rbv[0], i=q1, c=cosb: e.tensor_mul(o, i, c),
              reads=preg + treg, writes=["B:rb0"])
        P.add("vector", lambda e, o=rbv[1], i=q2, c=sinb: e.tensor_mul(o, i, c),
              reads=preg + treg, writes=["B:rb1"])
        P.add("vector", lambda e, o=rbv[2], i=q2, c=cosb: e.tensor_mul(o, i, c),
              reads=preg + treg, writes=["B:rb2"])
        P.add("vector", lambda e, o=rbv[3], i=q1, c=sinb: e.tensor_mul(o, i, c),
              reads=preg + treg, writes=["B:rb3"])
        P.add("vector", lambda e, o=d1: e.tensor_sub(o, rbv[0], rbv[1]),
              reads=["B:rb0", "B:rb1"], writes=dreg)
        P.add("vector", lambda e, o=d2: e.tensor_add(o, rbv[2], rbv[3]),
              reads=["B:rb2", "B:rb3"], writes=dreg)

    alloc("qn_acc", 16 * 4); alloc("kn_acc", 16 * 4); alloc("n_tmp", 16 * 4)

    def norm_tile(src_name, accn, st):
        acc_ap = view(accn)[:, 0:16]
        tmp_ap = view("n_tmp")[:, 0:16]
        sreg = [f"{src_name}_{st}"]
        src = view(src_name, BF16)[:, st * 1024:(st + 1) * 1024]
        sq_ap = view("sq0")
        P.add("vector", lambda e, s=src, o=sq_ap: e.tensor_mul(o, s, s),
              reads=sreg, writes=["B:sq0"])
        dst = acc_ap if st == 0 else tmp_ap
        dregs = [f"B:{accn}"] if st == 0 else ["R:ntmp"]
        P.add("vector",
              lambda e, o=dst, i=sq_ap: e.reduce_sum(
                  o.unsqueeze(2), i.rearrange("p (h d) -> p h d", h=16),
                  axis=AX.X),
              reads=["B:sq0"], writes=dregs)
        if st > 0:
            P.add("vector",
                  lambda e, a=acc_ap, t=tmp_ap: e.tensor_max(a, a, t),
                  reads=["R:ntmp", f"B:{accn}"], writes=[f"B:{accn}"])

    _pre_chunks = []
    for u in range(nunits):
        for c in range(8):
            _pre_chunks.append(("w1", u, c))
            _pre_chunks.append(("w2", u, c))

    def pump_prepass(n):
        for _ in range(n):
            if not _pre_chunks:
                return
            kind, u, c = _pre_chunks.pop(0)
            if kind == "w1":
                src = d_w1[u, c * 128:(c + 1) * 128, :]
                dst = d_w1pre[u, c * 128:(c + 1) * 128, :]
                P.dma("gpsimd", lambda e, o=dst, s=src: e.dma_start(o, s),
                      writes=[f"w1pre_{u}_{c}"], tag=f"wpre1_{u}_{c}")
            else:
                src = d_w2[u, c * 256:(c + 1) * 256, :]
                dst = d_w2pre[u, c * 256:(c + 1) * 256, :]
                P.dma("gpsimd", lambda e, o=dst, s=src: e.dma_start(o, s),
                      writes=[f"w2pre_{u}_{c}"], tag=f"wpre2_{u}_{c}")

    nfc = 3 * D // 512
    nqf = D // 512
    for fc in range(nfc):
        wbuf = f"wq{fc % 2}"
        wb_ap = view(wbuf, BF16)
        for st in range(nst):
            bank = 4 + (st % 4)
            for dt in range(ndt):
                mm(pb[bank],
                   hT_bf[:, dt * S + st * 128: dt * S + st * 128 + 128],
                   wb_ap[:, dt * 512:(dt + 1) * 512],
                   start=(dt == 0), stop=(dt == ndt - 1),
                   reads=[f"hT_{dt}_{st}", f"wqbuf{fc % 2}"],
                   writes=[f"pbk{bank}"], tag=f"qkv{fc}_{st}")
            if fc < nqf:
                rope_evac(bank, "q_nat", st, fc % nqf, "ctq", "stq")
            elif fc < 2 * nqf:
                rope_evac(bank, "k_nat", st, fc % nqf, "ctk", "stk")
            else:
                half = fc % nqf
                dst = view("v_nat", BF16)
                dst = dst[:, st * 1040 + half * 8 * 65:
                          st * 1040 + (half + 1) * 8 * 65]
                dst = dst.rearrange("p (h c) -> p h c", c=65)[:, :, 0:64]
                src_ps = pb[bank].rearrange("p (h c) -> p h c", c=64)
                P.add("scalar", lambda e, d=dst, s=src_ps: e.copy(d, s),
                      reads=[f"pbk{bank}"], writes=[f"v_nat_{st}"])
            if fc == 2:
                norm_tile("q_nat", "qn_acc", st)
            elif fc == 4:
                norm_tile("k_nat", "kn_acc", st)
            if st % 4 == 3:
                pump_prepass(1)
        if fc + 2 < nfc:
            wq_dma(fc + 2)

    dump("dbg_qn", view("q_nat", BF16), [f"q_nat_{st}" for st in range(nst)])
    dump("dbg_kn", view("k_nat", BF16), [f"k_nat_{st}" for st in range(nst)])
    dump("dbg_vn", view("v_nat", BF16), [f"v_nat_{st}" for st in range(nst)])
    free("hT", [f"hT_{dt}_{st}" for dt in range(ndt) for st in range(nst)])
    for nm in ("rb0", "rb1", "rb2", "rb3"):
        free(nm, [f"B:{nm}"])
    for nm in ("ctq", "stq", "ctk", "stk"):
        free(nm, [f"B:{nm}"])
    free("wq0", ["B:wq0", "wqbuf0"]); free("wq1", ["B:wq1", "wqbuf1"])

    c_qn, c_kn = col("qn1"), col("kn1")
    P.add("vector",
          lambda e, o=c_qn, i=view("qn_acc")[:, 0:16]: e.reduce_max(o, i, axis=AX.X),
          reads=["B:qn_acc"], writes=["c_qn1"])
    P.add("vector",
          lambda e, o=c_kn, i=view("kn_acc")[:, 0:16]: e.reduce_max(o, i, axis=AX.X),
          reads=["B:kn_acc"], writes=["c_kn1"])
    transpose(pb[0][0:1, 0:128], c_qn, id_f_ap,
              reads=["c_qn1", "B:id_f"], writes=["pbk0"])
    c_mq, c_mk, c_mm2, c_m, c_nm = (col("mq"), col("mk"), col("mm2"), col("m"),
                                    col("nm"))
    P.add("vector", lambda e, o=c_mq: e.reduce_max(o[0:1, :], pb[0][0:1, 0:128],
                                                   axis=AX.X),
          reads=["pbk0"], writes=["c_mq"])
    transpose(pb[1][0:1, 0:128], c_kn, id_f_ap,
              reads=["c_kn1", "B:id_f"], writes=["pbk1"])
    P.add("vector", lambda e, o=c_mk: e.reduce_max(o[0:1, :], pb[1][0:1, 0:128],
                                                   axis=AX.X),
          reads=["pbk1"], writes=["c_mk"])
    P.add("vector",
          lambda e, o=c_mm2, a=c_mq, b=c_mk: e.tensor_mul(o[0:1, :], a[0:1, :],
                                                          b[0:1, :]),
          reads=["c_mq", "c_mk"], writes=["c_mm2"])
    P.add("scalar",
          lambda e, o=c_m, i=c_mm2: e.activation(o[0:1, :], i[0:1, :], AF.Sqrt),
          reads=["c_mm2"], writes=["c_m"])
    P.add("vector",
          lambda e, o=c_nm, i=c_m: e.tensor_scalar(o[0:1, :], i[0:1, :], -1.0,
                                                   None, op0=ALU.mult),
          reads=["c_m"], writes=["c_nm"])
    mm(pb[2][:, 0:1], onesf[0:1, 0:128], c_nm[0:1, :], True, True,
       reads=["c_nm", "B:ones_f"], writes=["pbk2"], tag="negM_bc")
    P.add("vector", lambda e, o=negM_ap: e.tensor_copy(o, pb[2][:, 0:1]),
          reads=["pbk2"], writes=wreg("negM"))
    dump("dbg_negM", negM_ap, ["B:negM"])
    free("qn_acc", ["B:qn_acc"]); free("kn_acc", ["B:kn_acc"])
    free("n_tmp", ["R:ntmp"])

    alloc("qT", ndt * S * 2, [f"qT_{dt}_{st}" for dt in range(ndt) for st in range(nst)])
    qT_bf = view("qT", BF16)
    for st in range(nst):
        for dt in range(ndt):
            bank = dt % 4
            transpose(pbf16(bank),
                      view("q_nat", BF16)[:, st * 1024 + dt * 128:
                                          st * 1024 + (dt + 1) * 128],
                      id_bf_ap, reads=[f"q_nat_{st}", "B:id_bf"],
                      writes=[f"pbk{bank}"])
            copy_on("scalar",
                    qT_bf[:, dt * S + st * 128: dt * S + st * 128 + 128],
                    pbf16(bank), reads=[f"pbk{bank}"], writes=[f"qT_{dt}_{st}"])
    free("q_nat", [f"q_nat_{st}" for st in range(nst)])
    alloc("kT", ndt * S * 2, [f"kT_{dt}_{st}" for dt in range(ndt) for st in range(nst)])
    kT_bf = view("kT", BF16)
    for st in range(nst):
        for dt in range(ndt):
            bank = 4 + dt % 4
            transpose(pbf16(bank),
                      view("k_nat", BF16)[:, st * 1024 + dt * 128:
                                          st * 1024 + (dt + 1) * 128],
                      id_bf_ap, reads=[f"k_nat_{st}", "B:id_bf"],
                      writes=[f"pbk{bank}"])
            copy_on("scalar",
                    kT_bf[:, dt * S + st * 128: dt * S + st * 128 + 128],
                    pbf16(bank), reads=[f"pbk{bank}"], writes=[f"kT_{dt}_{st}"])
    free("k_nat", [f"k_nat_{st}" for st in range(nst)])

    alloc("oT", ndt * S * 2, [f"oT_{hp}_{ic}_{p}" for hp in range(ndt) for ic in range(nic) for p in (0, 1)])
    alloc("slabA", njt * 512 * 2, [f"slabA_{jt}" for jt in range(njt)])
    alloc("slabB", njt * 512 * 2, [f"slabB_{jt}" for jt in range(njt)])
    alloc("recbuf", 1024 * 4, ["rec0", "rec1"])
    alloc("bcscr", 1024 * 4, ["bcs0", "bcs1"])
    alloc("ostg", 1024 * 2, ["ostg0", "ostg1"])
    oT_bf = view("oT", BF16)
    slab_ap = {"slabA": view("slabA", BF16), "slabB": view("slabB", BF16)}
    v_bf = view("v_nat", BF16)
    rec_ap = view("recbuf", F32, cols=1024)
    bcs_ap = view("bcscr", F32, cols=1024)
    ostg_ap = view("ostg", BF16, cols=1024)

    def emit_spair(hp, ic, jt):
        for (hh, r0, bank, slab) in ((2 * hp, 0, jt % 2, "slabA"),
                                     (2 * hp + 1, 64, 2 + jt % 2, "slabB")):
            lhs = kT_bf[r0:r0 + 64, hp * S + jt * 128: hp * S + (jt + 1) * 128]
            rhs = qT_bf[r0:r0 + 64, hp * S + ic * 512: hp * S + (ic + 1) * 512]
            rds = [f"kT_{hp}_{jt}"] + [f"qT_{hp}_{4 * ic + q}" for q in range(4)]
            mm(pb[bank], lhs, rhs, True, True, reads=rds,
               writes=[f"pbk{bank}"], tp=(r0, 0), tag=f"S{hh}_{ic}_{jt}")
            dst = slab_ap[slab][:, jt * 512:(jt + 1) * 512]
            P.add("scalar",
                  lambda e, o=dst, bk=bank, nm=negM_ap: e.activation(
                      o, pb[bk], AF.Exp, bias=nm, scale=1.0),
                  reads=[f"pbk{bank}", "B:negM"],
                  writes=[f"{slab}_{jt}"], tag=f"exp{hh}_{ic}_{jt}")

    def emit_ovstep(hp, ic, jt):
        for (hh, slab, bank) in ((2 * hp, "slabA", 4), (2 * hp + 1, "slabB", 5)):
            vl = v_bf[:, jt * 1040 + hh * 65: jt * 1040 + hh * 65 + 65]
            sl = slab_ap[slab][:, jt * 512:(jt + 1) * 512]
            mm(pb[bank][0:65, :], vl, sl, jt == 0, jt == njt - 1,
               reads=[f"v_nat_{jt}", f"{slab}_{jt}"], writes=[f"pbk{bank}"],
               tp=(0, 0), tag=f"ov{hh}_{ic}_{jt}")

    def emit_chunk(hp, ic, hook=None):
        LOOK = 2
        for jt in range(njt):
            emit_spair(hp, ic, jt)
            if jt == 0 and hook is not None:
                hook()
            if jt >= LOOK:
                emit_ovstep(hp, ic, jt - LOOK)
        for jt in range(njt - LOOK, njt):
            emit_ovstep(hp, ic, jt)

    def emit_norm(hp, ic):
        P.add("vector",
              lambda e, o=rec_ap: e.reciprocal(o[64:65, 0:512], pb[4][64:65, :]),
              reads=["pbk4"], writes=["rec0"])
        P.add("vector",
              lambda e, o=rec_ap: e.reciprocal(o[64:65, 512:1024], pb[5][64:65, :]),
              reads=["pbk5"], writes=["rec1"])
        mm(pb[7][0:64, :], onesf[64:65, 0:64], rec_ap[64:65, 0:512], True, True,
           reads=["rec0", "B:ones_f"], writes=["pbk7"],
           tp=(64, 0), tag=f"bc{2 * hp}_{ic}")
        P.add("vector",
              lambda e, o=bcs_ap: e.tensor_copy(o[0:64, 0:512], pb[7][0:64, :]),
              reads=["pbk7"], writes=["bcs0"])
        dst = oT_bf[0:64, hp * S + ic * 512: hp * S + (ic + 1) * 512]
        P.add("vector",
              lambda e, o=dst, c=bcs_ap: e.tensor_mul(
                  o, pb[4][0:64, :], c[0:64, 0:512]),
              reads=["pbk4", "bcs0"],
              writes=[f"oT_{hp}_{ic}_0"], tag=f"onrm{2 * hp}_{ic}")
        mm(pb[6][0:64, :], onesf[64:65, 0:64], rec_ap[64:65, 512:1024], True, True,
           reads=["rec1", "B:ones_f"], writes=["pbk6"],
           tp=(64, 0), tag=f"bc{2 * hp + 1}_{ic}")
        P.add("vector",
              lambda e, o=bcs_ap: e.tensor_copy(o[0:64, 512:1024], pb[6][0:64, :]),
              reads=["pbk6"], writes=["bcs1"])
        par = ic % 2
        stg = ostg_ap[0:64, par * 512:(par + 1) * 512]
        P.add("vector",
              lambda e, o=stg, c=bcs_ap: e.tensor_mul(
                  o, pb[5][0:64, :], c[0:64, 512:1024]),
              reads=["pbk5", "bcs1"], writes=[f"ostg{par}"],
              tag=f"onrm{2 * hp + 1}_{ic}")
        dsto = oT_bf[64:128, hp * S + ic * 512: hp * S + (ic + 1) * 512]
        P.dma("sync", lambda e, o=dsto, i=stg: e.dma_start(o, i),
              reads=[f"ostg{par}"], writes=[f"oT_{hp}_{ic}_1"],
              tag=f"oshift{2 * hp + 1}_{ic}")

    pending = []
    for hp in range(NH // 2):
        for ic in range(nic):
            hook = None
            if pending:
                prev = pending.pop(0)
                hook = (lambda pv: (lambda: emit_norm(*pv)))(prev)
            emit_chunk(hp, ic, hook)
            if dbg and hp == 0 and ic == 0:
                dump("dbg_slab", slab_ap["slabA"],
                     [f"slabA_{jt}" for jt in range(njt)])
            pending.append((hp, ic))
            pump_prepass(4)
        if hp == NH // 2 - 1:
            free("kT", [f"kT_{dt}_{st}" for dt in range(ndt) for st in range(nst)])
            alloc("woutT", ndt * 1024 * 2)
            wout_dst = view("woutT", BF16).rearrange("p (d f) -> p d f", f=1024)
            P.dma("gpsimd",
                  lambda e, o=wout_dst: e.dma_start(
                      o, d_woutT.rearrange("(d p) f -> p d f", p=128)),
                  writes=wreg("woutT"))
    while pending:
        emit_norm(*pending.pop(0))
    pump_prepass(len(_pre_chunks))

    dump("dbg_oT", oT_bf, [f"oT_{hp}_{ic}_{p}" for hp in range(ndt)
                           for ic in range(nic) for p in (0, 1)])
    free("slabA", [f"slabA_{jt}" for jt in range(njt)])
    free("slabB", [f"slabB_{jt}" for jt in range(njt)])
    free("v_nat", [f"v_nat_{st}" for st in range(nst)])
    free("qT", [f"qT_{dt}_{st}" for dt in range(ndt) for st in range(nst)])
    free("recbuf", ["rec0", "rec1"])
    free("bcscr", ["bcs0", "bcs1"])
    free("ostg", ["ostg0", "ostg1"])

    alloc("bufA0", 4096); alloc("bufA1", 4096)
    alloc("bufA2", 4096); alloc("bufA3", 4096)
    alloc("bufB2", 4096); alloc("bufB3", 4096)
    alloc("htile2", 2048); alloc("htile3", 2048)
    alloc("bout", 4096)
    bout_ap = view("bout")[0:1, 0:D]
    P.dma("sync", lambda e, o=bout_ap: e.dma_start(o, d_bout), writes=wreg("bout"))
    alloc("flatT", ndt * S * 2, [f"flatT_{dt}_{st}" for dt in range(ndt) for st in range(nst)])
    flat_bf = view("flatT", BF16)
    wout_bf = view("woutT", BF16)

    h0z = view("htile0", BF16)
    P.add("vector", lambda e, o=h0z: e.memset(o, 0.0), writes=["B:htile0"],
          tag="flatb_z")
    P.dma("sync",
          lambda e, i=h0z: e.dma_start(d_flatb[S:S + 128, :], i),
          reads=["B:htile0"], writes=["flatb_trash"], tag="flatb_zw")

    def p4_front(st):
        par = st % 4
        a, bB, ht, sqn = f"bufA{par}", f"bufB{par}", f"htile{par}", "sq0"
        a_ap, b_ap, h_ap = view(a), view(bB), view(ht, BF16)
        P.dma("sync",
              lambda e, o=a_ap, s=st: e.dma_start(o, d_x[s * 128:(s + 1) * 128, :]),
              writes=wreg(a), tag=f"xre{st}")
        for dc in range(ndc):
            bank = (2 * st + dc) % 4
            for dt in range(ndt):
                mm(pb[bank],
                   oT_bf[:, dt * S + st * 128: dt * S + st * 128 + 128],
                   wout_bf[:, dt * 1024 + dc * 512: dt * 1024 + (dc + 1) * 512],
                   start=(dt == 0), stop=False,
                   reads=[f"oT_{dt}_{st // 4}_{p}" for p in (0, 1)] + ["B:woutT"],
                   writes=[f"pbk{bank}"], tag=f"aopj{st}_{dc}")
            mm(pb[bank], onesf[0:1, 0:128], bout_ap[:, dc * 512:(dc + 1) * 512],
               False, True, reads=["B:bout", "B:ones_f"], writes=[f"pbk{bank}"],
               tag=f"aob{st}_{dc}")
            copy_on("scalar", b_ap[:, dc * 512:(dc + 1) * 512], pb[bank],
                    reads=[f"pbk{bank}"], writes=[f"B:{bB}"])
        cm, cr, mr = ln_stats(b_ap, [f"B:{bB}"], f"p3{par}_", sqn)
        P.add("vector",
              lambda e, o=b_ap, m=cm, r=cr: e.tensor_scalar(
                  o, o, m, r, op0=ALU.subtract, op1=ALU.mult),
              reads=[f"B:{bB}"] + mr, writes=[f"B:{bB}"])
        P.add("gpsimd", lambda e, o=b_ap, i=a_ap: e.tensor_add(o, o, i),
              reads=[f"B:{bB}", f"B:{a}"], writes=[f"B:{bB}"])
        P.dma("sync",
              lambda e, i=b_ap, s=st: e.dma_start(d_xsp[s * 128:(s + 1) * 128, :], i),
              reads=[f"B:{bB}"], writes=[f"xsp_{st}"], tag=f"spill{st}")
        cm4, cr4, mr4 = ln_stats(b_ap, [f"B:{bB}"], f"p4{par}_", sqn)
        P.add("vector",
              lambda e, i=b_ap, o=h_ap, m=cm4, r=cr4: e.tensor_scalar(
                  o, i, m, r, op0=ALU.subtract, op1=ALU.mult),
              reads=[f"B:{bB}"] + mr4, writes=wreg(ht))
        P.dma("sync",
              lambda e, i=h_ap, s=st: e.dma_start(
                  d_flatb[s * 128:(s + 1) * 128, :], i),
              reads=[f"B:{ht}"], writes=[f"flatb_{st}"], tag=f"flath{st}")

    def p4_back(st):
        par = st % 4
        ht = f"htile{par}"
        h_ap = view(ht, BF16)
        for dt in range(ndt):
            bank = 4 + dt % 4
            transpose(pbf16(bank), h_ap[:, dt * 128:(dt + 1) * 128], id_bf_ap,
                      reads=[f"B:{ht}", "B:id_bf"], writes=[f"pbk{bank}"],
                      tag=f"ftp{st}_{dt}")
            copy_on("scalar",
                    flat_bf[:, dt * S + st * 128: dt * S + st * 128 + 128],
                    pbf16(bank), reads=[f"pbk{bank}"], writes=[f"flatT_{dt}_{st}"])

    for st in range(nst):
        p4_front(st)
        if st > 0:
            p4_back(st - 1)
    p4_back(nst - 1)

    free("oT", [f"oT_{hp}_{ic}_{p}" for hp in range(ndt) for ic in range(nic)
                for p in (0, 1)])
    free("woutT", ["B:woutT"])
    free("bout", ["B:bout"])

    alloc("w_sb", nst * NE * 4, [f"w_{tt}" for tt in range(nst)])
    alloc("m_sb", nst * NE * 4, [f"m_{tt}" for tt in range(nst)])
    alloc("rbq", NE * 4)
    alloc("lgt", NE * 4); alloc("prb", NE * 4); alloc("msk", NE * 4)
    w_ap = view("w_sb")
    m_ap = view("m_sb")
    rbq_ap = view("rbq")[:, 0:NE]
    lgt_ap = view("lgt")[:, 0:NE]
    prb_ap = view("prb")[:, 0:NE]
    msk_ap = view("msk")[:, 0:NE]
    rwq_bf = view("rwq", BF16)
    P.dma("sync", lambda e, o=rbq_ap: e.dma_start(o, d_rbq), writes=wreg("rbq"))
    c_nmx, c_rs, c_m1p, c_m2p = col("nmx"), col("rs"), col("m1p"), col("m2p")
    c_rrs = col("rrs")
    for tt in range(nst):
        bank = tt % 2
        for dt in range(ndt):
            mm(pb[bank][:, 0:128],
               flat_bf[:, dt * S + tt * 128: dt * S + tt * 128 + 128],
               rwq_bf[:, dt * 128:(dt + 1) * 128],
               start=(dt == 0), stop=(dt == ndt - 1),
               reads=[f"flatT_{dt}_{tt}", "B:rwq"], writes=[f"pbk{bank}"],
               tag=f"rtr{tt}")
        P.add("vector",
              lambda e, bk=bank, o=lgt_ap, b=rbq_ap: e.tensor_add(
                  o, pb[bk][:, 0:NE], b),
              reads=[f"pbk{bank}", "B:rbq"], writes=["B:lgt"])
        P.add("vector", lambda e, o=c_nmx, i=lgt_ap: e.reduce_max(
            o, i, axis=AX.X, negate=True),
              reads=["B:lgt"], writes=["c_nmx"])
        P.add("scalar",
              lambda e, o=prb_ap, i=lgt_ap, b=c_nmx: e.activation(
                  o, i, AF.Exp, bias=b),
              reads=["B:lgt", "c_nmx"], writes=["B:prb"])
        P.add("vector", lambda e, o=c_rs, i=prb_ap: e.reduce_sum(o, i, axis=AX.X),
              reads=["B:prb"], writes=["c_rs"])
        P.add("vector", lambda e, o=c_rrs, i=c_rs: e.reciprocal(o, i),
              reads=["c_rs"], writes=["c_rrs"])
        P.add("vector",
              lambda e, o=prb_ap, s=c_rrs: e.tensor_scalar(o, o, s, None,
                                                           op0=ALU.mult),
              reads=["B:prb", "c_rrs"], writes=["B:prb"])
        P.add("vector", lambda e, o=c_m1p, i=prb_ap: e.reduce_max(o, i, axis=AX.X),
              reads=["B:prb"], writes=["c_m1p"])
        P.add("vector",
              lambda e, o=msk_ap, i=prb_ap, m=c_m1p: e.tensor_scalar(
                  o, i, m, -2.0, op0=ALU.is_equal, op1=ALU.mult),
              reads=["B:prb", "c_m1p"], writes=["B:msk"])
        P.add("vector", lambda e, o=msk_ap, i=prb_ap: e.tensor_add(o, o, i),
              reads=["B:msk", "B:prb"], writes=["B:msk"])
        P.add("vector", lambda e, o=c_m2p, i=msk_ap: e.reduce_max(o, i, axis=AX.X),
              reads=["B:msk"], writes=["c_m2p"])
        mdst = m_ap[:, tt * NE:(tt + 1) * NE]
        P.add("vector",
              lambda e, o=mdst, i=prb_ap, m=c_m2p: e.tensor_scalar(
                  o, i, m, None, op0=ALU.is_ge),
              reads=["B:prb", "c_m2p"], writes=[f"m_{tt}"])
        wdst = w_ap[:, tt * NE:(tt + 1) * NE]
        P.add("vector",
              lambda e, o=wdst, i=prb_ap, m=mdst: e.tensor_mul(o, i, m),
              reads=["B:prb", f"m_{tt}"], writes=[f"w_{tt}"])
    dump("dbg_flatT", flat_bf, [f"flatT_{dt}_{st}" for dt in range(ndt)
                                for st in range(nst)])
    dump("dbg_w", w_ap, [f"w_{tt}" for tt in range(nst)])
    free("rwq", ["B:rwq"])
    free("lgt", ["B:lgt"]); free("prb", ["B:prb"]); free("msk", ["B:msk"])
    free("rbq", ["B:rbq"])

    BIGS = 2.0e6
    alloc("tri_sb", 512)
    alloc("iota_sb", nst * 4)
    alloc("erow", 16 * 4)
    alloc("srt", 64 * 4)
    alloc("sig", 16 * 4)
    alloc("psf", nst * NE * 4)
    alloc("bas", nst * NE * 4)
    alloc("bas3", nst * NE * 4)
    alloc("bsc", NE * 4)
    alloc("irep", 512)
    alloc("pbuf", 512)
    alloc("cw", 16 * 4)
    alloc("stg", 16 * 4)
    alloc("sent", (NTILE + 1) * 2 * 4)
    tri_ap = view("tri_sb", F32, cols=128)
    irep_ap = view("irep", F32, cols=128)
    pbuf_ap = view("pbuf", F32, cols=128)
    cw_ap = view("cw", F32, cols=16)
    psf_ap = view("psf", F32, cols=nst * NE)
    bas_i = view("bas", I32, cols=nst * NE)
    bas3_i = view("bas3", I32, cols=nst * NE)
    bsc_ap = view("bsc", F32, cols=NE)
    iota_ap = view("iota_sb", F32, cols=nst)
    erow_ap = view("erow", F32, cols=16)
    srt_ap = view("srt", F32, cols=64)
    sig_i = view("sig", I32, cols=16)
    stg_ap = view("stg", F32, cols=16)
    sent_ap = view("sent", F32, cols=(NTILE + 1) * 2)
    P.dma("sync", lambda e, o=tri_ap: e.dma_start(o, d_tri), writes=wreg("tri_sb"))
    P.dma("sync", lambda e, o=irep_ap: e.dma_start(o, d_irep), writes=wreg("irep"))
    P.dma("sync", lambda e, o=iota_ap: e.dma_start(o, d_itok), writes=wreg("iota_sb"))
    P.dma("sync", lambda e, o=erow_ap[0:1, 0:NE]: e.dma_start(o, d_eidx),
          writes=wreg("erow"))
    P.dma("sync", lambda e, o=erow_ap[0:1, 8:8 + NE]: e.dma_start(o, d_etag),
          writes=["B:erow"])
    P.add("vector", lambda e, o=sent_ap: e.memset(o, 1.0e5), writes=wreg("sent"))
    P.dma("scalar",
          lambda e, o=d_tokw.rearrange("(r p) c -> p r c", p=128),
          i=sent_ap.rearrange("p (r c) -> p r c", c=2): e.dma_start(o, i),
          reads=["B:sent"],
          writes=["tokw"] + [f"tokw_e{ee}" for ee in range(NE)], tag="tokw_init")

    for tt in range(nst):
        mm(pb[6][0:1, 0:NE], onesf[:, 0:1], m_ap[:, tt * NE:(tt + 1) * NE],
           start=(tt == 0), stop=(tt == nst - 1),
           reads=[f"m_{tt}", "B:ones_f"], writes=["pbk6"], tag=f"lds{tt}")
    LD, OH, TMP, XOF, MK, SGF = 0, 8, 16, 24, 40, 48
    P.add("vector", lambda e, o=srt_ap[0:1, LD:LD + NE]: e.tensor_copy(
        o, pb[6][0:1, 0:NE]), reads=["pbk6"], writes=["srt"])
    P.add("vector",
          lambda e, o=srt_ap[0:1, LD:LD + NE], t=erow_ap[0:1, 8:8 + NE]:
          e.tensor_add(o, o, t),
          reads=["srt", "B:erow"], writes=["srt"])
    for k in range(NE):
        P.add("vector",
              lambda e, o=srt_ap[0:1, MK:MK + 1], i=srt_ap[0:1, LD:LD + NE]:
              e.reduce_max(o, i, axis=AX.X),
              reads=["srt"], writes=["srt"])
        P.add("vector",
              lambda e, o=srt_ap[0:1, OH:OH + NE], i=srt_ap[0:1, LD:LD + NE],
              m=srt_ap[0:1, MK:MK + 1]: e.tensor_scalar(
                  o, i, m, None, op0=ALU.is_equal),
              reads=["srt"], writes=["srt"])
        P.add("vector",
              lambda e, o=srt_ap[0:1, TMP:TMP + NE], a=srt_ap[0:1, OH:OH + NE],
              b=erow_ap[0:1, 0:NE]: e.tensor_mul(o, a, b),
              reads=["srt", "B:erow"], writes=["srt"])
        P.add("vector",
              lambda e, o=srt_ap[0:1, SGF + k:SGF + k + 1],
              i=srt_ap[0:1, TMP:TMP + NE]: e.reduce_sum(o, i, axis=AX.X),
              reads=["srt"], writes=["srt"])
        if k == 0:
            P.add("vector",
                  lambda e, o=srt_ap[0:1, XOF:XOF + NE],
                  a=srt_ap[0:1, OH:OH + NE], u=float(UOFF[0]): e.tensor_scalar(
                      o, a, u, None, op0=ALU.mult),
                  reads=["srt"], writes=["srt"])
        else:
            P.add("vector",
                  lambda e, o=srt_ap[0:1, XOF:XOF + NE],
                  a=srt_ap[0:1, OH:OH + NE], u=float(UOFF[k]):
                  e.scalar_tensor_tensor(o, a, u, o, op0=ALU.mult, op1=ALU.add),
                  reads=["srt"], writes=["srt"])
        P.add("vector",
              lambda e, o=srt_ap[0:1, LD:LD + NE], a=srt_ap[0:1, OH:OH + NE]:
              e.scalar_tensor_tensor(o, a, -1.0e9, o, op0=ALU.mult, op1=ALU.add),
              reads=["srt"], writes=["srt"])
    P.add("vector",
          lambda e, o=sig_i[0:1, 0:NE], i=srt_ap[0:1, SGF:SGF + NE]:
          e.tensor_copy(o, i),
          reads=["srt"], writes=["sig"])

    for tt in range(nst):
        mm(pb[7][:, 0:NE], onesf[0:1, 0:128], srt_ap[0:1, XOF:XOF + NE],
           start=True, stop=False, reads=["srt", "B:ones_f"], writes=["pbk7"],
           tag=f"sb{tt}")
        for tp_ in range(tt):
            mm(pb[7][:, 0:NE], onesf, m_ap[:, tp_ * NE:(tp_ + 1) * NE],
               start=False, stop=False,
               reads=[f"m_{tp_}", "B:ones_f"], writes=["pbk7"], tag=f"sc{tt}_{tp_}")
        mm(pb[7][:, NE:2 * NE], tri_ap, m_ap[:, tt * NE:(tt + 1) * NE],
           start=False, stop=True,
           reads=[f"m_{tt}", "B:tri_sb"], writes=["pbk7"], tag=f"st{tt}")
        P.add("vector",
              lambda e, o=bsc_ap[0:1, 0:NE]: e.tensor_scalar(
                  o, pb[7][0:1, 0:NE], 8.0, None, op0=ALU.mult),
              reads=["pbk7"], writes=["B:bsc"])
        P.add("vector",
              lambda e, o=bas_i[0:1, tt * NE:(tt + 1) * NE],
              i=bsc_ap[0:1, 0:NE]: e.tensor_copy(o, i),
              reads=["B:bsc"], writes=[f"bas_{tt}"])
        P.add("vector",
              lambda e, o=bsc_ap[0:1, 0:NE]: e.tensor_scalar(
                  o, pb[7][0:1, 0:NE], 2048.0, None, op0=ALU.mult),
              reads=["pbk7", f"bas_{tt}"], writes=["B:bsc"])
        P.add("vector",
              lambda e, o=bas3_i[0:1, tt * NE:(tt + 1) * NE],
              i=bsc_ap[0:1, 0:NE]: e.tensor_copy(o, i),
              reads=["B:bsc"], writes=[f"bas_{tt}"])
        pslice = psf_ap[:, tt * NE:(tt + 1) * NE]
        P.add("vector",
              lambda e, o=pslice, m=m_ap[:, tt * NE:(tt + 1) * NE]:
              e.scalar_tensor_tensor(o, m, 1000.0, pb[7][:, NE:2 * NE],
                                     op0=ALU.mult, op1=ALU.add),
              reads=[f"m_{tt}", "pbk7"], writes=[f"psf_{tt}"])
        P.add("vector",
              lambda e, o=pslice: e.tensor_scalar(o, o, -1000.0, None,
                                                  op0=ALU.add),
              reads=[f"psf_{tt}"], writes=[f"psf_{tt}"])
        stg3 = stg_ap.rearrange("p (e c) -> p e c", c=2)
        P.add("vector",
              lambda e, o=stg3[:, :, 0:1],
              i=iota_ap[:, tt:tt + 1].unsqueeze(1).broadcast_to([128, NE, 1]):
              e.tensor_copy(o, i),
              reads=["B:iota_sb"], writes=["B:stg"])
        P.add("vector",
              lambda e, o=stg3[:, :, 1:2],
              i=w_ap[:, tt * NE:(tt + 1) * NE].unsqueeze(2): e.tensor_copy(o, i),
              reads=[f"w_{tt}"], writes=["B:stg"])
        for ee in range(NE):
            P.add("vector",
                  lambda e, o=pbuf_ap, m=psf_ap[:, tt * NE + ee:tt * NE + ee + 1]:
                  e.tensor_scalar(o, irep_ap, m, None, op0=ALU.is_equal),
                  reads=[f"psf_{tt}", "B:irep"], writes=["B:pbuf"],
                  tag=f"pb{tt}_{ee}")
            mm(pb[6][:, 2 * ee:2 * ee + 2], pbuf_ap,
               stg_ap[:, 2 * ee:2 * ee + 2],
               start=(ee == 0), stop=(ee == NE - 1),
               reads=["B:pbuf", "B:stg"], writes=["pbk6"], tag=f"cm{tt}_{ee}")
        P.add("vector", lambda e, o=cw_ap: e.tensor_copy(o, pb[6][:, 0:16]),
              reads=["pbk6"], writes=["B:cw"])
        for ee in range(NE):
            def bwfn(e_, sem, _tt=tt, _ee=ee,
                     _src=cw_ap[:, 2 * ee:2 * ee + 2].bitcast(U8)):
                with e_.register() as r:
                    e_.load(r, bas_i[0:1, _tt * NE + _ee:_tt * NE + _ee + 1])
                    sv = e_.compute_val(r, donate=True)
                    dst = bass.AP(d_tokw8.tensor, sv, [[8, 128], [1, 8]])
                    e_.dma_start(dst, _src).then_inc(sem, 16)
            P.dma_selfinc("gpsimd", bwfn,
                          reads=[f"bas_{tt}", "B:cw", f"tokw_e{ee}"],
                          writes=[f"tokw_e{ee}"], tag=f"bw{tt}_{ee}")

    alloc("h1", 8 * S * 2, [f"h1_{h}_{t}" for h in range(8) for t in range(nic)])
    alloc("rw1a", ndt * 512 * 2, ["rw1buf0"]); alloc("rw1b", ndt * 512 * 2, ["rw1buf1"])
    alloc("rw1c", ndt * 512 * 2, ["rw1buf2"])
    alloc("rw2b", 8 * 1024 * 2, ["rw2buf"])
    alloc("b1t0", 16 * 4, ["b1buf0"]); alloc("b1t1", 16 * 4, ["b1buf1"])
    alloc("b2r0", 4096, ["b2buf0"]); alloc("b2r1", 4096, ["b2buf1"])
    h1_bf = view("h1", BF16)
    rw1_names = ["rw1a", "rw1b", "rw1c"]
    rw1_aps = [view(n, BF16) for n in rw1_names]
    rw2_ap = view("rw2b", BF16)
    b1_aps = [view("b1t0")[:, 0:nht], view("b1t1")[:, 0:nht]]
    b2_aps = [view("b2r0")[0:1, 0:D], view("b2r1")[0:1, 0:D]]
    nq_per_unit = HF // 512

    def rw1_dma(u, qi):
        slot = qi % 3
        nm = rw1_names[slot]
        dst = rw1_aps[slot].rearrange("p (d f) -> p d f", f=512)
        src = d_w1pre[u, :, qi * 512:(qi + 1) * 512].rearrange(
            "(d p) f -> p d f", p=128)
        P.dma("scalar", lambda e, o=dst, s=src: e.dma_start(o, s),
              reads=[f"w1pre_{u}_{c}" for c in range(8)],
              writes=wreg(nm, [f"rw1buf{slot}"]), tag=f"rw1_{u}_{qi}")

    def rw2_dma(u, Hh):
        dst = rw2_ap.rearrange("p (t f) -> p t f", f=1024)
        src = d_w2pre[u, Hh * (HF // 2):(Hh + 1) * (HF // 2), :].rearrange(
            "(t p) f -> p t f", p=128)
        P.dma("scalar", lambda e, o=dst, s=src: e.dma_start(o, s),
              reads=[f"w2pre_{u}_{c}" for c in range(8)],
              writes=wreg("rw2b", ["rw2buf"]), tag=f"rw2_{u}_{Hh}")

    alloc("mx0", 4096); alloc("mx1", 4096)
    rw1_dma(0, 0); rw1_dma(0, 1)
    P.dma("sync", lambda e, o=b1_aps[0]: e.dma_start(o, d_b1[0]),
          writes=wreg("b1t0", ["b1buf0"]), tag="b1_0")
    P.dma("sync", lambda e, o=b2_aps[0]: e.dma_start(o, d_b2[0:1, :]),
          writes=wreg("b2r0", ["b2buf0"]), tag="b2_0")
    for Hh in range(2):
        rw2_dma(0, Hh)
        for ht8 in range(8):
            gq = (Hh * 8 + ht8) // 4
            qslot = gq % 3
            if ht8 % 4 == 0:
                nxt = Hh * 2 + ht8 // 4 + 2
                if nxt < nq_per_unit:
                    rw1_dma(0, nxt)
            qrow = ((Hh * 8 + ht8) % 4) * 128
            for tc in range(nic):
                bank = tc % 4
                for dt in range(ndt):
                    mm(pb[bank],
                       rw1_aps[qslot][:, dt * 512 + qrow: dt * 512 + qrow + 128],
                       flat_bf[:, dt * S + tc * 512: dt * S + (tc + 1) * 512],
                       start=(dt == 0), stop=(dt == ndt - 1),
                       reads=[f"rw1buf{qslot}"] +
                             [f"flatT_{dt}_{4 * tc + q}" for q in range(4)],
                       writes=[f"pbk{bank}"], tag=f"h1s_{Hh}_{ht8}_{tc}")
                gdst = h1_bf[:, ht8 * S + tc * 512: ht8 * S + (tc + 1) * 512]
                gbias = b1_aps[0][:, Hh * 8 + ht8: Hh * 8 + ht8 + 1]
                P.add("scalar",
                      lambda e, o=gdst, bk=bank, b=gbias: e.activation(
                          o, pb[bk], AF.Gelu, bias=b),
                      reads=[f"pbk{bank}", "b1buf0"],
                      writes=[f"h1_{ht8}_{tc}"], tag=f"gls_{Hh}_{ht8}_{tc}")
        for tt in range(nst):
            par = tt % 2
            bB = f"bufB{par}"
            b_ap = view(bB)
            mxr = view(f"mx{par}")
            if Hh == 1:
                P.dma("sync",
                      lambda e, o=mxr, s=tt: e.dma_start(
                          o, d_mixh[s * 128:(s + 1) * 128, :]),
                      reads=[f"mixh_{tt}"], writes=wreg(f"mx{par}"),
                      tag=f"shr{tt}")
            for dc in range(ndc):
                bank = 4 + (2 * tt + dc) % 4
                for ht8 in range(8):
                    last = (ht8 == 7) and (Hh == 0)
                    mm(pb[bank],
                       h1_bf[:, ht8 * S + tt * 128: ht8 * S + tt * 128 + 128],
                       rw2_ap[:, ht8 * 1024 + dc * 512: ht8 * 1024 + (dc + 1) * 512],
                       start=(ht8 == 0), stop=last,
                       reads=[f"h1_{ht8}_{tt // 4}", "rw2buf"],
                       writes=[f"pbk{bank}"], tag=f"o2s_{Hh}_{tt}_{dc}")
                if Hh == 1:
                    mm(pb[bank], onesf[0:1, 0:128],
                       b2_aps[0][:, dc * 512:(dc + 1) * 512],
                       False, True, reads=["b2buf0", "B:ones_f"],
                       writes=[f"pbk{bank}"], tag=f"o2sb_{tt}_{dc}")
                if Hh == 0:
                    P.add("vector",
                          lambda e, bk=bank, o=b_ap[:, dc * 512:(dc + 1) * 512]:
                          e.tensor_copy(o, pb[bk]),
                          reads=[f"pbk{bank}"], writes=[f"B:{bB}"],
                          tag=f"shc_{tt}_{dc}")
                else:
                    P.add("vector",
                          lambda e, bk=bank, o=b_ap[:, dc * 512:(dc + 1) * 512],
                          m_=mxr[:, dc * 512:(dc + 1) * 512]:
                          e.tensor_add(o, pb[bk], m_),
                          reads=[f"pbk{bank}", f"B:mx{par}"], writes=[f"B:{bB}"],
                          tag=f"sha_{tt}_{dc}")
            P.dma("sync",
                  lambda e, i=b_ap, s=tt: e.dma_start(
                      d_mixh[s * 128:(s + 1) * 128, :], i),
                  reads=[f"B:{bB}"], writes=[f"mixh_{tt}"], tag=f"shw{Hh}_{tt}")

    free("h1", [f"h1_{h}_{t}" for h in range(8) for t in range(nic)])
    free("rw1a", ["B:rw1a", "rw1buf0"])
    free("rw1b", ["B:rw1b", "rw1buf1"])
    free("rw1c", ["B:rw1c", "rw1buf2"])
    free("rw2b", ["B:rw2b", "rw2buf"])
    free("flatT", [f"flatT_{dt}_{st}" for dt in range(ndt) for st in range(nst)])
    free("w_sb", [f"w_{tt}" for tt in range(nst)])
    free("m_sb", [f"m_{tt}" for tt in range(nst)])
    free("tri_sb", ["B:tri_sb"]); free("erow", ["B:erow"])
    free("stg", ["B:stg"]); free("sent", ["B:sent"])
    free("bufA2", ["B:bufA2"]); free("bufA3", ["B:bufA3"])
    free("bufB2", ["B:bufB2"]); free("bufB3", ["B:bufB3"])
    free("htile2", ["B:htile2"]); free("htile3", ["B:htile3"])

    alloc("tkw", NTILE * 2 * 4)
    alloc("tokf", NTILE * 4)
    alloc("toki", NTILE * 4)
    alloc("wcolb", NTILE * 4)
    tkw_ap = view("tkw", F32, cols=NTILE * 2)
    toki_ap = view("toki", I32, cols=NTILE)
    wcol_ap = view("wcolb", F32, cols=NTILE)
    P.dma("sync",
          lambda e, o=tkw_ap.rearrange("p (r c) -> p r c", c=2),
          i=d_tokw[0:NSLOT, :].rearrange("(r p) c -> p r c", p=128):
          e.dma_start(o, i),
          reads=["tokw"] + [f"tokw_e{ee}" for ee in range(NE)],
          writes=wreg("tkw"), tag="tokw_rd")
    tokf_ap = view("tokf", F32, cols=NTILE)
    P.add("vector",
          lambda e, o=tokf_ap.unsqueeze(2),
          i=tkw_ap.rearrange("p (r c) -> p r c", c=2)[:, :, 0:1]:
          e.tensor_copy(o, i), reads=["B:tkw"], writes=["B:tokf"])
    P.add("vector",
          lambda e, o=tokf_ap, s=float(S): e.tensor_scalar_min(o, o, s),
          reads=["B:tokf"], writes=["B:tokf"])
    P.add("vector",
          lambda e, o=toki_ap, i=tokf_ap: e.tensor_copy(o, i),
          reads=["B:tokf"], writes=["B:toki"])
    P.add("vector",
          lambda e, o=wcol_ap.unsqueeze(2),
          i=tkw_ap.rearrange("p (r c) -> p r c", c=2)[:, :, 1:2]:
          e.tensor_copy(o, i), reads=["B:tkw"], writes=["B:wcolb"])

    CMAX = CAP[0]
    alloc("gatT", 8 * CMAX * 2, [f"gatT_l{i}" for i in range(TK[0])])
    alloc("rw1f0", 8 * HF * 2, [f"rw1f0_d{d}" for d in range(8)])
    alloc("rw1f1", 8 * HF * 2, [f"rw1f1_d{d}" for d in range(8)])
    alloc("rw2f", nht * 1024 * 2, [f"rw2f_c{c}" for c in range(8)])
    alloc("h1sg", 16 * 256 * 2, ["h1sg"])
    alloc("gati0", 1024 * 2, ["gati0"]); alloc("gati1", 1024 * 2, ["gati1"])
    alloc("gati2", 1024 * 2, ["gati2"])
    alloc("scl0", 2048, ["scl0"]); alloc("scl1", 2048, ["scl1"])
    gatT_ap = view("gatT", BF16)
    rw1f_aps = [view("rw1f0", BF16), view("rw1f1", BF16)]
    rw2f_ap = view("rw2f", BF16)
    h1sg_ap = view("h1sg", BF16)
    gati_aps = [view("gati0", BF16, cols=1024), view("gati1", BF16, cols=1024),
                view("gati2", BF16, cols=1024)]
    scl_aps = [view("scl0", BF16, cols=1024), view("scl1", BF16, cols=1024)]

    def seldma(kk, dst_ap, proto_ap, unit_stride, writes, tag, reads=()):
        def fn(e_, sem, _st=unit_stride, _p=proto_ap, _d=dst_ap, _kk=kk):
            with e_.register() as r:
                e_.load(r, sig_i[0:1, _kk:_kk + 1])
                e_.reg_add(r, r, 1)
                e_.reg_mul(r, r, _st)
                sv = e_.compute_val(r, donate=True)
                sap = bass.AP(_p.tensor, sv, list(_p.ap))
                e_.dma_start(_d, sap).then_inc(sem, 16)
        P.dma_selfinc("gpsimd", fn, reads=["sig"] + list(reads),
                      writes=writes, tag=tag)

    _w1p = d_w1pre8[0].rearrange("(d p) f -> p d f", p=128)
    _w2p = d_w2pre8[0].rearrange("(t p) f -> p t f", p=128)
    _b1p = d_b1.bitcast(U8)[0]
    _b2p = d_b2.bitcast(U8)[0:1, :]
    pre1r = [f"w1pre_{u}_{c}" for u in range(1, nunits) for c in range(8)]
    pre2r = [f"w2pre_{u}_{c}" for u in range(1, nunits) for c in range(8)]

    def seldma_off(kk, dst_ap, proto_tensor, proto_pat, unit_stride, off,
                   writes, tag, reads=()):
        def fn(e_, sem, _st=unit_stride, _o=off, _d=dst_ap, _kk=kk,
               _pt=proto_tensor, _pp=proto_pat):
            with e_.register() as r:
                e_.load(r, sig_i[0:1, _kk:_kk + 1])
                e_.reg_add(r, r, 1)
                e_.reg_mul(r, r, _st)
                if _o:
                    e_.reg_add(r, r, _o)
                sv = e_.compute_val(r, donate=True)
                sap = bass.AP(_pt, sv, _pp)
                e_.dma_start(_d, sap).then_inc(sem, 16)
        P.dma_selfinc("gpsimd", fn, reads=["sig"] + list(reads),
                      writes=writes, tag=tag)

    def exp_rw1_chunk(kk, dt):
        dst = rw1f_aps[kk % 2][:, dt * HF:(dt + 1) * HF].bitcast(U8)
        w = wreg(f"rw1f{kk % 2}", [f"rw1f{kk % 2}_d{dt}"]) if dt == 0 else             [f"rw1f{kk % 2}_d{dt}"]
        seldma_off(kk, dst, _w1p.tensor, [[2 * HF, 128], [1, 2 * HF]],
                   2 * D * HF, dt * 128 * 2 * HF, writes=w,
                   tag=f"xw1_{kk}_{dt}", reads=pre1r)

    def exp_rw2_chunk(kk, c):
        dst = rw2f_ap[:, c * 2048:(c + 1) * 2048].bitcast(U8).rearrange(
            "p (t f) -> p t f", f=2 * D)
        w = wreg("rw2f", [f"rw2f_c{c}"]) if c == 0 else [f"rw2f_c{c}"]
        seldma_off(kk, dst, _w2p.tensor,
                   [[2 * D, 128], [128 * 2 * D, 2], [1, 2 * D]],
                   2 * HF * D, c * 256 * 2 * D, writes=w,
                   tag=f"xw2_{kk}_{c}", reads=pre2r)

    for dt in range(ndt):
        exp_rw1_chunk(0, dt)
    flatb_regs = [f"flatb_{st}" for st in range(nst)]
    for kk in range(NE):
        Ck, Tk, base = CAP[kk], TK[kk], UOFF[kk] // 128
        rw2q = [lambda c=c: exp_rw2_chunk(kk, c) for c in range(8)]
        fetchq = []
        seldma(kk, b1_aps[kk % 2].bitcast(U8), _b1p, 128 * nht * 4,
               writes=wreg(f"b1t{kk % 2}", [f"b1buf{kk % 2}"]), tag=f"xb1_{kk}")
        seldma(kk, b2_aps[kk % 2].bitcast(U8), _b2p, D * 4,
               writes=wreg(f"b2r{kk % 2}", [f"b2buf{kk % 2}"]), tag=f"xb2_{kk}")
        if kk + 1 < NE:
            fetchq += [lambda dt=dt, nk=kk + 1: exp_rw1_chunk(nk, dt)
                       for dt in range(ndt)]
        groups = [list(range(i, min(i + 2, Tk))) for i in range(0, Tk, 2)]
        prate = -(-len(fetchq) // max(len(groups) - 1, 1))
        first_group = True
        for sg in groups:
            for i in sg:
                g = base + i
                gb = gati_aps[g % 3]
                P.dma("gpsimd",
                      lambda e, o=gb, ix=toki_ap[:, g:g + 1]:
                      e.indirect_dma_start(o, None, d_flatb,
                                           IndirectOffsetOnAxis(ix, 0)),
                      reads=["B:toki"] + flatb_regs, writes=[f"gati{g % 3}"],
                      tag=f"gth{kk}_{i}")
                for dt in range(ndt):
                    bank = dt % 2
                    transpose(pbf16(bank), gb[:, dt * 128:(dt + 1) * 128],
                              id_bf_ap,
                              reads=[f"gati{g % 3}", "B:id_bf"],
                              writes=[f"pbk{bank}"])
                    copy_on("scalar",
                            gatT_ap[:, dt * CMAX + i * 128: dt * CMAX + i * 128 + 128],
                            pbf16(bank), reads=[f"pbk{bank}"],
                            writes=[f"gatT_l{i}"])
            if first_group:
                for f in rw2q:
                    f()
                first_group = False
            else:
                for _ in range(prate):
                    if fetchq:
                        fetchq.pop(0)()
            W = 128 * len(sg)
            i0 = sg[0]
            for ht in range(nht):
                bank = 2 + ht % 2
                for dt in range(ndt):
                    mm(pb[bank][:, 0:W],
                       rw1f_aps[kk % 2][:, dt * HF + ht * 128: dt * HF + ht * 128 + 128],
                       gatT_ap[:, dt * CMAX + i0 * 128: dt * CMAX + i0 * 128 + W],
                       start=(dt == 0), stop=(dt == ndt - 1),
                       reads=[f"rw1f{kk % 2}_d{dt}"] + [f"gatT_l{i}" for i in sg],
                       writes=[f"pbk{bank}"], tag=f"xh1_{kk}_{i0}_{ht}")
                P.add("scalar",
                      lambda e, o=h1sg_ap[:, ht * 256: ht * 256 + W],
                      i=pb[bank][:, 0:W],
                      b=b1_aps[kk % 2][:, ht:ht + 1]: e.activation(
                          o, i, AF.Gelu, bias=b),
                      reads=[f"pbk{bank}", f"b1buf{kk % 2}"],
                      writes=["h1sg"], tag=f"xgl_{kk}_{i0}_{ht}")
            for tl, i in enumerate(sg):
                g = base + i
                sclb = scl_aps[g % 2]
                for dc in range(ndc):
                    bank = 4 + (2 * tl + dc) % 4
                    for ht in range(nht):
                        mm(pb[bank],
                           h1sg_ap[:, ht * 256 + tl * 128: ht * 256 + tl * 128 + 128],
                           rw2f_ap[:, ht * 1024 + dc * 512: ht * 1024 + (dc + 1) * 512],
                           start=(ht == 0), stop=False,
                           reads=["h1sg", f"rw2f_c{ht // 2}"],
                           writes=[f"pbk{bank}"],
                           tag=f"xo2_{kk}_{i}_{dc}")
                    mm(pb[bank], onesf[0:1, 0:128],
                       b2_aps[kk % 2][:, dc * 512:(dc + 1) * 512],
                       False, True, reads=[f"b2buf{kk % 2}", "B:ones_f"],
                       writes=[f"pbk{bank}"], tag=f"xo2b_{kk}_{i}_{dc}")
                    P.add("vector",
                          lambda e, o=sclb[:, dc * 512:(dc + 1) * 512], bk=bank,
                          w=wcol_ap[:, g:g + 1]: e.tensor_scalar(
                              o, pb[bk], w, None, op0=ALU.mult),
                          reads=[f"pbk{bank}", "B:wcolb"],
                          writes=[f"scl{g % 2}"], tag=f"xsc_{kk}_{i}_{dc}")
                P.dma("sync",
                      lambda e, i_=sclb, gg=g: e.dma_start(
                          d_expout[gg * 128:(gg + 1) * 128, :], i_),
                      reads=[f"scl{g % 2}"], writes=[f"expout_{g}"],
                      tag=f"xw_{kk}_{i}")
        while fetchq:
            fetchq.pop(0)()

    free("gatT", [f"gatT_l{i}" for i in range(TK[0])])
    free("rw1f0", [f"rw1f0_d{dt}" for dt in range(8)])
    free("rw1f1", [f"rw1f1_d{dt}" for dt in range(8)])
    free("rw2f", [f"rw2f_c{c}" for c in range(8)])
    free("h1sg", ["h1sg"])
    free("gati0", ["gati0"]); free("gati1", ["gati1"]); free("gati2", ["gati2"])
    free("b1t0", ["B:b1t0", "b1buf0"]); free("b1t1", ["B:b1t1", "b1buf1"])
    free("b2r0", ["B:b2r0", "b2buf0"]); free("b2r1", ["B:b2r1", "b2buf1"])

    alloc("blk0", 1024 * 2); alloc("blk1", 1024 * 2)
    alloc("blk2", 1024 * 2); alloc("blk3", 1024 * 2)
    alloc("blk4", 1024 * 2); alloc("blk5", 1024 * 2)
    alloc("ptb0", 128 * 2); alloc("ptb1", 128 * 2)
    alloc("pbufb0", 128 * 2); alloc("pbufb1", 128 * 2)
    blk_aps = [view(f"blk{j}", BF16, cols=1024) for j in range(6)]
    ptb_aps = [view("ptb0", BF16, cols=128), view("ptb1", BF16, cols=128)]
    pbufb_aps = [view("pbufb0", BF16, cols=128), view("pbufb1", BF16, cols=128)]
    exr = [f"expout_{g}" for g in range(NTILE)]

    def p6_expand(tt):
        par = tt % 2
        a, bB = f"bufA{par}", f"bufB{par}"
        a_ap, b_ap = view(a), view(bB)
        mx_ap = view(f"mx{par}")
        P.dma("sync",
              lambda e, o=a_ap, s=tt: e.dma_start(o, d_xsp[s * 128:(s + 1) * 128, :]),
              reads=[f"xsp_{tt}"], writes=wreg(a), tag=f"x2re{tt}")
        P.dma("sync",
              lambda e, o=mx_ap, s=tt: e.dma_start(o, d_mixh[s * 128:(s + 1) * 128, :]),
              reads=[f"mixh_{tt}"], writes=wreg(f"mx{par}"), tag=f"mxre{tt}")
        for ee in range(NE):
            bj = (tt * NE + ee) % 6
            pp = ee % 2
            pbufb_ap = pbufb_aps[pp]
            ptb_ap = ptb_aps[pp]
            def blfn(e_, sem, _tt=tt, _ee=ee, _dst=blk_aps[bj].bitcast(U8)):
                with e_.register() as r:
                    e_.load(r, bas3_i[0:1, _tt * NE + _ee:_tt * NE + _ee + 1])
                    sv = e_.compute_val(r, donate=True)
                    sap = bass.AP(d_expout8.tensor, sv, [[2 * D, 128], [1, 2 * D]])
                    e_.dma_start(_dst, sap).then_inc(sem, 16)
            P.dma_selfinc("gpsimd", blfn,
                          reads=[f"bas_{tt}"] + exr, writes=[f"blk{bj}"],
                          tag=f"bl{tt}_{ee}")
            P.add("vector",
                  lambda e, o=pbufb_ap, m=psf_ap[:, tt * NE + ee:tt * NE + ee + 1]:
                  e.tensor_scalar(o, irep_ap, m, None, op0=ALU.is_equal),
                  reads=[f"psf_{tt}", "B:irep"], writes=[f"B:pbufb{pp}"],
                  tag=f"pb6_{tt}_{ee}")
            bank6 = 2 + ee % 2
            transpose(pbf16(bank6), pbufb_ap, id_bf_ap,
                      reads=[f"B:pbufb{pp}", "B:id_bf"], writes=[f"pbk{bank6}"],
                      tag=f"pt6_{tt}_{ee}")
            copy_on("scalar", ptb_ap, pbf16(bank6), reads=[f"pbk{bank6}"],
                    writes=[f"B:ptb{pp}"], tag=f"pe6_{tt}_{ee}")
            for dc in range(ndc):
                bank = 4 + (2 * tt + dc) % 4
                mm(pb[bank], ptb_ap, blk_aps[bj][:, dc * 512:(dc + 1) * 512],
                   start=(ee == 0), stop=(ee == NE - 1),
                   reads=[f"B:ptb{pp}", f"blk{bj}"], writes=[f"pbk{bank}"],
                   tag=f"xp6_{tt}_{ee}_{dc}")

    def p6_finish(tt):
        par = tt % 2
        a, bB = f"bufA{par}", f"bufB{par}"
        a_ap, b_ap = view(a), view(bB)
        mx_ap = view(f"mx{par}")
        for dc in range(ndc):
            bank = 4 + (2 * tt + dc) % 4
            P.add("vector",
                  lambda e, o=b_ap[:, dc * 512:(dc + 1) * 512], bk=bank,
                  m=mx_ap[:, dc * 512:(dc + 1) * 512]: e.tensor_add(o, pb[bk], m),
                  reads=[f"pbk{bank}", f"B:mx{par}"], writes=[f"B:{bB}"],
                  tag=f"mxs6_{tt}_{dc}")
        cm, cr, mr = ln_stats(b_ap, [f"B:{bB}"], f"p5{par}_", "sq0")
        P.add("vector",
              lambda e, o=b_ap, m=cm, r=cr: e.tensor_scalar(
                  o, o, m, r, op0=ALU.subtract, op1=ALU.mult),
              reads=[f"B:{bB}"] + mr, writes=[f"B:{bB}"])
        P.add("vector", lambda e, o=b_ap, i=a_ap: e.tensor_add(o, o, i),
              reads=[f"B:{bB}", f"B:{a}"], writes=[f"B:{bB}"])
        P.dma("sync",
              lambda e, i=b_ap, s=tt: e.dma_start(d_y[s * 128:(s + 1) * 128, :], i),
              reads=[f"B:{bB}"], writes=[f"y_{tt}"], tag=f"yout{tt}")

    for tt in range(nst):
        p6_expand(tt)
        if tt > 0:
            p6_finish(tt - 1)
    p6_finish(nst - 1)

    with nc.Block() as blk:
        P.emit(blk, sems)
    es.close()
    if ret_prog:
        return nc, P
    return nc


_CACHE = {}


def _rope_tables(S, half=32, theta=10000.0, scale=1.0):
    inv = 1.0 / theta ** (np.arange(half, dtype=np.float64) / half)
    fr = np.arange(S, dtype=np.float64)[:, None] * inv[None, :]
    return (np.cos(fr) * scale).astype(np.float32), \
           (np.sin(fr) * scale).astype(np.float32)


def kernel(L0_attn_x, L0_attn_ln_g, L0_attn_ln_b, L0_attn_w_qkv, L0_attn_w_out,
           L0_attn_b_out, L0_ffn_router_w, L0_ffn_router_b, L0_ffn_sw1,
           L0_ffn_sb1, L0_ffn_sw2, L0_ffn_sb2, L0_ffn_rw1, L0_ffn_rb1,
           L0_ffn_rw2, L0_ffn_rb2):
    B, S, D = L0_attn_x.shape
    NE = L0_ffn_rw1.shape[0]
    HF = L0_ffn_sw1.shape[1]
    nunits = NE + 1
    nht = HF // 128

    if "nc" not in _CACHE:
        _CACHE["nc"] = build_program(S=S, D=D, NH=16, HD=64, HF=HF, NE=NE)
    nc = _CACHE["nc"]

    f32 = np.float32
    asc = np.ascontiguousarray
    cosn, sinn = _rope_tables(S, scale=1.0)
    cosq, sinq = _rope_tables(S, scale=0.125)

    wqkvT = asc(np.asarray(L0_attn_w_qkv, f32).T)
    woutT = asc(np.asarray(L0_attn_w_out, f32).T)
    lng_rep = asc(np.broadcast_to(np.asarray(L0_attn_ln_g, f32), (128, D)))
    lnb_rep = asc(np.broadcast_to(np.asarray(L0_attn_ln_b, f32), (128, D)))
    b_out = asc(np.asarray(L0_attn_b_out, f32).reshape(1, D))
    rwq_pad = np.zeros((D, 128), f32)
    rwq_pad[:, :NE] = np.asarray(L0_ffn_router_w, f32)
    rbq_rep = asc(np.broadcast_to(np.asarray(L0_ffn_router_b, f32), (128, NE)))
    ident = np.eye(128, dtype=f32)
    w1_all = asc(np.concatenate(
        [np.asarray(L0_ffn_sw1, f32)[None], np.asarray(L0_ffn_rw1, f32)], axis=0))
    w2_all = asc(np.concatenate(
        [np.asarray(L0_ffn_sw2, f32)[None], np.asarray(L0_ffn_rw2, f32)], axis=0))
    b1_all = asc(np.concatenate(
        [np.asarray(L0_ffn_sb1, f32)[None], np.asarray(L0_ffn_rb1, f32)],
        axis=0).reshape(nunits, nht, 128).transpose(0, 2, 1))
    b2_all = asc(np.concatenate(
        [np.asarray(L0_ffn_sb2, f32)[None], np.asarray(L0_ffn_rb2, f32)], axis=0))

    nst = S // 128
    tri_u = np.triu(np.ones((128, 128), f32), 1)
    iota_tok = (np.arange(128, dtype=f32)[:, None]
                + 128.0 * np.arange(nst, dtype=f32)[None, :])
    iota_tok = asc(iota_tok.astype(f32))
    eidx = np.arange(NE, dtype=f32).reshape(1, NE)
    etag = (np.arange(NE, dtype=f32) / 16.0).reshape(1, NE)
    iota_rep = asc(np.broadcast_to(np.arange(128, dtype=f32)[None, :], (128, 128)))
    shared = {
        "wqkvT": wqkvT, "woutT": woutT, "lng_rep": lng_rep, "lnb_rep": lnb_rep,
        "b_out": b_out, "cosq": cosq, "sinq": sinq, "cosk": cosn, "sink": sinn,
        "ident": ident, "rwq_pad": rwq_pad, "rbq_rep": rbq_rep,
        "w1_all": w1_all, "b1_all": b1_all, "w2_all": w2_all, "b2_all": b2_all,
        "tri_u": tri_u, "iota_tok": iota_tok, "eidx": eidx, "etag": etag,
        "iota_rep": iota_rep,
    }
    x = np.asarray(L0_attn_x, f32)
    in_maps = []
    for c in range(B):
        m = dict(shared)
        m["x"] = asc(x[c])
        in_maps.append(m)

    res = run_bass_kernel_spmd(nc, in_maps, core_ids=list(range(B)))
    out = np.stack([np.asarray(res.results[c]["y"]) for c in range(B)], axis=0)
    return out


# version 36
# speedup vs baseline: 1.2065x; 1.2065x over previous
import contextlib
import numpy as np
import ml_dtypes
import concourse.bass as bass
import concourse.mybir as mybir
from concourse.bass_utils import run_bass_kernel_spmd
from concourse.bass_core_rust import IndirectOffsetOnAxis

F32 = mybir.dt.float32
BF16 = mybir.dt.bfloat16
U8 = mybir.dt.uint8
I32 = mybir.dt.int32
AF = mybir.ActivationFunctionType
ALU = mybir.AluOpType
AX = mybir.AxisListType
bfloat16 = ml_dtypes.bfloat16

ENGINES = ("tensor", "vector", "scalar", "gpsimd", "sync")
DMA_QUEUES = ("gpsimd", "sync", "scalar")
NSEM_PER_QUEUE = 8


class _Op:
    __slots__ = ("idx", "engine", "fn", "reads", "writes", "queue", "tag",
                 "deps", "milestone", "mnum", "waits", "qsem", "self_inc")

    def __init__(self, idx, engine, fn, reads, writes, queue, tag):
        self.idx = idx
        self.engine = engine
        self.fn = fn
        self.reads = tuple(reads)
        self.writes = tuple(writes)
        self.queue = queue
        self.tag = tag
        self.deps = None
        self.milestone = False
        self.mnum = None
        self.waits = None
        self.qsem = None
        self.self_inc = False


class Prog:
    def __init__(self):
        self.ops = []

    def add(self, engine, fn, reads=(), writes=(), queue=None, tag=""):
        op = _Op(len(self.ops), engine, fn, reads, writes, queue, tag)
        self.ops.append(op)
        return op

    def dma(self, queue, fn, reads=(), writes=(), tag=""):
        return self.add(queue, fn, reads, writes, queue=queue, tag=tag)

    def dma_selfinc(self, queue, fn, reads=(), writes=(), tag=""):
        op = self.add(queue, fn, reads, writes, queue=queue, tag=tag)
        op.self_inc = True
        return op

    def _analyze(self):
        regions = {}
        for op in self.ops:
            deps = set()
            for r in op.reads:
                st = regions.get(r)
                if st and st[0] is not None:
                    deps.add(st[0])
            for w in op.writes:
                st = regions.get(w)
                if st:
                    if st[0] is not None:
                        deps.add(st[0])
                    deps.update(st[1])
            deps.discard(op)
            real = []
            for d in deps:
                if (d.queue is None and op.queue is None
                        and d.engine == "tensor" and op.engine == "tensor"):
                    continue
                real.append(d)
                d.milestone = True
            op.deps = real
            for w in op.writes:
                regions[w] = [op, []]
            for r in op.reads:
                st = regions.setdefault(r, [None, []])
                st[1].append(op)

    def _domain(self, op):
        if op.queue is not None:
            return ("d", op.queue, op.qsem)
        return ("e", op.engine)

    def _schedule(self):
        qcount = {q: 0 for q in DMA_QUEUES}
        for op in self.ops:
            if op.queue is not None:
                op.qsem = qcount[op.queue] % NSEM_PER_QUEUE
                qcount[op.queue] += 1
                op.milestone = True
        mcount = {}
        for op in self.ops:
            if op.milestone:
                dom = self._domain(op)
                mcount[dom] = mcount.get(dom, 0) + 1
                op.mnum = mcount[dom]
        know = {e: {} for e in ENGINES}
        snap = {}
        for op in self.ops:
            k = know[op.engine]
            need = {}
            for d in op.deps:
                dom = self._domain(d)
                if need.get(dom, (0, -1))[0] < d.mnum:
                    need[dom] = (d.mnum, d.idx)
            waits = []
            if op.queue is not None and op.mnum > 1:
                dom = self._domain(op)
                if k.get(dom, 0) < op.mnum - 1:
                    waits.append((dom, op.mnum - 1))
                    k[dom] = op.mnum - 1
            for dom, (num, didx) in sorted(need.items(), key=lambda x: x[1][1]):
                if k.get(dom, 0) >= num:
                    continue
                waits.append((dom, num))
                s = snap.get(didx)
                if s:
                    for d2, n2 in s.items():
                        if k.get(d2, 0) < n2:
                            k[d2] = n2
                if k.get(dom, 0) < num:
                    k[dom] = num
            op.waits = waits
            if op.milestone:
                snap[op.idx] = dict(k)
        self._final_counts = mcount

    def emit(self, blk, sems):
        self._analyze()
        self._schedule()
        per_engine = {e: [] for e in ENGINES}
        for op in self.ops:
            per_engine[op.engine].append(op)

        def run_stream(engine_name):
            def body(eng):
                for op in per_engine[engine_name]:
                    for dom, num in op.waits:
                        val = num * 16 if dom[0] == "d" else num
                        eng.wait_ge(sems[dom], val)
                    if op.self_inc:
                        op.fn(eng, sems[self._domain(op)])
                    else:
                        inst = op.fn(eng)
                        if op.milestone:
                            dom = self._domain(op)
                            inst.then_inc(sems[dom], 16 if dom[0] == "d" else 1)
                for dom, total in self._final_counts.items():
                    if dom[0] == "d" and dom[1] == engine_name:
                        eng.wait_ge(sems[dom], total * 16)
            return body

        blk.tensor(run_stream("tensor"))
        blk.vector(run_stream("vector"))
        blk.scalar(run_stream("scalar"))
        blk.gpsimd(run_stream("gpsimd"))
        blk.sync(run_stream("sync"))

    def all_domains(self):
        doms = [("e", e) for e in ENGINES]
        for q in DMA_QUEUES:
            for i in range(NSEM_PER_QUEUE):
                doms.append(("d", q, i))
        return doms


class Buf:
    __slots__ = ("name", "off", "nbytes", "ghosts")

    def __init__(self, name, off, nbytes, ghosts):
        self.name = name
        self.off = off
        self.nbytes = nbytes
        self.ghosts = ghosts


class Arena:
    def __init__(self, nbytes):
        self.cap = nbytes
        self.free_list = [(0, nbytes, [])]
        self.live = {}

    def alloc(self, name, nbytes, align=128):
        nbytes = (nbytes + align - 1) // align * align
        for i, (off, size, ghosts) in enumerate(self.free_list):
            aoff = (off + align - 1) // align * align
            pad = aoff - off
            if size - pad >= nbytes:
                newfree = []
                if pad:
                    newfree.append((off, pad, list(ghosts)))
                rem = size - pad - nbytes
                if rem:
                    newfree.append((aoff + nbytes, rem, list(ghosts)))
                self.free_list[i:i + 1] = newfree
                b = Buf(name, aoff, nbytes, list(ghosts))
                self.live[name] = b
                return b
        raise MemoryError(f"arena OOM for {name} ({nbytes}B); free={[(o, s) for o, s, _ in self.free_list]}; live={sorted((b.off, b.nbytes, n) for n, b in self.live.items())}")

    def free(self, name, ghost_regions):
        b = self.live.pop(name)
        self.free_list.append((b.off, b.nbytes, list(ghost_regions)))
        self.free_list.sort(key=lambda t: t[0])
        merged = []
        for off, size, g in self.free_list:
            if merged and merged[-1][0] + merged[-1][1] == off:
                merged[-1] = (merged[-1][0], merged[-1][1] + size,
                              merged[-1][2] + g)
            else:
                merged.append((off, size, list(g)))
        self.free_list = merged


def build_program(S=2048, D=1024, NH=16, HD=64, HF=2048, NE=8, dbg=False,
                  ret_prog=False):
    nst = S // 128
    ndt = D // 128
    nic = S // 512
    ndc = D // 512
    nht = HF // 128
    njt = nst
    nunits = NE + 1
    EPS = 1e-5

    nc = bass.Bass("TRN2", target_bir_lowering=False, debug=False)

    d_x = nc.dram_tensor("x", [S, D], F32, kind="ExternalInput").ap()
    d_wqkvT = nc.dram_tensor("wqkvT", [D, 3 * D], F32, kind="ExternalInput").ap()
    d_woutT = nc.dram_tensor("woutT", [D, D], F32, kind="ExternalInput").ap()
    d_lng = nc.dram_tensor("lng_rep", [128, D], F32, kind="ExternalInput").ap()
    d_lnb = nc.dram_tensor("lnb_rep", [128, D], F32, kind="ExternalInput").ap()
    d_bout = nc.dram_tensor("b_out", [1, D], F32, kind="ExternalInput").ap()
    d_cosq = nc.dram_tensor("cosq", [S, 32], F32, kind="ExternalInput").ap()
    d_sinq = nc.dram_tensor("sinq", [S, 32], F32, kind="ExternalInput").ap()
    d_cosk = nc.dram_tensor("cosk", [S, 32], F32, kind="ExternalInput").ap()
    d_sink = nc.dram_tensor("sink", [S, 32], F32, kind="ExternalInput").ap()
    d_ident = nc.dram_tensor("ident", [128, 128], F32, kind="ExternalInput").ap()
    d_rwq = nc.dram_tensor("rwq_pad", [D, 128], F32, kind="ExternalInput").ap()
    d_rbq = nc.dram_tensor("rbq_rep", [128, NE], F32, kind="ExternalInput").ap()
    d_w1 = nc.dram_tensor("w1_all", [nunits, D, HF], F32, kind="ExternalInput").ap()
    d_b1 = nc.dram_tensor("b1_all", [nunits, 128, nht], F32, kind="ExternalInput").ap()
    d_w2 = nc.dram_tensor("w2_all", [nunits, HF, D], F32, kind="ExternalInput").ap()
    d_b2 = nc.dram_tensor("b2_all", [nunits, D], F32, kind="ExternalInput").ap()
    d_y = nc.dram_tensor("y", [S, D], F32, kind="ExternalOutput").ap()
    d_xsp = nc.dram_tensor("xsp", [S, D], F32, kind="Internal").ap()
    d_tri = nc.dram_tensor("tri_u", [128, 128], F32, kind="ExternalInput").ap()
    d_itok = nc.dram_tensor("iota_tok", [128, S // 128], F32, kind="ExternalInput").ap()
    d_eidx = nc.dram_tensor("eidx", [1, NE], F32, kind="ExternalInput").ap()
    d_etag = nc.dram_tensor("etag", [1, NE], F32, kind="ExternalInput").ap()
    d_irep = nc.dram_tensor("iota_rep", [128, 128], F32, kind="ExternalInput").ap()
    CAP = [-(-min(S, (2 * S) // (k + 1)) // 128) * 128 + 128 for k in range(NE)]
    NSLOT = sum(CAP)
    NTILE = NSLOT // 128
    UOFF = [sum(CAP[:k]) for k in range(NE)]
    TK = [c // 128 for c in CAP]
    d_flatb = nc.dram_tensor("flatb", [S + 128, D], BF16, kind="Internal").ap()
    d_tokw = nc.dram_tensor("tokw", [NSLOT + 128, 2], F32, kind="Internal").ap()
    d_expout = nc.dram_tensor("expout", [NSLOT, D], BF16, kind="Internal").ap()
    d_tokw8 = d_tokw.bitcast(U8)
    d_expout8 = d_expout.bitcast(U8)
    d_w1pre = nc.dram_tensor("w1pre", [nunits, D, HF], BF16, kind="Internal").ap()
    d_w2pre = nc.dram_tensor("w2pre", [nunits, HF, D], BF16, kind="Internal").ap()
    d_w1pre8 = d_w1pre.bitcast(U8)
    d_w2pre8 = d_w2pre.bitcast(U8)
    d_woutpre = nc.dram_tensor("woutpre", [D, D], BF16, kind="Internal").ap()
    d_mixh = nc.dram_tensor("mixh", [S, D], F32, kind="Internal").ap()
    nst_ = S // 128
    ndt_ = D // 128
    if dbg:
        d_dbg = {
            "dbg_hT": nc.dram_tensor("dbg_hT", [128, ndt_ * S], BF16, kind="Internal").ap(),
            "dbg_qn": nc.dram_tensor("dbg_qn", [128, nst_ * 1024], BF16, kind="Internal").ap(),
            "dbg_kn": nc.dram_tensor("dbg_kn", [128, nst_ * 1024], BF16, kind="Internal").ap(),
            "dbg_vn": nc.dram_tensor("dbg_vn", [128, nst_ * 1040], BF16, kind="Internal").ap(),
            "dbg_negM": nc.dram_tensor("dbg_negM", [128, 1], F32, kind="Internal").ap(),
            "dbg_slab": nc.dram_tensor("dbg_slab", [128, nst_ * 512], BF16, kind="Internal").ap(),
            "dbg_oT": nc.dram_tensor("dbg_oT", [128, ndt_ * S], BF16, kind="Internal").ap(),
            "dbg_flatT": nc.dram_tensor("dbg_flatT", [128, ndt_ * S], BF16, kind="Internal").ap(),
            "dbg_w": nc.dram_tensor("dbg_w", [128, nst_ * NE], F32, kind="Internal").ap(),
            "dbg_mixed": nc.dram_tensor("dbg_mixed", [128, nst_ * D], F32, kind="Internal").ap(),
        }

    def dump(key, src_ap, reads):
        if dbg:
            P.dma("scalar", lambda e, o=d_dbg[key], s=src_ap: e.dma_start(o, s),
                  reads=reads, tag=key)

    P = Prog()
    ARENA_BYTES = 206 * 1024
    es = contextlib.ExitStack()
    arena_t = es.enter_context(nc.sbuf_tensor("arena", [128, ARENA_BYTES], U8))
    ar = arena_t.ap()
    arena = Arena(ARENA_BYTES)
    pbanks = [es.enter_context(nc.psum_tensor(f"pbk{i}", [128, 512], F32))
              for i in range(8)]
    pb = [t.ap() for t in pbanks]

    def pbf16(i, n=128):
        return pb[i].bitcast(BF16)[:, 0:n]

    sems = {}
    for dom in P.all_domains():
        sems[dom] = es.enter_context(
            nc.semaphore(name="s_" + "_".join(map(str, dom))))

    bufs = {}

    def alloc(name, nbytes, regs=()):
        b = arena.alloc(name, nbytes)
        bufs[name] = b
        if b.ghosts:
            g = list(b.ghosts)
            b.ghosts = []
            P.add("vector", lambda e: e.engine_nop(),
                  writes=g + [f"B:{name}"] + list(regs), tag=f"fence_{name}")
        return b

    def free(name, regs):
        bufs.pop(name)
        arena.free(name, regs)

    def view(name, dt=F32, cols=None):
        b = bufs[name]
        esz = 2 if dt == BF16 else 4
        n = b.nbytes // esz if cols is None else cols
        return ar[:, b.off:b.off + n * esz].bitcast(dt)

    def wreg(name, extra=()):
        b = bufs[name]
        g = b.ghosts
        b.ghosts = []
        return [f"B:{name}"] + list(extra) + g

    def mm(psum_ap, lhsT, rhs, start, stop, reads, writes, tp=None, tag=""):
        P.add("tensor",
              lambda e, o=psum_ap, a=lhsT, b=rhs, s1=start, s2=stop, t=tp:
              e.matmul(o, a, b, start=s1, stop=s2, tile_position=t),
              reads=reads, writes=writes, tag=tag)

    def transpose(psum_ap, in_ap, id_ap, reads, writes, tag=""):
        P.add("tensor",
              lambda e, o=psum_ap, i=in_ap, d=id_ap: e.transpose(o, i, d),
              reads=reads, writes=writes, tag=tag)

    def copy_on(engine, dst, src, reads, writes, tag=""):
        if engine == "scalar":
            P.add("scalar", lambda e, d=dst, s=src: e.copy(d, s),
                  reads=reads, writes=writes, tag=tag)
        else:
            P.add(engine, lambda e, d=dst, s=src: e.tensor_copy(d, s),
                  reads=reads, writes=writes, tag=tag)

    alloc("ones_f", 128 * 4)
    alloc("ones_bf", 128)
    alloc("id_bf", 128 * 2)
    alloc("id_f", 128 * 4)
    alloc("negM", 128)
    alloc("cols", 1024)
    _COLS = {}

    def col(name):
        if name not in _COLS:
            _COLS[name] = len(_COLS)
            assert len(_COLS) <= 256
        return view("cols")[:, _COLS[name]:_COLS[name] + 1]

    onesf_ap = None

    P.add("vector", lambda e, d=None: e.memset(view("ones_f"), 1.0),
          writes=wreg("ones_f"))
    P.add("vector", lambda e: e.memset(view("ones_bf", BF16, cols=1), 1.0),
          writes=wreg("ones_bf"))
    id_bf_ap = view("id_bf", BF16, cols=128)
    id_f_ap = view("id_f", cols=128)
    P.dma("gpsimd", lambda e, o=id_bf_ap: e.dma_start(o, d_ident),
          writes=wreg("id_bf"))
    P.dma("sync", lambda e, o=id_f_ap: e.dma_start(o, d_ident),
          writes=wreg("id_f"))
    onesf = view("ones_f")
    onesbf = view("ones_bf", BF16, cols=1)
    negM_ap = view("negM")[:, 0:1]

    alloc("bufA0", 4096); alloc("bufA1", 4096)
    alloc("bufA2", 4096); alloc("bufA3", 4096)
    alloc("bufB0", 4096); alloc("bufB1", 4096)
    alloc("bufB2", 4096); alloc("bufB3", 4096)
    alloc("sq0", 4096)
    alloc("htile0", 2048); alloc("htile1", 2048)
    alloc("htile2", 2048); alloc("htile3", 2048)
    alloc("hT", ndt * S * 2, [f"hT_{dt}_{st}" for dt in range(ndt) for st in range(nst)])
    alloc("q_nat", nst * 1024 * 2, [f"q_nat_{st}" for st in range(nst)])
    alloc("k_nat", nst * 1024 * 2, [f"k_nat_{st}" for st in range(nst)])
    alloc("rwq", ndt * 128 * 2)
    alloc("v_nat", nst * 1040 * 2, [f"v_nat_{st}" for st in range(nst)])
    alloc("lng", 4096); alloc("lnb", 4096)

    lng_ap = view("lng"); lnb_ap = view("lnb")
    P.dma("sync", lambda e, o=lng_ap: e.dma_start(o, d_lng), writes=wreg("lng"))
    P.dma("sync", lambda e, o=lnb_ap: e.dma_start(o, d_lnb), writes=wreg("lnb"))

    alloc("ctq", nst * 32 * 4); alloc("stq", nst * 32 * 4)
    alloc("ctk", nst * 32 * 4); alloc("stk", nst * 32 * 4)
    for nm, src in (("ctq", d_cosq), ("stq", d_sinq), ("ctk", d_cosk),
                    ("stk", d_sink)):
        dst = view(nm).rearrange("p (t f) -> p t f", f=32)
        P.dma("sync",
              lambda e, o=dst, s=src: e.dma_start(
                  o, s.rearrange("(t p) f -> p t f", p=128)),
              writes=wreg(nm))

    def wq_dma(fc):
        nm = f"wq{fc % 2}"
        dst = view(nm, BF16).rearrange("p (d f) -> p d f", f=512)
        src = d_wqkvT[:, fc * 512:(fc + 1) * 512].rearrange("(d p) f -> p d f", p=128)
        P.dma("gpsimd", lambda e, o=dst, s=src: e.dma_start(o, s),
              writes=wreg(nm, [f"wqbuf{fc % 2}"]), tag=f"wqdma{fc}")

    rwq_dst = view("rwq", BF16).rearrange("p (d f) -> p d f", f=128)
    P.dma("gpsimd",
          lambda e, o=rwq_dst: e.dma_start(
              o, d_rwq.rearrange("(d p) f -> p d f", p=128)),
          writes=wreg("rwq"))

    def ln_stats(src_ap, src_regions, pfx, sqn):
        c_s1, c_s2 = col(pfx + "s1"), col(pfx + "s2")
        c_m, c_msq = col(pfx + "m"), col(pfx + "msq")
        c_var, c_sqv, c_r = col(pfx + "var"), col(pfx + "sqv"), col(pfx + "r")
        sqv = view(sqn)
        P.add("vector", lambda e, s=src_ap, o=c_s1: e.reduce_sum(o, s, axis=AX.X),
              reads=src_regions, writes=[pfx + "s1"])
        P.add("scalar",
              lambda e, s=src_ap, o=sqv, a=c_s2: e.activation(
                  o, s, AF.Square, accum_out=a),
              reads=src_regions, writes=[f"B:{sqn}", pfx + "s2"])
        P.add("vector",
              lambda e, i=c_s1, o=c_m: e.tensor_scalar(o, i, 1.0 / D, None,
                                                       op0=ALU.mult),
              reads=[pfx + "s1"], writes=[pfx + "m"])
        P.add("vector", lambda e, i=c_m, o=c_msq: e.tensor_mul(o, i, i),
              reads=[pfx + "m"], writes=[pfx + "msq"])
        P.add("vector",
              lambda e, i=c_s2, j=c_msq, o=c_var: e.scalar_tensor_tensor(
                  o, i, 1.0 / D, j, op0=ALU.mult, op1=ALU.subtract),
              reads=[pfx + "s2", pfx + "msq"], writes=[pfx + "var"])
        P.add("vector",
              lambda e, o=c_var: e.tensor_scalar(o, o, EPS, None, op0=ALU.add),
              reads=[pfx + "var"], writes=[pfx + "var"])
        P.add("scalar",
              lambda e, i=c_var, o=c_sqv: e.activation(o, i, AF.Sqrt),
              reads=[pfx + "var"], writes=[pfx + "sqv"])
        P.add("vector", lambda e, i=c_sqv, o=c_r: e.reciprocal(o, i),
              reads=[pfx + "sqv"], writes=[pfx + "r"])
        return c_m, c_r, [pfx + "m", pfx + "r"]

    vone_ap = view("v_nat", BF16).rearrange("p (t c) -> p t c", c=65)[:, :, 64:65]
    P.add("vector", lambda e, o=vone_ap: e.memset(o, 1.0),
          writes=[f"v_nat_{st}" for st in range(nst)], tag="vones")
    hT_bf = view("hT", BF16)
    for st in range(nst):
        par = st % 4
        a, bB, ht, sqn = f"bufA{par}", f"bufB{par}", f"htile{par}", "sq0"
        a_ap, b_ap, h_ap = view(a), view(bB), view(ht, BF16)
        P.dma("sync",
              lambda e, o=a_ap, s=st: e.dma_start(o, d_x[s * 128:(s + 1) * 128, :]),
              writes=wreg(a), tag=f"xin{st}")
        cm, cr, mr = ln_stats(a_ap, [f"B:{a}"], f"p1{par}_", sqn)
        P.add("vector",
              lambda e, i=a_ap, o=b_ap, m=cm, r=cr: e.tensor_scalar(
                  o, i, m, r, op0=ALU.subtract, op1=ALU.mult),
              reads=[f"B:{a}"] + mr, writes=wreg(bB))
        cm2, cr2, mr2 = ln_stats(b_ap, [f"B:{bB}"], f"p2{par}_", sqn)
        P.add("vector",
              lambda e, o=b_ap, m=cm2, r=cr2: e.tensor_scalar(
                  o, o, m, r, op0=ALU.subtract, op1=ALU.mult),
              reads=[f"B:{bB}"] + mr2, writes=[f"B:{bB}"])
        P.add("vector", lambda e, o=b_ap, g=lng_ap: e.tensor_mul(o, o, g),
              reads=[f"B:{bB}", "B:lng"], writes=[f"B:{bB}"])
        P.add("gpsimd", lambda e, i=b_ap, o=h_ap, bb=lnb_ap: e.tensor_add(o, i, bb),
              reads=[f"B:{bB}", "B:lnb"], writes=wreg(ht))
        for dt in range(ndt):
            bank = dt % 4
            dst = hT_bf[:, dt * S + st * 128: dt * S + st * 128 + 128]
            transpose(pbf16(bank), h_ap[:, dt * 128:(dt + 1) * 128], id_bf_ap,
                      reads=[f"B:{ht}", "B:id_bf"], writes=[f"pbk{bank}"])
            copy_on("scalar", dst, pbf16(bank),
                    reads=[f"pbk{bank}"], writes=[f"hT_{dt}_{st}"])

    dump("dbg_hT", hT_bf, [f"hT_{dt}_{st}" for dt in range(ndt) for st in range(nst)])
    free("lng", ["B:lng"]); free("lnb", ["B:lnb"])
    free("bufA2", ["B:bufA2"]); free("bufA3", ["B:bufA3"])
    free("bufB2", ["B:bufB2"]); free("bufB3", ["B:bufB3"])
    free("htile2", ["B:htile2"]); free("htile3", ["B:htile3"])
    free("bufA0", ["B:bufA0"]); free("bufA1", ["B:bufA1"])
    alloc("wq0", ndt * 512 * 2); alloc("wq1", ndt * 512 * 2)
    wq_dma(0); wq_dma(1)

    alloc("rb0", 1024); alloc("rb1", 1024); alloc("rb2", 1024); alloc("rb3", 1024)
    rbv = [view(f"rb{i}").rearrange("p (h d) -> p h d", h=8) for i in range(4)]

    def rope_evac(bank, dst_name, st, half, ctn, stn):
        psap = pb[bank]
        q1 = psap.rearrange("p (h d) -> p h d", h=8)[:, :, 0:32]
        q2 = psap.rearrange("p (h d) -> p h d", h=8)[:, :, 32:64]
        cosb = view(ctn)[:, st * 32:(st + 1) * 32].unsqueeze(1).broadcast_to([128, 8, 32])
        sinb = view(stn)[:, st * 32:(st + 1) * 32].unsqueeze(1).broadcast_to([128, 8, 32])
        dst = view(dst_name, BF16)[:, st * 1024 + half * 512:
                                   st * 1024 + (half + 1) * 512]
        d1 = dst.rearrange("p (h d) -> p h d", h=8)[:, :, 0:32]
        d2 = dst.rearrange("p (h d) -> p h d", h=8)[:, :, 32:64]
        preg = [f"pbk{bank}"]
        treg = ["B:" + ctn, "B:" + stn]
        dreg = [f"{dst_name}_{st}"]
        P.add("vector", lambda e, o=rbv[0], i=q1, c=cosb: e.tensor_mul(o, i, c),
              reads=preg + treg, writes=["B:rb0"])
        P.add("vector", lambda e, o=rbv[1], i=q2, c=sinb: e.tensor_mul(o, i, c),
              reads=preg + treg, writes=["B:rb1"])
        P.add("vector", lambda e, o=rbv[2], i=q2, c=cosb: e.tensor_mul(o, i, c),
              reads=preg + treg, writes=["B:rb2"])
        P.add("vector", lambda e, o=rbv[3], i=q1, c=sinb: e.tensor_mul(o, i, c),
              reads=preg + treg, writes=["B:rb3"])
        P.add("vector", lambda e, o=d1: e.tensor_sub(o, rbv[0], rbv[1]),
              reads=["B:rb0", "B:rb1"], writes=dreg)
        P.add("vector", lambda e, o=d2: e.tensor_add(o, rbv[2], rbv[3]),
              reads=["B:rb2", "B:rb3"], writes=dreg)

    alloc("qn_acc", 16 * 4); alloc("kn_acc", 16 * 4); alloc("n_tmp", 16 * 4)

    def norm_tile(src_name, accn, st):
        acc_ap = view(accn)[:, 0:16]
        tmp_ap = view("n_tmp")[:, 0:16]
        sreg = [f"{src_name}_{st}"]
        src = view(src_name, BF16)[:, st * 1024:(st + 1) * 1024]
        sq_ap = view("sq0")
        P.add("vector", lambda e, s=src, o=sq_ap: e.tensor_mul(o, s, s),
              reads=sreg, writes=["B:sq0"])
        dst = acc_ap if st == 0 else tmp_ap
        dregs = [f"B:{accn}"] if st == 0 else ["R:ntmp"]
        P.add("vector",
              lambda e, o=dst, i=sq_ap: e.reduce_sum(
                  o.unsqueeze(2), i.rearrange("p (h d) -> p h d", h=16),
                  axis=AX.X),
              reads=["B:sq0"], writes=dregs)
        if st > 0:
            P.add("vector",
                  lambda e, a=acc_ap, t=tmp_ap: e.tensor_max(a, a, t),
                  reads=["R:ntmp", f"B:{accn}"], writes=[f"B:{accn}"])

    _pre_chunks = []
    for u in range(nunits):
        for c in range(8):
            _pre_chunks.append(("w1", u, c))
            _pre_chunks.append(("w2", u, c))

    def pump_prepass(n):
        for _ in range(n):
            if not _pre_chunks:
                return
            kind, u, c = _pre_chunks.pop(0)
            if kind == "w1":
                src = d_w1[u, c * 128:(c + 1) * 128, :]
                dst = d_w1pre[u, c * 128:(c + 1) * 128, :]
                P.dma("gpsimd", lambda e, o=dst, s=src: e.dma_start(o, s),
                      writes=[f"w1pre_{u}_{c}"], tag=f"wpre1_{u}_{c}")
            else:
                src = d_w2[u, c * 256:(c + 1) * 256, :]
                dst = d_w2pre[u, c * 256:(c + 1) * 256, :]
                P.dma("gpsimd", lambda e, o=dst, s=src: e.dma_start(o, s),
                      writes=[f"w2pre_{u}_{c}"], tag=f"wpre2_{u}_{c}")

    nfc = 3 * D // 512
    nqf = D // 512
    for fc in range(nfc):
        wbuf = f"wq{fc % 2}"
        wb_ap = view(wbuf, BF16)
        for st in range(nst):
            bank = 4 + (st % 4)
            for dt in range(ndt):
                mm(pb[bank],
                   hT_bf[:, dt * S + st * 128: dt * S + st * 128 + 128],
                   wb_ap[:, dt * 512:(dt + 1) * 512],
                   start=(dt == 0), stop=(dt == ndt - 1),
                   reads=[f"hT_{dt}_{st}", f"wqbuf{fc % 2}"],
                   writes=[f"pbk{bank}"], tag=f"qkv{fc}_{st}")
            if fc < nqf:
                rope_evac(bank, "q_nat", st, fc % nqf, "ctq", "stq")
            elif fc < 2 * nqf:
                rope_evac(bank, "k_nat", st, fc % nqf, "ctk", "stk")
            else:
                half = fc % nqf
                dst = view("v_nat", BF16)
                dst = dst[:, st * 1040 + half * 8 * 65:
                          st * 1040 + (half + 1) * 8 * 65]
                dst = dst.rearrange("p (h c) -> p h c", c=65)[:, :, 0:64]
                src_ps = pb[bank].rearrange("p (h c) -> p h c", c=64)
                P.add("scalar", lambda e, d=dst, s=src_ps: e.copy(d, s),
                      reads=[f"pbk{bank}"], writes=[f"v_nat_{st}"])
            if fc == 2:
                norm_tile("q_nat", "qn_acc", st)
            elif fc == 4:
                norm_tile("k_nat", "kn_acc", st)
            if st % 4 == 3:
                pump_prepass(1)
        if fc + 2 < nfc:
            wq_dma(fc + 2)

    dump("dbg_qn", view("q_nat", BF16), [f"q_nat_{st}" for st in range(nst)])
    dump("dbg_kn", view("k_nat", BF16), [f"k_nat_{st}" for st in range(nst)])
    dump("dbg_vn", view("v_nat", BF16), [f"v_nat_{st}" for st in range(nst)])
    free("hT", [f"hT_{dt}_{st}" for dt in range(ndt) for st in range(nst)])
    for nm in ("rb0", "rb1", "rb2", "rb3"):
        free(nm, [f"B:{nm}"])
    for nm in ("ctq", "stq", "ctk", "stk"):
        free(nm, [f"B:{nm}"])
    free("wq0", ["B:wq0", "wqbuf0"]); free("wq1", ["B:wq1", "wqbuf1"])

    c_qn, c_kn = col("qn1"), col("kn1")
    P.add("vector",
          lambda e, o=c_qn, i=view("qn_acc")[:, 0:16]: e.reduce_max(o, i, axis=AX.X),
          reads=["B:qn_acc"], writes=["c_qn1"])
    P.add("vector",
          lambda e, o=c_kn, i=view("kn_acc")[:, 0:16]: e.reduce_max(o, i, axis=AX.X),
          reads=["B:kn_acc"], writes=["c_kn1"])
    transpose(pb[0][0:1, 0:128], c_qn, id_f_ap,
              reads=["c_qn1", "B:id_f"], writes=["pbk0"])
    c_mq, c_mk, c_mm2, c_m, c_nm = (col("mq"), col("mk"), col("mm2"), col("m"),
                                    col("nm"))
    P.add("vector", lambda e, o=c_mq: e.reduce_max(o[0:1, :], pb[0][0:1, 0:128],
                                                   axis=AX.X),
          reads=["pbk0"], writes=["c_mq"])
    transpose(pb[1][0:1, 0:128], c_kn, id_f_ap,
              reads=["c_kn1", "B:id_f"], writes=["pbk1"])
    P.add("vector", lambda e, o=c_mk: e.reduce_max(o[0:1, :], pb[1][0:1, 0:128],
                                                   axis=AX.X),
          reads=["pbk1"], writes=["c_mk"])
    P.add("vector",
          lambda e, o=c_mm2, a=c_mq, b=c_mk: e.tensor_mul(o[0:1, :], a[0:1, :],
                                                          b[0:1, :]),
          reads=["c_mq", "c_mk"], writes=["c_mm2"])
    P.add("scalar",
          lambda e, o=c_m, i=c_mm2: e.activation(o[0:1, :], i[0:1, :], AF.Sqrt),
          reads=["c_mm2"], writes=["c_m"])
    P.add("vector",
          lambda e, o=c_nm, i=c_m: e.tensor_scalar(o[0:1, :], i[0:1, :], -1.0,
                                                   None, op0=ALU.mult),
          reads=["c_m"], writes=["c_nm"])
    mm(pb[2][:, 0:1], onesf[0:1, 0:128], c_nm[0:1, :], True, True,
       reads=["c_nm", "B:ones_f"], writes=["pbk2"], tag="negM_bc")
    P.add("vector", lambda e, o=negM_ap: e.tensor_copy(o, pb[2][:, 0:1]),
          reads=["pbk2"], writes=wreg("negM"))
    dump("dbg_negM", negM_ap, ["B:negM"])
    free("qn_acc", ["B:qn_acc"]); free("kn_acc", ["B:kn_acc"])
    free("n_tmp", ["R:ntmp"])

    alloc("qT", ndt * S * 2, [f"qT_{dt}_{st}" for dt in range(ndt) for st in range(nst)])
    qT_bf = view("qT", BF16)
    for st in range(nst):
        for dt in range(ndt):
            bank = dt % 4
            transpose(pbf16(bank),
                      view("q_nat", BF16)[:, st * 1024 + dt * 128:
                                          st * 1024 + (dt + 1) * 128],
                      id_bf_ap, reads=[f"q_nat_{st}", "B:id_bf"],
                      writes=[f"pbk{bank}"])
            copy_on("scalar",
                    qT_bf[:, dt * S + st * 128: dt * S + st * 128 + 128],
                    pbf16(bank), reads=[f"pbk{bank}"], writes=[f"qT_{dt}_{st}"])
    free("q_nat", [f"q_nat_{st}" for st in range(nst)])
    alloc("kT", ndt * S * 2, [f"kT_{dt}_{st}" for dt in range(ndt) for st in range(nst)])
    kT_bf = view("kT", BF16)
    for st in range(nst):
        for dt in range(ndt):
            bank = 4 + dt % 4
            transpose(pbf16(bank),
                      view("k_nat", BF16)[:, st * 1024 + dt * 128:
                                          st * 1024 + (dt + 1) * 128],
                      id_bf_ap, reads=[f"k_nat_{st}", "B:id_bf"],
                      writes=[f"pbk{bank}"])
            copy_on("scalar",
                    kT_bf[:, dt * S + st * 128: dt * S + st * 128 + 128],
                    pbf16(bank), reads=[f"pbk{bank}"], writes=[f"kT_{dt}_{st}"])
    free("k_nat", [f"k_nat_{st}" for st in range(nst)])

    alloc("oT", ndt * S * 2, [f"oT_{hp}_{ic}_{p}" for hp in range(ndt) for ic in range(nic) for p in (0, 1)])
    alloc("slabA", njt * 512 * 2, [f"slabA_{jt}" for jt in range(njt)])
    alloc("slabB", njt * 512 * 2, [f"slabB_{jt}" for jt in range(njt)])
    alloc("recbuf", 1024 * 4, ["rec0", "rec1"])
    alloc("bcscr", 1024 * 4, ["bcs0", "bcs1"])
    alloc("ostg", 1024 * 2, ["ostg0", "ostg1"])
    oT_bf = view("oT", BF16)
    slab_ap = {"slabA": view("slabA", BF16), "slabB": view("slabB", BF16)}
    v_bf = view("v_nat", BF16)
    rec_ap = view("recbuf", F32, cols=1024)
    bcs_ap = view("bcscr", F32, cols=1024)
    ostg_ap = view("ostg", BF16, cols=1024)

    def emit_spair(hp, ic, jt):
        for (hh, r0, bank, slab) in ((2 * hp, 0, jt % 2, "slabA"),
                                     (2 * hp + 1, 64, 2 + jt % 2, "slabB")):
            lhs = kT_bf[r0:r0 + 64, hp * S + jt * 128: hp * S + (jt + 1) * 128]
            rhs = qT_bf[r0:r0 + 64, hp * S + ic * 512: hp * S + (ic + 1) * 512]
            rds = [f"kT_{hp}_{jt}"] + [f"qT_{hp}_{4 * ic + q}" for q in range(4)]
            mm(pb[bank], lhs, rhs, True, True, reads=rds,
               writes=[f"pbk{bank}"], tp=(r0, 0), tag=f"S{hh}_{ic}_{jt}")
            dst = slab_ap[slab][:, jt * 512:(jt + 1) * 512]
            P.add("scalar",
                  lambda e, o=dst, bk=bank, nm=negM_ap: e.activation(
                      o, pb[bk], AF.Exp, bias=nm, scale=1.0),
                  reads=[f"pbk{bank}", "B:negM"],
                  writes=[f"{slab}_{jt}"], tag=f"exp{hh}_{ic}_{jt}")

    def emit_ovstep(hp, ic, jt):
        for (hh, slab, bank) in ((2 * hp, "slabA", 4), (2 * hp + 1, "slabB", 5)):
            vl = v_bf[:, jt * 1040 + hh * 65: jt * 1040 + hh * 65 + 65]
            sl = slab_ap[slab][:, jt * 512:(jt + 1) * 512]
            mm(pb[bank][0:65, :], vl, sl, jt == 0, jt == njt - 1,
               reads=[f"v_nat_{jt}", f"{slab}_{jt}"], writes=[f"pbk{bank}"],
               tp=(0, 0), tag=f"ov{hh}_{ic}_{jt}")

    def emit_chunk(hp, ic, hook=None):
        LOOK = 2
        for jt in range(njt):
            emit_spair(hp, ic, jt)
            if jt == 0 and hook is not None:
                hook()
            if jt >= LOOK:
                emit_ovstep(hp, ic, jt - LOOK)
        for jt in range(njt - LOOK, njt):
            emit_ovstep(hp, ic, jt)

    def emit_norm(hp, ic):
        P.add("vector",
              lambda e, o=rec_ap: e.reciprocal(o[64:65, 0:512], pb[4][64:65, :]),
              reads=["pbk4"], writes=["rec0"])
        P.add("vector",
              lambda e, o=rec_ap: e.reciprocal(o[64:65, 512:1024], pb[5][64:65, :]),
              reads=["pbk5"], writes=["rec1"])
        mm(pb[7][0:64, :], onesf[64:65, 0:64], rec_ap[64:65, 0:512], True, True,
           reads=["rec0", "B:ones_f"], writes=["pbk7"],
           tp=(64, 0), tag=f"bc{2 * hp}_{ic}")
        P.add("vector",
              lambda e, o=bcs_ap: e.tensor_copy(o[0:64, 0:512], pb[7][0:64, :]),
              reads=["pbk7"], writes=["bcs0"])
        dst = oT_bf[0:64, hp * S + ic * 512: hp * S + (ic + 1) * 512]
        P.add("vector",
              lambda e, o=dst, c=bcs_ap: e.tensor_mul(
                  o, pb[4][0:64, :], c[0:64, 0:512]),
              reads=["pbk4", "bcs0"],
              writes=[f"oT_{hp}_{ic}_0"], tag=f"onrm{2 * hp}_{ic}")
        mm(pb[6][0:64, :], onesf[64:65, 0:64], rec_ap[64:65, 512:1024], True, True,
           reads=["rec1", "B:ones_f"], writes=["pbk6"],
           tp=(64, 0), tag=f"bc{2 * hp + 1}_{ic}")
        P.add("vector",
              lambda e, o=bcs_ap: e.tensor_copy(o[0:64, 512:1024], pb[6][0:64, :]),
              reads=["pbk6"], writes=["bcs1"])
        par = ic % 2
        stg = ostg_ap[0:64, par * 512:(par + 1) * 512]
        P.add("vector",
              lambda e, o=stg, c=bcs_ap: e.tensor_mul(
                  o, pb[5][0:64, :], c[0:64, 512:1024]),
              reads=["pbk5", "bcs1"], writes=[f"ostg{par}"],
              tag=f"onrm{2 * hp + 1}_{ic}")
        dsto = oT_bf[64:128, hp * S + ic * 512: hp * S + (ic + 1) * 512]
        P.dma("sync", lambda e, o=dsto, i=stg: e.dma_start(o, i),
              reads=[f"ostg{par}"], writes=[f"oT_{hp}_{ic}_1"],
              tag=f"oshift{2 * hp + 1}_{ic}")

    pending = []
    for hp in range(NH // 2):
        for ic in range(nic):
            hook = None
            if pending:
                prev = pending.pop(0)
                hook = (lambda pv: (lambda: emit_norm(*pv)))(prev)
            emit_chunk(hp, ic, hook)
            if dbg and hp == 0 and ic == 0:
                dump("dbg_slab", slab_ap["slabA"],
                     [f"slabA_{jt}" for jt in range(njt)])
            pending.append((hp, ic))
            pump_prepass(4)
        if hp == NH // 2 - 1:
            free("kT", [f"kT_{dt}_{st}" for dt in range(ndt) for st in range(nst)])
            alloc("woutT", ndt * 1024 * 2)
            wout_dst = view("woutT", BF16).rearrange("p (d f) -> p d f", f=1024)
            P.dma("gpsimd",
                  lambda e, o=wout_dst: e.dma_start(
                      o, d_woutT.rearrange("(d p) f -> p d f", p=128)),
                  writes=wreg("woutT"))
    while pending:
        emit_norm(*pending.pop(0))
    pump_prepass(len(_pre_chunks))

    dump("dbg_oT", oT_bf, [f"oT_{hp}_{ic}_{p}" for hp in range(ndt)
                           for ic in range(nic) for p in (0, 1)])
    free("slabA", [f"slabA_{jt}" for jt in range(njt)])
    free("slabB", [f"slabB_{jt}" for jt in range(njt)])
    free("v_nat", [f"v_nat_{st}" for st in range(nst)])
    free("qT", [f"qT_{dt}_{st}" for dt in range(ndt) for st in range(nst)])
    free("recbuf", ["rec0", "rec1"])
    free("bcscr", ["bcs0", "bcs1"])
    free("ostg", ["ostg0", "ostg1"])

    alloc("bufA0", 4096); alloc("bufA1", 4096)
    alloc("bufA2", 4096); alloc("bufA3", 4096)
    alloc("bufB2", 4096); alloc("bufB3", 4096)
    alloc("htile2", 2048); alloc("htile3", 2048)
    alloc("bout", 4096)
    bout_ap = view("bout")[0:1, 0:D]
    P.dma("sync", lambda e, o=bout_ap: e.dma_start(o, d_bout), writes=wreg("bout"))
    alloc("flatT", ndt * S * 2, [f"flatT_{dt}_{st}" for dt in range(ndt) for st in range(nst)])
    flat_bf = view("flatT", BF16)
    wout_bf = view("woutT", BF16)

    h0z = view("htile0", BF16)
    P.add("vector", lambda e, o=h0z: e.memset(o, 0.0), writes=["B:htile0"],
          tag="flatb_z")
    P.dma("sync",
          lambda e, i=h0z: e.dma_start(d_flatb[S:S + 128, :], i),
          reads=["B:htile0"], writes=["flatb_trash"], tag="flatb_zw")

    def p4_front(st):
        par = st % 4
        a, bB, ht, sqn = f"bufA{par}", f"bufB{par}", f"htile{par}", "sq0"
        a_ap, b_ap, h_ap = view(a), view(bB), view(ht, BF16)
        P.dma("sync",
              lambda e, o=a_ap, s=st: e.dma_start(o, d_x[s * 128:(s + 1) * 128, :]),
              writes=wreg(a), tag=f"xre{st}")
        for dc in range(ndc):
            bank = (2 * st + dc) % 4
            for dt in range(ndt):
                mm(pb[bank],
                   oT_bf[:, dt * S + st * 128: dt * S + st * 128 + 128],
                   wout_bf[:, dt * 1024 + dc * 512: dt * 1024 + (dc + 1) * 512],
                   start=(dt == 0), stop=False,
                   reads=[f"oT_{dt}_{st // 4}_{p}" for p in (0, 1)] + ["B:woutT"],
                   writes=[f"pbk{bank}"], tag=f"aopj{st}_{dc}")
            mm(pb[bank], onesf[0:1, 0:128], bout_ap[:, dc * 512:(dc + 1) * 512],
               False, True, reads=["B:bout", "B:ones_f"], writes=[f"pbk{bank}"],
               tag=f"aob{st}_{dc}")
            copy_on("scalar", b_ap[:, dc * 512:(dc + 1) * 512], pb[bank],
                    reads=[f"pbk{bank}"], writes=[f"B:{bB}"])
        cm, cr, mr = ln_stats(b_ap, [f"B:{bB}"], f"p3{par}_", sqn)
        P.add("vector",
              lambda e, o=b_ap, m=cm, r=cr: e.tensor_scalar(
                  o, o, m, r, op0=ALU.subtract, op1=ALU.mult),
              reads=[f"B:{bB}"] + mr, writes=[f"B:{bB}"])
        P.add("gpsimd", lambda e, o=b_ap, i=a_ap: e.tensor_add(o, o, i),
              reads=[f"B:{bB}", f"B:{a}"], writes=[f"B:{bB}"])
        P.dma("sync",
              lambda e, i=b_ap, s=st: e.dma_start(d_xsp[s * 128:(s + 1) * 128, :], i),
              reads=[f"B:{bB}"], writes=[f"xsp_{st}"], tag=f"spill{st}")
        cm4, cr4, mr4 = ln_stats(b_ap, [f"B:{bB}"], f"p4{par}_", sqn)
        P.add("vector",
              lambda e, i=b_ap, o=h_ap, m=cm4, r=cr4: e.tensor_scalar(
                  o, i, m, r, op0=ALU.subtract, op1=ALU.mult),
              reads=[f"B:{bB}"] + mr4, writes=wreg(ht))
        P.dma("sync",
              lambda e, i=h_ap, s=st: e.dma_start(
                  d_flatb[s * 128:(s + 1) * 128, :], i),
              reads=[f"B:{ht}"], writes=[f"flatb_{st}"], tag=f"flath{st}")

    def p4_back(st):
        par = st % 4
        ht = f"htile{par}"
        h_ap = view(ht, BF16)
        for dt in range(ndt):
            bank = 4 + dt % 4
            transpose(pbf16(bank), h_ap[:, dt * 128:(dt + 1) * 128], id_bf_ap,
                      reads=[f"B:{ht}", "B:id_bf"], writes=[f"pbk{bank}"],
                      tag=f"ftp{st}_{dt}")
            copy_on("scalar",
                    flat_bf[:, dt * S + st * 128: dt * S + st * 128 + 128],
                    pbf16(bank), reads=[f"pbk{bank}"], writes=[f"flatT_{dt}_{st}"])

    for st in range(nst):
        p4_front(st)
        if st > 0:
            p4_back(st - 1)
    p4_back(nst - 1)

    free("oT", [f"oT_{hp}_{ic}_{p}" for hp in range(ndt) for ic in range(nic)
                for p in (0, 1)])
    free("woutT", ["B:woutT"])
    free("bout", ["B:bout"])

    alloc("w_sb", nst * NE * 4, [f"w_{tt}" for tt in range(nst)])
    alloc("m_sb", nst * NE * 4, [f"m_{tt}" for tt in range(nst)])
    alloc("rbq", NE * 4)
    alloc("lgt", NE * 4); alloc("prb", NE * 4); alloc("msk", NE * 4)
    w_ap = view("w_sb")
    m_ap = view("m_sb")
    rbq_ap = view("rbq")[:, 0:NE]
    lgt_ap = view("lgt")[:, 0:NE]
    prb_ap = view("prb")[:, 0:NE]
    msk_ap = view("msk")[:, 0:NE]
    rwq_bf = view("rwq", BF16)
    P.dma("sync", lambda e, o=rbq_ap: e.dma_start(o, d_rbq), writes=wreg("rbq"))
    c_nmx, c_rs, c_m1p, c_m2p = col("nmx"), col("rs"), col("m1p"), col("m2p")
    c_rrs = col("rrs")
    for tt in range(nst):
        bank = tt % 2
        for dt in range(ndt):
            mm(pb[bank][:, 0:128],
               flat_bf[:, dt * S + tt * 128: dt * S + tt * 128 + 128],
               rwq_bf[:, dt * 128:(dt + 1) * 128],
               start=(dt == 0), stop=(dt == ndt - 1),
               reads=[f"flatT_{dt}_{tt}", "B:rwq"], writes=[f"pbk{bank}"],
               tag=f"rtr{tt}")
        P.add("vector",
              lambda e, bk=bank, o=lgt_ap, b=rbq_ap: e.tensor_add(
                  o, pb[bk][:, 0:NE], b),
              reads=[f"pbk{bank}", "B:rbq"], writes=["B:lgt"])
        P.add("vector", lambda e, o=c_nmx, i=lgt_ap: e.reduce_max(
            o, i, axis=AX.X, negate=True),
              reads=["B:lgt"], writes=["c_nmx"])
        P.add("scalar",
              lambda e, o=prb_ap, i=lgt_ap, b=c_nmx: e.activation(
                  o, i, AF.Exp, bias=b),
              reads=["B:lgt", "c_nmx"], writes=["B:prb"])
        P.add("vector", lambda e, o=c_rs, i=prb_ap: e.reduce_sum(o, i, axis=AX.X),
              reads=["B:prb"], writes=["c_rs"])
        P.add("vector", lambda e, o=c_rrs, i=c_rs: e.reciprocal(o, i),
              reads=["c_rs"], writes=["c_rrs"])
        P.add("vector",
              lambda e, o=prb_ap, s=c_rrs: e.tensor_scalar(o, o, s, None,
                                                           op0=ALU.mult),
              reads=["B:prb", "c_rrs"], writes=["B:prb"])
        P.add("vector", lambda e, o=c_m1p, i=prb_ap: e.reduce_max(o, i, axis=AX.X),
              reads=["B:prb"], writes=["c_m1p"])
        P.add("vector",
              lambda e, o=msk_ap, i=prb_ap, m=c_m1p: e.tensor_scalar(
                  o, i, m, -2.0, op0=ALU.is_equal, op1=ALU.mult),
              reads=["B:prb", "c_m1p"], writes=["B:msk"])
        P.add("vector", lambda e, o=msk_ap, i=prb_ap: e.tensor_add(o, o, i),
              reads=["B:msk", "B:prb"], writes=["B:msk"])
        P.add("vector", lambda e, o=c_m2p, i=msk_ap: e.reduce_max(o, i, axis=AX.X),
              reads=["B:msk"], writes=["c_m2p"])
        mdst = m_ap[:, tt * NE:(tt + 1) * NE]
        P.add("vector",
              lambda e, o=mdst, i=prb_ap, m=c_m2p: e.tensor_scalar(
                  o, i, m, None, op0=ALU.is_ge),
              reads=["B:prb", "c_m2p"], writes=[f"m_{tt}"])
        wdst = w_ap[:, tt * NE:(tt + 1) * NE]
        P.add("vector",
              lambda e, o=wdst, i=prb_ap, m=mdst: e.tensor_mul(o, i, m),
              reads=["B:prb", f"m_{tt}"], writes=[f"w_{tt}"])
    dump("dbg_flatT", flat_bf, [f"flatT_{dt}_{st}" for dt in range(ndt)
                                for st in range(nst)])
    dump("dbg_w", w_ap, [f"w_{tt}" for tt in range(nst)])
    free("rwq", ["B:rwq"])
    free("lgt", ["B:lgt"]); free("prb", ["B:prb"]); free("msk", ["B:msk"])
    free("rbq", ["B:rbq"])

    BIGS = 2.0e6
    alloc("tri_sb", 512)
    alloc("iota_sb", nst * 4)
    alloc("erow", 16 * 4)
    alloc("srt", 64 * 4)
    alloc("sig", 16 * 4)
    alloc("psf", nst * NE * 4)
    alloc("bas", nst * NE * 4)
    alloc("bas3", nst * NE * 4)
    alloc("bsc", NE * 4)
    alloc("irep", 512)
    alloc("pbuf", 512)
    alloc("cw", 16 * 4)
    alloc("stg", 16 * 4)
    alloc("sent", (NTILE + 1) * 2 * 4)
    tri_ap = view("tri_sb", F32, cols=128)
    irep_ap = view("irep", F32, cols=128)
    pbuf_ap = view("pbuf", F32, cols=128)
    cw_ap = view("cw", F32, cols=16)
    psf_ap = view("psf", F32, cols=nst * NE)
    bas_i = view("bas", I32, cols=nst * NE)
    bas3_i = view("bas3", I32, cols=nst * NE)
    bsc_ap = view("bsc", F32, cols=NE)
    iota_ap = view("iota_sb", F32, cols=nst)
    erow_ap = view("erow", F32, cols=16)
    srt_ap = view("srt", F32, cols=64)
    sig_i = view("sig", I32, cols=16)
    stg_ap = view("stg", F32, cols=16)
    sent_ap = view("sent", F32, cols=(NTILE + 1) * 2)
    P.dma("sync", lambda e, o=tri_ap: e.dma_start(o, d_tri), writes=wreg("tri_sb"))
    P.dma("sync", lambda e, o=irep_ap: e.dma_start(o, d_irep), writes=wreg("irep"))
    P.dma("sync", lambda e, o=iota_ap: e.dma_start(o, d_itok), writes=wreg("iota_sb"))
    P.dma("sync", lambda e, o=erow_ap[0:1, 0:NE]: e.dma_start(o, d_eidx),
          writes=wreg("erow"))
    P.dma("sync", lambda e, o=erow_ap[0:1, 8:8 + NE]: e.dma_start(o, d_etag),
          writes=["B:erow"])
    P.add("vector", lambda e, o=sent_ap: e.memset(o, 1.0e5), writes=wreg("sent"))
    P.dma("scalar",
          lambda e, o=d_tokw.rearrange("(r p) c -> p r c", p=128),
          i=sent_ap.rearrange("p (r c) -> p r c", c=2): e.dma_start(o, i),
          reads=["B:sent"],
          writes=["tokw"] + [f"tokw_e{ee}" for ee in range(NE)], tag="tokw_init")

    for tt in range(nst):
        mm(pb[6][0:1, 0:NE], onesf[:, 0:1], m_ap[:, tt * NE:(tt + 1) * NE],
           start=(tt == 0), stop=(tt == nst - 1),
           reads=[f"m_{tt}", "B:ones_f"], writes=["pbk6"], tag=f"lds{tt}")
    LD, OH, TMP, XOF, MK, SGF = 0, 8, 16, 24, 40, 48
    P.add("vector", lambda e, o=srt_ap[0:1, LD:LD + NE]: e.tensor_copy(
        o, pb[6][0:1, 0:NE]), reads=["pbk6"], writes=["srt"])
    P.add("vector",
          lambda e, o=srt_ap[0:1, LD:LD + NE], t=erow_ap[0:1, 8:8 + NE]:
          e.tensor_add(o, o, t),
          reads=["srt", "B:erow"], writes=["srt"])
    for k in range(NE):
        P.add("vector",
              lambda e, o=srt_ap[0:1, MK:MK + 1], i=srt_ap[0:1, LD:LD + NE]:
              e.reduce_max(o, i, axis=AX.X),
              reads=["srt"], writes=["srt"])
        P.add("vector",
              lambda e, o=srt_ap[0:1, OH:OH + NE], i=srt_ap[0:1, LD:LD + NE],
              m=srt_ap[0:1, MK:MK + 1]: e.tensor_scalar(
                  o, i, m, None, op0=ALU.is_equal),
              reads=["srt"], writes=["srt"])
        P.add("vector",
              lambda e, o=srt_ap[0:1, TMP:TMP + NE], a=srt_ap[0:1, OH:OH + NE],
              b=erow_ap[0:1, 0:NE]: e.tensor_mul(o, a, b),
              reads=["srt", "B:erow"], writes=["srt"])
        P.add("vector",
              lambda e, o=srt_ap[0:1, SGF + k:SGF + k + 1],
              i=srt_ap[0:1, TMP:TMP + NE]: e.reduce_sum(o, i, axis=AX.X),
              reads=["srt"], writes=["srt"])
        if k == 0:
            P.add("vector",
                  lambda e, o=srt_ap[0:1, XOF:XOF + NE],
                  a=srt_ap[0:1, OH:OH + NE], u=float(UOFF[0]): e.tensor_scalar(
                      o, a, u, None, op0=ALU.mult),
                  reads=["srt"], writes=["srt"])
        else:
            P.add("vector",
                  lambda e, o=srt_ap[0:1, XOF:XOF + NE],
                  a=srt_ap[0:1, OH:OH + NE], u=float(UOFF[k]):
                  e.scalar_tensor_tensor(o, a, u, o, op0=ALU.mult, op1=ALU.add),
                  reads=["srt"], writes=["srt"])
        P.add("vector",
              lambda e, o=srt_ap[0:1, LD:LD + NE], a=srt_ap[0:1, OH:OH + NE]:
              e.scalar_tensor_tensor(o, a, -1.0e9, o, op0=ALU.mult, op1=ALU.add),
              reads=["srt"], writes=["srt"])
    P.add("vector",
          lambda e, o=sig_i[0:1, 0:NE], i=srt_ap[0:1, SGF:SGF + NE]:
          e.tensor_copy(o, i),
          reads=["srt"], writes=["sig"])

    for tt in range(nst):
        mm(pb[7][:, 0:NE], onesf[0:1, 0:128], srt_ap[0:1, XOF:XOF + NE],
           start=True, stop=False, reads=["srt", "B:ones_f"], writes=["pbk7"],
           tag=f"sb{tt}")
        for tp_ in range(tt):
            mm(pb[7][:, 0:NE], onesf, m_ap[:, tp_ * NE:(tp_ + 1) * NE],
               start=False, stop=False,
               reads=[f"m_{tp_}", "B:ones_f"], writes=["pbk7"], tag=f"sc{tt}_{tp_}")
        mm(pb[7][:, NE:2 * NE], tri_ap, m_ap[:, tt * NE:(tt + 1) * NE],
           start=False, stop=True,
           reads=[f"m_{tt}", "B:tri_sb"], writes=["pbk7"], tag=f"st{tt}")
        P.add("vector",
              lambda e, o=bsc_ap[0:1, 0:NE]: e.tensor_scalar(
                  o, pb[7][0:1, 0:NE], 8.0, None, op0=ALU.mult),
              reads=["pbk7"], writes=["B:bsc"])
        P.add("vector",
              lambda e, o=bas_i[0:1, tt * NE:(tt + 1) * NE],
              i=bsc_ap[0:1, 0:NE]: e.tensor_copy(o, i),
              reads=["B:bsc"], writes=[f"bas_{tt}"])
        P.add("vector",
              lambda e, o=bsc_ap[0:1, 0:NE]: e.tensor_scalar(
                  o, pb[7][0:1, 0:NE], 2048.0, None, op0=ALU.mult),
              reads=["pbk7", f"bas_{tt}"], writes=["B:bsc"])
        P.add("vector",
              lambda e, o=bas3_i[0:1, tt * NE:(tt + 1) * NE],
              i=bsc_ap[0:1, 0:NE]: e.tensor_copy(o, i),
              reads=["B:bsc"], writes=[f"bas_{tt}"])
        pslice = psf_ap[:, tt * NE:(tt + 1) * NE]
        P.add("vector",
              lambda e, o=pslice, m=m_ap[:, tt * NE:(tt + 1) * NE]:
              e.scalar_tensor_tensor(o, m, 1000.0, pb[7][:, NE:2 * NE],
                                     op0=ALU.mult, op1=ALU.add),
              reads=[f"m_{tt}", "pbk7"], writes=[f"psf_{tt}"])
        P.add("vector",
              lambda e, o=pslice: e.tensor_scalar(o, o, -1000.0, None,
                                                  op0=ALU.add),
              reads=[f"psf_{tt}"], writes=[f"psf_{tt}"])
        stg3 = stg_ap.rearrange("p (e c) -> p e c", c=2)
        P.add("vector",
              lambda e, o=stg3[:, :, 0:1],
              i=iota_ap[:, tt:tt + 1].unsqueeze(1).broadcast_to([128, NE, 1]):
              e.tensor_copy(o, i),
              reads=["B:iota_sb"], writes=["B:stg"])
        P.add("vector",
              lambda e, o=stg3[:, :, 1:2],
              i=w_ap[:, tt * NE:(tt + 1) * NE].unsqueeze(2): e.tensor_copy(o, i),
              reads=[f"w_{tt}"], writes=["B:stg"])
        for ee in range(NE):
            P.add("vector",
                  lambda e, o=pbuf_ap, m=psf_ap[:, tt * NE + ee:tt * NE + ee + 1]:
                  e.tensor_scalar(o, irep_ap, m, None, op0=ALU.is_equal),
                  reads=[f"psf_{tt}", "B:irep"], writes=["B:pbuf"],
                  tag=f"pb{tt}_{ee}")
            mm(pb[6][:, 2 * ee:2 * ee + 2], pbuf_ap,
               stg_ap[:, 2 * ee:2 * ee + 2],
               start=(ee == 0), stop=(ee == NE - 1),
               reads=["B:pbuf", "B:stg"], writes=["pbk6"], tag=f"cm{tt}_{ee}")
        P.add("vector", lambda e, o=cw_ap: e.tensor_copy(o, pb[6][:, 0:16]),
              reads=["pbk6"], writes=["B:cw"])
        for ee in range(NE):
            def bwfn(e_, sem, _tt=tt, _ee=ee,
                     _src=cw_ap[:, 2 * ee:2 * ee + 2].bitcast(U8)):
                with e_.register() as r:
                    e_.load(r, bas_i[0:1, _tt * NE + _ee:_tt * NE + _ee + 1])
                    sv = e_.compute_val(r, donate=True)
                    dst = bass.AP(d_tokw8.tensor, sv, [[8, 128], [1, 8]])
                    e_.dma_start(dst, _src).then_inc(sem, 16)
            P.dma_selfinc("gpsimd", bwfn,
                          reads=[f"bas_{tt}", "B:cw", f"tokw_e{ee}"],
                          writes=[f"tokw_e{ee}"], tag=f"bw{tt}_{ee}")

    alloc("h1", 8 * S * 2, [f"h1_{h}_{t}" for h in range(8) for t in range(nic)])
    alloc("rw1a", ndt * 512 * 2, ["rw1buf0"]); alloc("rw1b", ndt * 512 * 2, ["rw1buf1"])
    alloc("rw1c", ndt * 512 * 2, ["rw1buf2"])
    alloc("rw2b", 8 * 1024 * 2, ["rw2buf"])
    alloc("b1t0", 16 * 4, ["b1buf0"]); alloc("b1t1", 16 * 4, ["b1buf1"])
    alloc("b2r0", 4096, ["b2buf0"]); alloc("b2r1", 4096, ["b2buf1"])
    h1_bf = view("h1", BF16)
    rw1_names = ["rw1a", "rw1b", "rw1c"]
    rw1_aps = [view(n, BF16) for n in rw1_names]
    rw2_ap = view("rw2b", BF16)
    b1_aps = [view("b1t0")[:, 0:nht], view("b1t1")[:, 0:nht]]
    b2_aps = [view("b2r0")[0:1, 0:D], view("b2r1")[0:1, 0:D]]
    nq_per_unit = HF // 512

    def rw1_dma(u, qi):
        slot = qi % 3
        nm = rw1_names[slot]
        dst = rw1_aps[slot].rearrange("p (d f) -> p d f", f=512)
        src = d_w1pre[u, :, qi * 512:(qi + 1) * 512].rearrange(
            "(d p) f -> p d f", p=128)
        P.dma("scalar", lambda e, o=dst, s=src: e.dma_start(o, s),
              reads=[f"w1pre_{u}_{c}" for c in range(8)],
              writes=wreg(nm, [f"rw1buf{slot}"]), tag=f"rw1_{u}_{qi}")

    def rw2_dma(u, Hh):
        dst = rw2_ap.rearrange("p (t f) -> p t f", f=1024)
        src = d_w2pre[u, Hh * (HF // 2):(Hh + 1) * (HF // 2), :].rearrange(
            "(t p) f -> p t f", p=128)
        P.dma("scalar", lambda e, o=dst, s=src: e.dma_start(o, s),
              reads=[f"w2pre_{u}_{c}" for c in range(8)],
              writes=wreg("rw2b", ["rw2buf"]), tag=f"rw2_{u}_{Hh}")

    alloc("mx0", 4096); alloc("mx1", 4096)
    rw1_dma(0, 0); rw1_dma(0, 1)
    P.dma("sync", lambda e, o=b1_aps[0]: e.dma_start(o, d_b1[0]),
          writes=wreg("b1t0", ["b1buf0"]), tag="b1_0")
    P.dma("sync", lambda e, o=b2_aps[0]: e.dma_start(o, d_b2[0:1, :]),
          writes=wreg("b2r0", ["b2buf0"]), tag="b2_0")
    for Hh in range(2):
        rw2_dma(0, Hh)
        for ht8 in range(8):
            gq = (Hh * 8 + ht8) // 4
            qslot = gq % 3
            if ht8 % 4 == 0:
                nxt = Hh * 2 + ht8 // 4 + 2
                if nxt < nq_per_unit:
                    rw1_dma(0, nxt)
            qrow = ((Hh * 8 + ht8) % 4) * 128
            for tc in range(nic):
                bank = tc % 4
                for dt in range(ndt):
                    mm(pb[bank],
                       rw1_aps[qslot][:, dt * 512 + qrow: dt * 512 + qrow + 128],
                       flat_bf[:, dt * S + tc * 512: dt * S + (tc + 1) * 512],
                       start=(dt == 0), stop=(dt == ndt - 1),
                       reads=[f"rw1buf{qslot}"] +
                             [f"flatT_{dt}_{4 * tc + q}" for q in range(4)],
                       writes=[f"pbk{bank}"], tag=f"h1s_{Hh}_{ht8}_{tc}")
                gdst = h1_bf[:, ht8 * S + tc * 512: ht8 * S + (tc + 1) * 512]
                gbias = b1_aps[0][:, Hh * 8 + ht8: Hh * 8 + ht8 + 1]
                P.add("scalar",
                      lambda e, o=gdst, bk=bank, b=gbias: e.activation(
                          o, pb[bk], AF.Gelu, bias=b),
                      reads=[f"pbk{bank}", "b1buf0"],
                      writes=[f"h1_{ht8}_{tc}"], tag=f"gls_{Hh}_{ht8}_{tc}")
        for tt in range(nst):
            par = tt % 2
            bB = f"bufB{par}"
            b_ap = view(bB)
            mxr = view(f"mx{par}")
            if Hh == 1:
                P.dma("sync",
                      lambda e, o=mxr, s=tt: e.dma_start(
                          o, d_mixh[s * 128:(s + 1) * 128, :]),
                      reads=[f"mixh_{tt}"], writes=wreg(f"mx{par}"),
                      tag=f"shr{tt}")
            for dc in range(ndc):
                bank = 4 + (2 * tt + dc) % 4
                for ht8 in range(8):
                    last = (ht8 == 7) and (Hh == 0)
                    mm(pb[bank],
                       h1_bf[:, ht8 * S + tt * 128: ht8 * S + tt * 128 + 128],
                       rw2_ap[:, ht8 * 1024 + dc * 512: ht8 * 1024 + (dc + 1) * 512],
                       start=(ht8 == 0), stop=last,
                       reads=[f"h1_{ht8}_{tt // 4}", "rw2buf"],
                       writes=[f"pbk{bank}"], tag=f"o2s_{Hh}_{tt}_{dc}")
                if Hh == 1:
                    mm(pb[bank], onesf[0:1, 0:128],
                       b2_aps[0][:, dc * 512:(dc + 1) * 512],
                       False, True, reads=["b2buf0", "B:ones_f"],
                       writes=[f"pbk{bank}"], tag=f"o2sb_{tt}_{dc}")
                if Hh == 0:
                    P.add("vector",
                          lambda e, bk=bank, o=b_ap[:, dc * 512:(dc + 1) * 512]:
                          e.tensor_copy(o, pb[bk]),
                          reads=[f"pbk{bank}"], writes=[f"B:{bB}"],
                          tag=f"shc_{tt}_{dc}")
                else:
                    P.add("vector",
                          lambda e, bk=bank, o=b_ap[:, dc * 512:(dc + 1) * 512],
                          m_=mxr[:, dc * 512:(dc + 1) * 512]:
                          e.tensor_add(o, pb[bk], m_),
                          reads=[f"pbk{bank}", f"B:mx{par}"], writes=[f"B:{bB}"],
                          tag=f"sha_{tt}_{dc}")
            P.dma("sync",
                  lambda e, i=b_ap, s=tt: e.dma_start(
                      d_mixh[s * 128:(s + 1) * 128, :], i),
                  reads=[f"B:{bB}"], writes=[f"mixh_{tt}"], tag=f"shw{Hh}_{tt}")

    free("h1", [f"h1_{h}_{t}" for h in range(8) for t in range(nic)])
    free("rw1a", ["B:rw1a", "rw1buf0"])
    free("rw1b", ["B:rw1b", "rw1buf1"])
    free("rw1c", ["B:rw1c", "rw1buf2"])
    free("rw2b", ["B:rw2b", "rw2buf"])
    free("flatT", [f"flatT_{dt}_{st}" for dt in range(ndt) for st in range(nst)])
    free("w_sb", [f"w_{tt}" for tt in range(nst)])
    free("m_sb", [f"m_{tt}" for tt in range(nst)])
    free("tri_sb", ["B:tri_sb"]); free("erow", ["B:erow"])
    free("stg", ["B:stg"]); free("sent", ["B:sent"])
    free("bufA2", ["B:bufA2"]); free("bufA3", ["B:bufA3"])
    free("bufB2", ["B:bufB2"]); free("bufB3", ["B:bufB3"])
    free("htile2", ["B:htile2"]); free("htile3", ["B:htile3"])

    alloc("tkw", NTILE * 2 * 4)
    alloc("tokf", NTILE * 4)
    alloc("toki", NTILE * 4)
    alloc("wcolb", NTILE * 4)
    tkw_ap = view("tkw", F32, cols=NTILE * 2)
    toki_ap = view("toki", I32, cols=NTILE)
    wcol_ap = view("wcolb", F32, cols=NTILE)
    P.dma("sync",
          lambda e, o=tkw_ap.rearrange("p (r c) -> p r c", c=2),
          i=d_tokw[0:NSLOT, :].rearrange("(r p) c -> p r c", p=128):
          e.dma_start(o, i),
          reads=["tokw"] + [f"tokw_e{ee}" for ee in range(NE)],
          writes=wreg("tkw"), tag="tokw_rd")
    tokf_ap = view("tokf", F32, cols=NTILE)
    P.add("vector",
          lambda e, o=tokf_ap.unsqueeze(2),
          i=tkw_ap.rearrange("p (r c) -> p r c", c=2)[:, :, 0:1]:
          e.tensor_copy(o, i), reads=["B:tkw"], writes=["B:tokf"])
    P.add("vector",
          lambda e, o=tokf_ap, s=float(S): e.tensor_scalar_min(o, o, s),
          reads=["B:tokf"], writes=["B:tokf"])
    P.add("vector",
          lambda e, o=toki_ap, i=tokf_ap: e.tensor_copy(o, i),
          reads=["B:tokf"], writes=["B:toki"])
    P.add("vector",
          lambda e, o=wcol_ap.unsqueeze(2),
          i=tkw_ap.rearrange("p (r c) -> p r c", c=2)[:, :, 1:2]:
          e.tensor_copy(o, i), reads=["B:tkw"], writes=["B:wcolb"])

    CMAX = CAP[0]
    alloc("gatT", 8 * CMAX * 2, [f"gatT_l{i}" for i in range(TK[0])])
    alloc("rw1f0", 8 * HF * 2, [f"rw1f0_d{d}" for d in range(8)])
    alloc("rw1f1", 8 * HF * 2, [f"rw1f1_d{d}" for d in range(8)])
    alloc("rw2f", nht * 1024 * 2, [f"rw2f_c{c}" for c in range(8)])
    free("bufB0", ["B:bufB0"]); free("bufB1", ["B:bufB1"])
    alloc("h1sg", 16 * 384 * 2, ["h1sg"])
    alloc("gati0", 1024 * 2, ["gati0"]); alloc("gati1", 1024 * 2, ["gati1"])
    alloc("gati2", 1024 * 2, ["gati2"])
    alloc("scl0", 2048, ["scl0"]); alloc("scl1", 2048, ["scl1"])
    gatT_ap = view("gatT", BF16)
    rw1f_aps = [view("rw1f0", BF16), view("rw1f1", BF16)]
    rw2f_ap = view("rw2f", BF16)
    h1sg_ap = view("h1sg", BF16)
    gati_aps = [view("gati0", BF16, cols=1024), view("gati1", BF16, cols=1024),
                view("gati2", BF16, cols=1024)]
    scl_aps = [view("scl0", BF16, cols=1024), view("scl1", BF16, cols=1024)]

    def seldma(kk, dst_ap, proto_ap, unit_stride, writes, tag, reads=()):
        def fn(e_, sem, _st=unit_stride, _p=proto_ap, _d=dst_ap, _kk=kk):
            with e_.register() as r:
                e_.load(r, sig_i[0:1, _kk:_kk + 1])
                e_.reg_add(r, r, 1)
                e_.reg_mul(r, r, _st)
                sv = e_.compute_val(r, donate=True)
                sap = bass.AP(_p.tensor, sv, list(_p.ap))
                e_.dma_start(_d, sap).then_inc(sem, 16)
        P.dma_selfinc("gpsimd", fn, reads=["sig"] + list(reads),
                      writes=writes, tag=tag)

    _w1p = d_w1pre8[0].rearrange("(d p) f -> p d f", p=128)
    _w2p = d_w2pre8[0].rearrange("(t p) f -> p t f", p=128)
    _b1p = d_b1.bitcast(U8)[0]
    _b2p = d_b2.bitcast(U8)[0:1, :]
    pre1r = [f"w1pre_{u}_{c}" for u in range(1, nunits) for c in range(8)]
    pre2r = [f"w2pre_{u}_{c}" for u in range(1, nunits) for c in range(8)]

    def seldma_off(kk, dst_ap, proto_tensor, proto_pat, unit_stride, off,
                   writes, tag, reads=()):
        def fn(e_, sem, _st=unit_stride, _o=off, _d=dst_ap, _kk=kk,
               _pt=proto_tensor, _pp=proto_pat):
            with e_.register() as r:
                e_.load(r, sig_i[0:1, _kk:_kk + 1])
                e_.reg_add(r, r, 1)
                e_.reg_mul(r, r, _st)
                if _o:
                    e_.reg_add(r, r, _o)
                sv = e_.compute_val(r, donate=True)
                sap = bass.AP(_pt, sv, _pp)
                e_.dma_start(_d, sap).then_inc(sem, 16)
        P.dma_selfinc("gpsimd", fn, reads=["sig"] + list(reads),
                      writes=writes, tag=tag)

    def exp_rw1_chunk(kk, dt):
        dst = rw1f_aps[kk % 2][:, dt * HF:(dt + 1) * HF].bitcast(U8)
        w = wreg(f"rw1f{kk % 2}", [f"rw1f{kk % 2}_d{dt}"]) if dt == 0 else             [f"rw1f{kk % 2}_d{dt}"]
        seldma_off(kk, dst, _w1p.tensor, [[2 * HF, 128], [1, 2 * HF]],
                   2 * D * HF, dt * 128 * 2 * HF, writes=w,
                   tag=f"xw1_{kk}_{dt}", reads=pre1r)

    def exp_rw2_chunk(kk, c):
        dst = rw2f_ap[:, c * 2048:(c + 1) * 2048].bitcast(U8).rearrange(
            "p (t f) -> p t f", f=2 * D)
        w = wreg("rw2f", [f"rw2f_c{c}"]) if c == 0 else [f"rw2f_c{c}"]
        seldma_off(kk, dst, _w2p.tensor,
                   [[2 * D, 128], [128 * 2 * D, 2], [1, 2 * D]],
                   2 * HF * D, c * 256 * 2 * D, writes=w,
                   tag=f"xw2_{kk}_{c}", reads=pre2r)

    for dt in range(ndt):
        exp_rw1_chunk(0, dt)
    flatb_regs = [f"flatb_{st}" for st in range(nst)]
    for kk in range(NE):
        Ck, Tk, base = CAP[kk], TK[kk], UOFF[kk] // 128
        rw2q = [lambda c=c: exp_rw2_chunk(kk, c) for c in range(8)]
        fetchq = []
        seldma(kk, b1_aps[kk % 2].bitcast(U8), _b1p, 128 * nht * 4,
               writes=wreg(f"b1t{kk % 2}", [f"b1buf{kk % 2}"]), tag=f"xb1_{kk}")
        seldma(kk, b2_aps[kk % 2].bitcast(U8), _b2p, D * 4,
               writes=wreg(f"b2r{kk % 2}", [f"b2buf{kk % 2}"]), tag=f"xb2_{kk}")
        if kk + 1 < NE:
            fetchq += [lambda dt=dt, nk=kk + 1: exp_rw1_chunk(nk, dt)
                       for dt in range(ndt)]
        groups = [list(range(i, min(i + 3, Tk))) for i in range(0, Tk, 3)]
        prate = -(-len(fetchq) // max(len(groups) - 1, 1))
        first_group = True
        for sg in groups:
            for i in sg:
                g = base + i
                gb = gati_aps[g % 3]
                P.dma("gpsimd",
                      lambda e, o=gb, ix=toki_ap[:, g:g + 1]:
                      e.indirect_dma_start(o, None, d_flatb,
                                           IndirectOffsetOnAxis(ix, 0)),
                      reads=["B:toki"] + flatb_regs, writes=[f"gati{g % 3}"],
                      tag=f"gth{kk}_{i}")
                for dt in range(ndt):
                    bank = dt % 2
                    transpose(pbf16(bank), gb[:, dt * 128:(dt + 1) * 128],
                              id_bf_ap,
                              reads=[f"gati{g % 3}", "B:id_bf"],
                              writes=[f"pbk{bank}"])
                    copy_on("scalar",
                            gatT_ap[:, dt * CMAX + i * 128: dt * CMAX + i * 128 + 128],
                            pbf16(bank), reads=[f"pbk{bank}"],
                            writes=[f"gatT_l{i}"])
            if first_group:
                for f in rw2q:
                    f()
                first_group = False
            else:
                for _ in range(prate):
                    if fetchq:
                        fetchq.pop(0)()
            W = 128 * len(sg)
            i0 = sg[0]
            for ht in range(nht):
                bank = 2 + ht % 2
                for dt in range(ndt):
                    mm(pb[bank][:, 0:W],
                       rw1f_aps[kk % 2][:, dt * HF + ht * 128: dt * HF + ht * 128 + 128],
                       gatT_ap[:, dt * CMAX + i0 * 128: dt * CMAX + i0 * 128 + W],
                       start=(dt == 0), stop=(dt == ndt - 1),
                       reads=[f"rw1f{kk % 2}_d{dt}"] + [f"gatT_l{i}" for i in sg],
                       writes=[f"pbk{bank}"], tag=f"xh1_{kk}_{i0}_{ht}")
                P.add("scalar",
                      lambda e, o=h1sg_ap[:, ht * 384: ht * 384 + W],
                      i=pb[bank][:, 0:W],
                      b=b1_aps[kk % 2][:, ht:ht + 1]: e.activation(
                          o, i, AF.Gelu, bias=b),
                      reads=[f"pbk{bank}", f"b1buf{kk % 2}"],
                      writes=["h1sg"], tag=f"xgl_{kk}_{i0}_{ht}")
            for tl, i in enumerate(sg):
                g = base + i
                sclb = scl_aps[g % 2]
                for dc in range(ndc):
                    bank = 4 + (2 * tl + dc) % 4
                    for ht in range(nht):
                        mm(pb[bank],
                           h1sg_ap[:, ht * 384 + tl * 128: ht * 384 + tl * 128 + 128],
                           rw2f_ap[:, ht * 1024 + dc * 512: ht * 1024 + (dc + 1) * 512],
                           start=(ht == 0), stop=False,
                           reads=["h1sg", f"rw2f_c{ht // 2}"],
                           writes=[f"pbk{bank}"],
                           tag=f"xo2_{kk}_{i}_{dc}")
                    mm(pb[bank], onesf[0:1, 0:128],
                       b2_aps[kk % 2][:, dc * 512:(dc + 1) * 512],
                       False, True, reads=[f"b2buf{kk % 2}", "B:ones_f"],
                       writes=[f"pbk{bank}"], tag=f"xo2b_{kk}_{i}_{dc}")
                    P.add("vector",
                          lambda e, o=sclb[:, dc * 512:(dc + 1) * 512], bk=bank,
                          w=wcol_ap[:, g:g + 1]: e.tensor_scalar(
                              o, pb[bk], w, None, op0=ALU.mult),
                          reads=[f"pbk{bank}", "B:wcolb"],
                          writes=[f"scl{g % 2}"], tag=f"xsc_{kk}_{i}_{dc}")
                P.dma("sync",
                      lambda e, i_=sclb, gg=g: e.dma_start(
                          d_expout[gg * 128:(gg + 1) * 128, :], i_),
                      reads=[f"scl{g % 2}"], writes=[f"expout_{g}"],
                      tag=f"xw_{kk}_{i}")
        while fetchq:
            fetchq.pop(0)()

    free("gatT", [f"gatT_l{i}" for i in range(TK[0])])
    free("rw1f0", [f"rw1f0_d{dt}" for dt in range(8)])
    free("rw1f1", [f"rw1f1_d{dt}" for dt in range(8)])
    free("rw2f", [f"rw2f_c{c}" for c in range(8)])
    free("h1sg", ["h1sg"])
    alloc("bufB0", 4096); alloc("bufB1", 4096)
    free("gati0", ["gati0"]); free("gati1", ["gati1"]); free("gati2", ["gati2"])
    free("b1t0", ["B:b1t0", "b1buf0"]); free("b1t1", ["B:b1t1", "b1buf1"])
    free("b2r0", ["B:b2r0", "b2buf0"]); free("b2r1", ["B:b2r1", "b2buf1"])

    alloc("blk0", 1024 * 2); alloc("blk1", 1024 * 2)
    alloc("blk2", 1024 * 2); alloc("blk3", 1024 * 2)
    alloc("blk4", 1024 * 2); alloc("blk5", 1024 * 2)
    alloc("ptb0", 128 * 2); alloc("ptb1", 128 * 2)
    alloc("pbufb0", 128 * 2); alloc("pbufb1", 128 * 2)
    blk_aps = [view(f"blk{j}", BF16, cols=1024) for j in range(6)]
    ptb_aps = [view("ptb0", BF16, cols=128), view("ptb1", BF16, cols=128)]
    pbufb_aps = [view("pbufb0", BF16, cols=128), view("pbufb1", BF16, cols=128)]
    exr = [f"expout_{g}" for g in range(NTILE)]

    def p6_expand(tt):
        par = tt % 2
        a, bB = f"bufA{par}", f"bufB{par}"
        a_ap, b_ap = view(a), view(bB)
        mx_ap = view(f"mx{par}")
        P.dma("sync",
              lambda e, o=a_ap, s=tt: e.dma_start(o, d_xsp[s * 128:(s + 1) * 128, :]),
              reads=[f"xsp_{tt}"], writes=wreg(a), tag=f"x2re{tt}")
        P.dma("sync",
              lambda e, o=mx_ap, s=tt: e.dma_start(o, d_mixh[s * 128:(s + 1) * 128, :]),
              reads=[f"mixh_{tt}"], writes=wreg(f"mx{par}"), tag=f"mxre{tt}")
        for ee in range(NE):
            bj = (tt * NE + ee) % 6
            pp = ee % 2
            pbufb_ap = pbufb_aps[pp]
            ptb_ap = ptb_aps[pp]
            def blfn(e_, sem, _tt=tt, _ee=ee, _dst=blk_aps[bj].bitcast(U8)):
                with e_.register() as r:
                    e_.load(r, bas3_i[0:1, _tt * NE + _ee:_tt * NE + _ee + 1])
                    sv = e_.compute_val(r, donate=True)
                    sap = bass.AP(d_expout8.tensor, sv, [[2 * D, 128], [1, 2 * D]])
                    e_.dma_start(_dst, sap).then_inc(sem, 16)
            P.dma_selfinc("gpsimd", blfn,
                          reads=[f"bas_{tt}"] + exr, writes=[f"blk{bj}"],
                          tag=f"bl{tt}_{ee}")
            P.add("vector",
                  lambda e, o=pbufb_ap, m=psf_ap[:, tt * NE + ee:tt * NE + ee + 1]:
                  e.tensor_scalar(o, irep_ap, m, None, op0=ALU.is_equal),
                  reads=[f"psf_{tt}", "B:irep"], writes=[f"B:pbufb{pp}"],
                  tag=f"pb6_{tt}_{ee}")
            bank6 = 2 + ee % 2
            transpose(pbf16(bank6), pbufb_ap, id_bf_ap,
                      reads=[f"B:pbufb{pp}", "B:id_bf"], writes=[f"pbk{bank6}"],
                      tag=f"pt6_{tt}_{ee}")
            copy_on("scalar", ptb_ap, pbf16(bank6), reads=[f"pbk{bank6}"],
                    writes=[f"B:ptb{pp}"], tag=f"pe6_{tt}_{ee}")
            for dc in range(ndc):
                bank = 4 + (2 * tt + dc) % 4
                mm(pb[bank], ptb_ap, blk_aps[bj][:, dc * 512:(dc + 1) * 512],
                   start=(ee == 0), stop=(ee == NE - 1),
                   reads=[f"B:ptb{pp}", f"blk{bj}"], writes=[f"pbk{bank}"],
                   tag=f"xp6_{tt}_{ee}_{dc}")

    def p6_finish(tt):
        par = tt % 2
        a, bB = f"bufA{par}", f"bufB{par}"
        a_ap, b_ap = view(a), view(bB)
        mx_ap = view(f"mx{par}")
        for dc in range(ndc):
            bank = 4 + (2 * tt + dc) % 4
            P.add("vector",
                  lambda e, o=b_ap[:, dc * 512:(dc + 1) * 512], bk=bank,
                  m=mx_ap[:, dc * 512:(dc + 1) * 512]: e.tensor_add(o, pb[bk], m),
                  reads=[f"pbk{bank}", f"B:mx{par}"], writes=[f"B:{bB}"],
                  tag=f"mxs6_{tt}_{dc}")
        cm, cr, mr = ln_stats(b_ap, [f"B:{bB}"], f"p5{par}_", "sq0")
        P.add("vector",
              lambda e, o=b_ap, m=cm, r=cr: e.tensor_scalar(
                  o, o, m, r, op0=ALU.subtract, op1=ALU.mult),
              reads=[f"B:{bB}"] + mr, writes=[f"B:{bB}"])
        P.add("vector", lambda e, o=b_ap, i=a_ap: e.tensor_add(o, o, i),
              reads=[f"B:{bB}", f"B:{a}"], writes=[f"B:{bB}"])
        P.dma("sync",
              lambda e, i=b_ap, s=tt: e.dma_start(d_y[s * 128:(s + 1) * 128, :], i),
              reads=[f"B:{bB}"], writes=[f"y_{tt}"], tag=f"yout{tt}")

    for tt in range(nst):
        p6_expand(tt)
        if tt > 0:
            p6_finish(tt - 1)
    p6_finish(nst - 1)

    with nc.Block() as blk:
        P.emit(blk, sems)
    es.close()
    if ret_prog:
        return nc, P
    return nc


_CACHE = {}


def _rope_tables(S, half=32, theta=10000.0, scale=1.0):
    inv = 1.0 / theta ** (np.arange(half, dtype=np.float64) / half)
    fr = np.arange(S, dtype=np.float64)[:, None] * inv[None, :]
    return (np.cos(fr) * scale).astype(np.float32), \
           (np.sin(fr) * scale).astype(np.float32)


def kernel(L0_attn_x, L0_attn_ln_g, L0_attn_ln_b, L0_attn_w_qkv, L0_attn_w_out,
           L0_attn_b_out, L0_ffn_router_w, L0_ffn_router_b, L0_ffn_sw1,
           L0_ffn_sb1, L0_ffn_sw2, L0_ffn_sb2, L0_ffn_rw1, L0_ffn_rb1,
           L0_ffn_rw2, L0_ffn_rb2):
    B, S, D = L0_attn_x.shape
    NE = L0_ffn_rw1.shape[0]
    HF = L0_ffn_sw1.shape[1]
    nunits = NE + 1
    nht = HF // 128

    if "nc" not in _CACHE:
        _CACHE["nc"] = build_program(S=S, D=D, NH=16, HD=64, HF=HF, NE=NE)
    nc = _CACHE["nc"]

    f32 = np.float32
    asc = np.ascontiguousarray
    cosn, sinn = _rope_tables(S, scale=1.0)
    cosq, sinq = _rope_tables(S, scale=0.125)

    wqkvT = asc(np.asarray(L0_attn_w_qkv, f32).T)
    woutT = asc(np.asarray(L0_attn_w_out, f32).T)
    lng_rep = asc(np.broadcast_to(np.asarray(L0_attn_ln_g, f32), (128, D)))
    lnb_rep = asc(np.broadcast_to(np.asarray(L0_attn_ln_b, f32), (128, D)))
    b_out = asc(np.asarray(L0_attn_b_out, f32).reshape(1, D))
    rwq_pad = np.zeros((D, 128), f32)
    rwq_pad[:, :NE] = np.asarray(L0_ffn_router_w, f32)
    rbq_rep = asc(np.broadcast_to(np.asarray(L0_ffn_router_b, f32), (128, NE)))
    ident = np.eye(128, dtype=f32)
    w1_all = asc(np.concatenate(
        [np.asarray(L0_ffn_sw1, f32)[None], np.asarray(L0_ffn_rw1, f32)], axis=0))
    w2_all = asc(np.concatenate(
        [np.asarray(L0_ffn_sw2, f32)[None], np.asarray(L0_ffn_rw2, f32)], axis=0))
    b1_all = asc(np.concatenate(
        [np.asarray(L0_ffn_sb1, f32)[None], np.asarray(L0_ffn_rb1, f32)],
        axis=0).reshape(nunits, nht, 128).transpose(0, 2, 1))
    b2_all = asc(np.concatenate(
        [np.asarray(L0_ffn_sb2, f32)[None], np.asarray(L0_ffn_rb2, f32)], axis=0))

    nst = S // 128
    tri_u = np.triu(np.ones((128, 128), f32), 1)
    iota_tok = (np.arange(128, dtype=f32)[:, None]
                + 128.0 * np.arange(nst, dtype=f32)[None, :])
    iota_tok = asc(iota_tok.astype(f32))
    eidx = np.arange(NE, dtype=f32).reshape(1, NE)
    etag = (np.arange(NE, dtype=f32) / 16.0).reshape(1, NE)
    iota_rep = asc(np.broadcast_to(np.arange(128, dtype=f32)[None, :], (128, 128)))
    shared = {
        "wqkvT": wqkvT, "woutT": woutT, "lng_rep": lng_rep, "lnb_rep": lnb_rep,
        "b_out": b_out, "cosq": cosq, "sinq": sinq, "cosk": cosn, "sink": sinn,
        "ident": ident, "rwq_pad": rwq_pad, "rbq_rep": rbq_rep,
        "w1_all": w1_all, "b1_all": b1_all, "w2_all": w2_all, "b2_all": b2_all,
        "tri_u": tri_u, "iota_tok": iota_tok, "eidx": eidx, "etag": etag,
        "iota_rep": iota_rep,
    }
    x = np.asarray(L0_attn_x, f32)
    in_maps = []
    for c in range(B):
        m = dict(shared)
        m["x"] = asc(x[c])
        in_maps.append(m)

    res = run_bass_kernel_spmd(nc, in_maps, core_ids=list(range(B)))
    out = np.stack([np.asarray(res.results[c]["y"]) for c in range(B)], axis=0)
    return out


# version 37
# speedup vs baseline: 1.2103x; 1.0031x over previous
import contextlib
import numpy as np
import ml_dtypes
import concourse.bass as bass
import concourse.mybir as mybir
from concourse.bass_utils import run_bass_kernel_spmd
from concourse.bass_core_rust import IndirectOffsetOnAxis

F32 = mybir.dt.float32
BF16 = mybir.dt.bfloat16
U8 = mybir.dt.uint8
I32 = mybir.dt.int32
AF = mybir.ActivationFunctionType
ALU = mybir.AluOpType
AX = mybir.AxisListType
bfloat16 = ml_dtypes.bfloat16

ENGINES = ("tensor", "vector", "scalar", "gpsimd", "sync")
DMA_QUEUES = ("gpsimd", "sync", "scalar")
NSEM_PER_QUEUE = 8


class _Op:
    __slots__ = ("idx", "engine", "fn", "reads", "writes", "queue", "tag",
                 "deps", "milestone", "mnum", "waits", "qsem", "self_inc")

    def __init__(self, idx, engine, fn, reads, writes, queue, tag):
        self.idx = idx
        self.engine = engine
        self.fn = fn
        self.reads = tuple(reads)
        self.writes = tuple(writes)
        self.queue = queue
        self.tag = tag
        self.deps = None
        self.milestone = False
        self.mnum = None
        self.waits = None
        self.qsem = None
        self.self_inc = False


class Prog:
    def __init__(self):
        self.ops = []

    def add(self, engine, fn, reads=(), writes=(), queue=None, tag=""):
        op = _Op(len(self.ops), engine, fn, reads, writes, queue, tag)
        self.ops.append(op)
        return op

    def dma(self, queue, fn, reads=(), writes=(), tag=""):
        return self.add(queue, fn, reads, writes, queue=queue, tag=tag)

    def dma_selfinc(self, queue, fn, reads=(), writes=(), tag=""):
        op = self.add(queue, fn, reads, writes, queue=queue, tag=tag)
        op.self_inc = True
        return op

    def _analyze(self):
        regions = {}
        for op in self.ops:
            deps = set()
            for r in op.reads:
                st = regions.get(r)
                if st and st[0] is not None:
                    deps.add(st[0])
            for w in op.writes:
                st = regions.get(w)
                if st:
                    if st[0] is not None:
                        deps.add(st[0])
                    deps.update(st[1])
            deps.discard(op)
            real = []
            for d in deps:
                if (d.queue is None and op.queue is None
                        and d.engine == "tensor" and op.engine == "tensor"):
                    continue
                real.append(d)
                d.milestone = True
            op.deps = real
            for w in op.writes:
                regions[w] = [op, []]
            for r in op.reads:
                st = regions.setdefault(r, [None, []])
                st[1].append(op)

    def _domain(self, op):
        if op.queue is not None:
            return ("d", op.queue, op.qsem)
        return ("e", op.engine)

    def _schedule(self):
        qcount = {q: 0 for q in DMA_QUEUES}
        for op in self.ops:
            if op.queue is not None:
                op.qsem = qcount[op.queue] % NSEM_PER_QUEUE
                qcount[op.queue] += 1
                op.milestone = True
        mcount = {}
        for op in self.ops:
            if op.milestone:
                dom = self._domain(op)
                mcount[dom] = mcount.get(dom, 0) + 1
                op.mnum = mcount[dom]
        know = {e: {} for e in ENGINES}
        snap = {}
        for op in self.ops:
            k = know[op.engine]
            need = {}
            for d in op.deps:
                dom = self._domain(d)
                if need.get(dom, (0, -1))[0] < d.mnum:
                    need[dom] = (d.mnum, d.idx)
            waits = []
            if op.queue is not None and op.mnum > 1:
                dom = self._domain(op)
                if k.get(dom, 0) < op.mnum - 1:
                    waits.append((dom, op.mnum - 1))
                    k[dom] = op.mnum - 1
            for dom, (num, didx) in sorted(need.items(), key=lambda x: x[1][1]):
                if k.get(dom, 0) >= num:
                    continue
                waits.append((dom, num))
                s = snap.get(didx)
                if s:
                    for d2, n2 in s.items():
                        if k.get(d2, 0) < n2:
                            k[d2] = n2
                if k.get(dom, 0) < num:
                    k[dom] = num
            op.waits = waits
            if op.milestone:
                snap[op.idx] = dict(k)
        self._final_counts = mcount

    def emit(self, blk, sems):
        self._analyze()
        self._schedule()
        per_engine = {e: [] for e in ENGINES}
        for op in self.ops:
            per_engine[op.engine].append(op)

        def run_stream(engine_name):
            def body(eng):
                for op in per_engine[engine_name]:
                    for dom, num in op.waits:
                        val = num * 16 if dom[0] == "d" else num
                        eng.wait_ge(sems[dom], val)
                    if op.self_inc:
                        op.fn(eng, sems[self._domain(op)])
                    else:
                        inst = op.fn(eng)
                        if op.milestone:
                            dom = self._domain(op)
                            inst.then_inc(sems[dom], 16 if dom[0] == "d" else 1)
                for dom, total in self._final_counts.items():
                    if dom[0] == "d" and dom[1] == engine_name:
                        eng.wait_ge(sems[dom], total * 16)
            return body

        blk.tensor(run_stream("tensor"))
        blk.vector(run_stream("vector"))
        blk.scalar(run_stream("scalar"))
        blk.gpsimd(run_stream("gpsimd"))
        blk.sync(run_stream("sync"))

    def all_domains(self):
        doms = [("e", e) for e in ENGINES]
        for q in DMA_QUEUES:
            for i in range(NSEM_PER_QUEUE):
                doms.append(("d", q, i))
        return doms


class Buf:
    __slots__ = ("name", "off", "nbytes", "ghosts")

    def __init__(self, name, off, nbytes, ghosts):
        self.name = name
        self.off = off
        self.nbytes = nbytes
        self.ghosts = ghosts


class Arena:
    def __init__(self, nbytes):
        self.cap = nbytes
        self.free_list = [(0, nbytes, [])]
        self.live = {}

    def alloc(self, name, nbytes, align=128):
        nbytes = (nbytes + align - 1) // align * align
        for i, (off, size, ghosts) in enumerate(self.free_list):
            aoff = (off + align - 1) // align * align
            pad = aoff - off
            if size - pad >= nbytes:
                newfree = []
                if pad:
                    newfree.append((off, pad, list(ghosts)))
                rem = size - pad - nbytes
                if rem:
                    newfree.append((aoff + nbytes, rem, list(ghosts)))
                self.free_list[i:i + 1] = newfree
                b = Buf(name, aoff, nbytes, list(ghosts))
                self.live[name] = b
                return b
        raise MemoryError(f"arena OOM for {name} ({nbytes}B); free={[(o, s) for o, s, _ in self.free_list]}; live={sorted((b.off, b.nbytes, n) for n, b in self.live.items())}")

    def free(self, name, ghost_regions):
        b = self.live.pop(name)
        self.free_list.append((b.off, b.nbytes, list(ghost_regions)))
        self.free_list.sort(key=lambda t: t[0])
        merged = []
        for off, size, g in self.free_list:
            if merged and merged[-1][0] + merged[-1][1] == off:
                merged[-1] = (merged[-1][0], merged[-1][1] + size,
                              merged[-1][2] + g)
            else:
                merged.append((off, size, list(g)))
        self.free_list = merged


def build_program(S=2048, D=1024, NH=16, HD=64, HF=2048, NE=8, dbg=False,
                  ret_prog=False):
    nst = S // 128
    ndt = D // 128
    nic = S // 512
    ndc = D // 512
    nht = HF // 128
    njt = nst
    nunits = NE + 1
    EPS = 1e-5

    nc = bass.Bass("TRN2", target_bir_lowering=False, debug=False)

    d_x = nc.dram_tensor("x", [S, D], F32, kind="ExternalInput").ap()
    d_wqkvT = nc.dram_tensor("wqkvT", [D, 3 * D], F32, kind="ExternalInput").ap()
    d_woutT = nc.dram_tensor("woutT", [D, D], F32, kind="ExternalInput").ap()
    d_lng = nc.dram_tensor("lng_rep", [128, D], F32, kind="ExternalInput").ap()
    d_lnb = nc.dram_tensor("lnb_rep", [128, D], F32, kind="ExternalInput").ap()
    d_bout = nc.dram_tensor("b_out", [1, D], F32, kind="ExternalInput").ap()
    d_cosq = nc.dram_tensor("cosq", [S, 32], F32, kind="ExternalInput").ap()
    d_sinq = nc.dram_tensor("sinq", [S, 32], F32, kind="ExternalInput").ap()
    d_cosk = nc.dram_tensor("cosk", [S, 32], F32, kind="ExternalInput").ap()
    d_sink = nc.dram_tensor("sink", [S, 32], F32, kind="ExternalInput").ap()
    d_ident = nc.dram_tensor("ident", [128, 128], F32, kind="ExternalInput").ap()
    d_rwq = nc.dram_tensor("rwq_pad", [D, 128], F32, kind="ExternalInput").ap()
    d_rbq = nc.dram_tensor("rbq_rep", [128, NE], F32, kind="ExternalInput").ap()
    d_w1 = nc.dram_tensor("w1_all", [nunits, D, HF], F32, kind="ExternalInput").ap()
    d_b1 = nc.dram_tensor("b1_all", [nunits, 128, nht], F32, kind="ExternalInput").ap()
    d_w2 = nc.dram_tensor("w2_all", [nunits, HF, D], F32, kind="ExternalInput").ap()
    d_b2 = nc.dram_tensor("b2_all", [nunits, D], F32, kind="ExternalInput").ap()
    d_y = nc.dram_tensor("y", [S, D], F32, kind="ExternalOutput").ap()
    d_xsp = nc.dram_tensor("xsp", [S, D], F32, kind="Internal").ap()
    d_tri = nc.dram_tensor("tri_u", [128, 128], F32, kind="ExternalInput").ap()
    d_itok = nc.dram_tensor("iota_tok", [128, S // 128], F32, kind="ExternalInput").ap()
    d_eidx = nc.dram_tensor("eidx", [1, NE], F32, kind="ExternalInput").ap()
    d_etag = nc.dram_tensor("etag", [1, NE], F32, kind="ExternalInput").ap()
    d_irep = nc.dram_tensor("iota_rep", [128, 128], F32, kind="ExternalInput").ap()
    CAP = [-(-min(S, (2 * S) // (k + 1)) // 128) * 128 + 128 for k in range(NE)]
    NSLOT = sum(CAP)
    NTILE = NSLOT // 128
    UOFF = [sum(CAP[:k]) for k in range(NE)]
    TK = [c // 128 for c in CAP]
    d_flatb = nc.dram_tensor("flatb", [S + 128, D], BF16, kind="Internal").ap()
    d_tokw = nc.dram_tensor("tokw", [NSLOT + 128, 2], F32, kind="Internal").ap()
    d_expout = nc.dram_tensor("expout", [NSLOT, D], BF16, kind="Internal").ap()
    d_tokw8 = d_tokw.bitcast(U8)
    d_expout8 = d_expout.bitcast(U8)
    d_w1pre = nc.dram_tensor("w1pre", [nunits, D, HF], BF16, kind="Internal").ap()
    d_w2pre = nc.dram_tensor("w2pre", [nunits, HF, D], BF16, kind="Internal").ap()
    d_w1pre8 = d_w1pre.bitcast(U8)
    d_w2pre8 = d_w2pre.bitcast(U8)
    d_woutpre = nc.dram_tensor("woutpre", [D, D], BF16, kind="Internal").ap()
    d_mixh = nc.dram_tensor("mixh", [S, D], F32, kind="Internal").ap()
    nst_ = S // 128
    ndt_ = D // 128
    if dbg:
        d_dbg = {
            "dbg_hT": nc.dram_tensor("dbg_hT", [128, ndt_ * S], BF16, kind="Internal").ap(),
            "dbg_qn": nc.dram_tensor("dbg_qn", [128, nst_ * 1024], BF16, kind="Internal").ap(),
            "dbg_kn": nc.dram_tensor("dbg_kn", [128, nst_ * 1024], BF16, kind="Internal").ap(),
            "dbg_vn": nc.dram_tensor("dbg_vn", [128, nst_ * 1040], BF16, kind="Internal").ap(),
            "dbg_negM": nc.dram_tensor("dbg_negM", [128, 1], F32, kind="Internal").ap(),
            "dbg_slab": nc.dram_tensor("dbg_slab", [128, nst_ * 512], BF16, kind="Internal").ap(),
            "dbg_oT": nc.dram_tensor("dbg_oT", [128, ndt_ * S], BF16, kind="Internal").ap(),
            "dbg_flatT": nc.dram_tensor("dbg_flatT", [128, ndt_ * S], BF16, kind="Internal").ap(),
            "dbg_w": nc.dram_tensor("dbg_w", [128, nst_ * NE], F32, kind="Internal").ap(),
            "dbg_mixed": nc.dram_tensor("dbg_mixed", [128, nst_ * D], F32, kind="Internal").ap(),
        }

    def dump(key, src_ap, reads):
        if dbg:
            P.dma("scalar", lambda e, o=d_dbg[key], s=src_ap: e.dma_start(o, s),
                  reads=reads, tag=key)

    P = Prog()
    ARENA_BYTES = 206 * 1024
    es = contextlib.ExitStack()
    arena_t = es.enter_context(nc.sbuf_tensor("arena", [128, ARENA_BYTES], U8))
    ar = arena_t.ap()
    arena = Arena(ARENA_BYTES)
    pbanks = [es.enter_context(nc.psum_tensor(f"pbk{i}", [128, 512], F32))
              for i in range(8)]
    pb = [t.ap() for t in pbanks]

    def pbf16(i, n=128):
        return pb[i].bitcast(BF16)[:, 0:n]

    sems = {}
    for dom in P.all_domains():
        sems[dom] = es.enter_context(
            nc.semaphore(name="s_" + "_".join(map(str, dom))))

    bufs = {}

    def alloc(name, nbytes, regs=()):
        b = arena.alloc(name, nbytes)
        bufs[name] = b
        if b.ghosts:
            g = list(b.ghosts)
            b.ghosts = []
            P.add("vector", lambda e: e.engine_nop(),
                  writes=g + [f"B:{name}"] + list(regs), tag=f"fence_{name}")
        return b

    def free(name, regs):
        bufs.pop(name)
        arena.free(name, regs)

    def view(name, dt=F32, cols=None):
        b = bufs[name]
        esz = 2 if dt == BF16 else 4
        n = b.nbytes // esz if cols is None else cols
        return ar[:, b.off:b.off + n * esz].bitcast(dt)

    def wreg(name, extra=()):
        b = bufs[name]
        g = b.ghosts
        b.ghosts = []
        return [f"B:{name}"] + list(extra) + g

    def mm(psum_ap, lhsT, rhs, start, stop, reads, writes, tp=None, tag=""):
        P.add("tensor",
              lambda e, o=psum_ap, a=lhsT, b=rhs, s1=start, s2=stop, t=tp:
              e.matmul(o, a, b, start=s1, stop=s2, tile_position=t),
              reads=reads, writes=writes, tag=tag)

    def transpose(psum_ap, in_ap, id_ap, reads, writes, tag=""):
        P.add("tensor",
              lambda e, o=psum_ap, i=in_ap, d=id_ap: e.transpose(o, i, d),
              reads=reads, writes=writes, tag=tag)

    def copy_on(engine, dst, src, reads, writes, tag=""):
        if engine == "scalar":
            P.add("scalar", lambda e, d=dst, s=src: e.copy(d, s),
                  reads=reads, writes=writes, tag=tag)
        else:
            P.add(engine, lambda e, d=dst, s=src: e.tensor_copy(d, s),
                  reads=reads, writes=writes, tag=tag)

    alloc("ones_f", 128 * 4)
    alloc("ones_bf", 128)
    alloc("id_bf", 128 * 2)
    alloc("id_f", 128 * 4)
    alloc("negM", 128)
    alloc("cols", 1024)
    _COLS = {}

    def col(name):
        if name not in _COLS:
            _COLS[name] = len(_COLS)
            assert len(_COLS) <= 256
        return view("cols")[:, _COLS[name]:_COLS[name] + 1]

    onesf_ap = None

    P.add("vector", lambda e, d=None: e.memset(view("ones_f"), 1.0),
          writes=wreg("ones_f"))
    P.add("vector", lambda e: e.memset(view("ones_bf", BF16, cols=1), 1.0),
          writes=wreg("ones_bf"))
    id_bf_ap = view("id_bf", BF16, cols=128)
    id_f_ap = view("id_f", cols=128)
    P.dma("gpsimd", lambda e, o=id_bf_ap: e.dma_start(o, d_ident),
          writes=wreg("id_bf"))
    P.dma("sync", lambda e, o=id_f_ap: e.dma_start(o, d_ident),
          writes=wreg("id_f"))
    onesf = view("ones_f")
    onesbf = view("ones_bf", BF16, cols=1)
    negM_ap = view("negM")[:, 0:1]

    alloc("bufA0", 4096); alloc("bufA1", 4096)
    alloc("bufA2", 4096); alloc("bufA3", 4096)
    alloc("bufB0", 4096); alloc("bufB1", 4096)
    alloc("bufB2", 4096); alloc("bufB3", 4096)
    alloc("sq0", 4096)
    alloc("htile0", 2048); alloc("htile1", 2048)
    alloc("htile2", 2048); alloc("htile3", 2048)
    alloc("hT", ndt * S * 2, [f"hT_{dt}_{st}" for dt in range(ndt) for st in range(nst)])
    alloc("q_nat", nst * 1024 * 2, [f"q_nat_{st}" for st in range(nst)])
    alloc("k_nat", nst * 1024 * 2, [f"k_nat_{st}" for st in range(nst)])
    alloc("rwq", ndt * 128 * 2)
    alloc("v_nat", nst * 1040 * 2, [f"v_nat_{st}" for st in range(nst)])
    alloc("lng", 4096); alloc("lnb", 4096)

    lng_ap = view("lng"); lnb_ap = view("lnb")
    P.dma("sync", lambda e, o=lng_ap: e.dma_start(o, d_lng), writes=wreg("lng"))
    P.dma("sync", lambda e, o=lnb_ap: e.dma_start(o, d_lnb), writes=wreg("lnb"))

    alloc("ctq", nst * 32 * 4); alloc("stq", nst * 32 * 4)
    alloc("ctk", nst * 32 * 4); alloc("stk", nst * 32 * 4)
    for nm, src in (("ctq", d_cosq), ("stq", d_sinq), ("ctk", d_cosk),
                    ("stk", d_sink)):
        dst = view(nm).rearrange("p (t f) -> p t f", f=32)
        P.dma("sync",
              lambda e, o=dst, s=src: e.dma_start(
                  o, s.rearrange("(t p) f -> p t f", p=128)),
              writes=wreg(nm))

    def wq_dma(fc):
        nm = f"wq{fc % 2}"
        dst = view(nm, BF16).rearrange("p (d f) -> p d f", f=512)
        src = d_wqkvT[:, fc * 512:(fc + 1) * 512].rearrange("(d p) f -> p d f", p=128)
        P.dma("gpsimd", lambda e, o=dst, s=src: e.dma_start(o, s),
              writes=wreg(nm, [f"wqbuf{fc % 2}"]), tag=f"wqdma{fc}")

    rwq_dst = view("rwq", BF16).rearrange("p (d f) -> p d f", f=128)
    P.dma("gpsimd",
          lambda e, o=rwq_dst: e.dma_start(
              o, d_rwq.rearrange("(d p) f -> p d f", p=128)),
          writes=wreg("rwq"))

    def ln_stats(src_ap, src_regions, pfx, sqn):
        c_s1, c_s2 = col(pfx + "s1"), col(pfx + "s2")
        c_m, c_msq = col(pfx + "m"), col(pfx + "msq")
        c_var, c_sqv, c_r = col(pfx + "var"), col(pfx + "sqv"), col(pfx + "r")
        sqv = view(sqn)
        P.add("vector", lambda e, s=src_ap, o=c_s1: e.reduce_sum(o, s, axis=AX.X),
              reads=src_regions, writes=[pfx + "s1"])
        P.add("scalar",
              lambda e, s=src_ap, o=sqv, a=c_s2: e.activation(
                  o, s, AF.Square, accum_out=a),
              reads=src_regions, writes=[f"B:{sqn}", pfx + "s2"])
        P.add("vector",
              lambda e, i=c_s1, o=c_m: e.tensor_scalar(o, i, 1.0 / D, None,
                                                       op0=ALU.mult),
              reads=[pfx + "s1"], writes=[pfx + "m"])
        P.add("vector", lambda e, i=c_m, o=c_msq: e.tensor_mul(o, i, i),
              reads=[pfx + "m"], writes=[pfx + "msq"])
        P.add("vector",
              lambda e, i=c_s2, j=c_msq, o=c_var: e.scalar_tensor_tensor(
                  o, i, 1.0 / D, j, op0=ALU.mult, op1=ALU.subtract),
              reads=[pfx + "s2", pfx + "msq"], writes=[pfx + "var"])
        P.add("vector",
              lambda e, o=c_var: e.tensor_scalar(o, o, EPS, None, op0=ALU.add),
              reads=[pfx + "var"], writes=[pfx + "var"])
        P.add("scalar",
              lambda e, i=c_var, o=c_sqv: e.activation(o, i, AF.Sqrt),
              reads=[pfx + "var"], writes=[pfx + "sqv"])
        P.add("vector", lambda e, i=c_sqv, o=c_r: e.reciprocal(o, i),
              reads=[pfx + "sqv"], writes=[pfx + "r"])
        return c_m, c_r, [pfx + "m", pfx + "r"]

    vone_ap = view("v_nat", BF16).rearrange("p (t c) -> p t c", c=65)[:, :, 64:65]
    P.add("vector", lambda e, o=vone_ap: e.memset(o, 1.0),
          writes=[f"v_nat_{st}" for st in range(nst)], tag="vones")
    hT_bf = view("hT", BF16)
    for st in range(nst):
        par = st % 4
        a, bB, ht, sqn = f"bufA{par}", f"bufB{par}", f"htile{par}", "sq0"
        a_ap, b_ap, h_ap = view(a), view(bB), view(ht, BF16)
        P.dma("sync",
              lambda e, o=a_ap, s=st: e.dma_start(o, d_x[s * 128:(s + 1) * 128, :]),
              writes=wreg(a), tag=f"xin{st}")
        cm, cr, mr = ln_stats(a_ap, [f"B:{a}"], f"p1{par}_", sqn)
        P.add("vector",
              lambda e, i=a_ap, o=b_ap, m=cm, r=cr: e.tensor_scalar(
                  o, i, m, r, op0=ALU.subtract, op1=ALU.mult),
              reads=[f"B:{a}"] + mr, writes=wreg(bB))
        cm2, cr2, mr2 = ln_stats(b_ap, [f"B:{bB}"], f"p2{par}_", sqn)
        P.add("vector",
              lambda e, o=b_ap, m=cm2, r=cr2: e.tensor_scalar(
                  o, o, m, r, op0=ALU.subtract, op1=ALU.mult),
              reads=[f"B:{bB}"] + mr2, writes=[f"B:{bB}"])
        P.add("vector", lambda e, o=b_ap, g=lng_ap: e.tensor_mul(o, o, g),
              reads=[f"B:{bB}", "B:lng"], writes=[f"B:{bB}"])
        P.add("gpsimd", lambda e, i=b_ap, o=h_ap, bb=lnb_ap: e.tensor_add(o, i, bb),
              reads=[f"B:{bB}", "B:lnb"], writes=wreg(ht))
        for dt in range(ndt):
            bank = dt % 4
            dst = hT_bf[:, dt * S + st * 128: dt * S + st * 128 + 128]
            transpose(pbf16(bank), h_ap[:, dt * 128:(dt + 1) * 128], id_bf_ap,
                      reads=[f"B:{ht}", "B:id_bf"], writes=[f"pbk{bank}"])
            copy_on("scalar", dst, pbf16(bank),
                    reads=[f"pbk{bank}"], writes=[f"hT_{dt}_{st}"])

    dump("dbg_hT", hT_bf, [f"hT_{dt}_{st}" for dt in range(ndt) for st in range(nst)])
    free("lng", ["B:lng"]); free("lnb", ["B:lnb"])
    free("bufA2", ["B:bufA2"]); free("bufA3", ["B:bufA3"])
    free("bufB2", ["B:bufB2"]); free("bufB3", ["B:bufB3"])
    free("htile2", ["B:htile2"]); free("htile3", ["B:htile3"])
    free("bufA0", ["B:bufA0"]); free("bufA1", ["B:bufA1"])
    alloc("wq0", ndt * 512 * 2); alloc("wq1", ndt * 512 * 2)
    wq_dma(0); wq_dma(1)

    alloc("rb0", 1024); alloc("rb1", 1024); alloc("rb2", 1024); alloc("rb3", 1024)
    rbv = [view(f"rb{i}").rearrange("p (h d) -> p h d", h=8) for i in range(4)]

    def rope_evac(bank, dst_name, st, half, ctn, stn):
        psap = pb[bank]
        q1 = psap.rearrange("p (h d) -> p h d", h=8)[:, :, 0:32]
        q2 = psap.rearrange("p (h d) -> p h d", h=8)[:, :, 32:64]
        cosb = view(ctn)[:, st * 32:(st + 1) * 32].unsqueeze(1).broadcast_to([128, 8, 32])
        sinb = view(stn)[:, st * 32:(st + 1) * 32].unsqueeze(1).broadcast_to([128, 8, 32])
        dst = view(dst_name, BF16)[:, st * 1024 + half * 512:
                                   st * 1024 + (half + 1) * 512]
        d1 = dst.rearrange("p (h d) -> p h d", h=8)[:, :, 0:32]
        d2 = dst.rearrange("p (h d) -> p h d", h=8)[:, :, 32:64]
        preg = [f"pbk{bank}"]
        treg = ["B:" + ctn, "B:" + stn]
        dreg = [f"{dst_name}_{st}"]
        P.add("vector", lambda e, o=rbv[0], i=q1, c=cosb: e.tensor_mul(o, i, c),
              reads=preg + treg, writes=["B:rb0"])
        P.add("vector", lambda e, o=rbv[1], i=q2, c=sinb: e.tensor_mul(o, i, c),
              reads=preg + treg, writes=["B:rb1"])
        P.add("vector", lambda e, o=rbv[2], i=q2, c=cosb: e.tensor_mul(o, i, c),
              reads=preg + treg, writes=["B:rb2"])
        P.add("vector", lambda e, o=rbv[3], i=q1, c=sinb: e.tensor_mul(o, i, c),
              reads=preg + treg, writes=["B:rb3"])
        P.add("vector", lambda e, o=d1: e.tensor_sub(o, rbv[0], rbv[1]),
              reads=["B:rb0", "B:rb1"], writes=dreg)
        P.add("vector", lambda e, o=d2: e.tensor_add(o, rbv[2], rbv[3]),
              reads=["B:rb2", "B:rb3"], writes=dreg)

    alloc("qn_acc", 16 * 4); alloc("kn_acc", 16 * 4); alloc("n_tmp", 16 * 4)

    def norm_tile(src_name, accn, st):
        acc_ap = view(accn)[:, 0:16]
        tmp_ap = view("n_tmp")[:, 0:16]
        sreg = [f"{src_name}_{st}"]
        src = view(src_name, BF16)[:, st * 1024:(st + 1) * 1024]
        sq_ap = view("sq0")
        P.add("vector", lambda e, s=src, o=sq_ap: e.tensor_mul(o, s, s),
              reads=sreg, writes=["B:sq0"])
        dst = acc_ap if st == 0 else tmp_ap
        dregs = [f"B:{accn}"] if st == 0 else ["R:ntmp"]
        P.add("vector",
              lambda e, o=dst, i=sq_ap: e.reduce_sum(
                  o.unsqueeze(2), i.rearrange("p (h d) -> p h d", h=16),
                  axis=AX.X),
              reads=["B:sq0"], writes=dregs)
        if st > 0:
            P.add("vector",
                  lambda e, a=acc_ap, t=tmp_ap: e.tensor_max(a, a, t),
                  reads=["R:ntmp", f"B:{accn}"], writes=[f"B:{accn}"])

    _pre_chunks = []
    for u in range(nunits):
        for c in range(8):
            _pre_chunks.append(("w1", u, c))
            _pre_chunks.append(("w2", u, c))

    def pump_prepass(n):
        for _ in range(n):
            if not _pre_chunks:
                return
            kind, u, c = _pre_chunks.pop(0)
            if kind == "w1":
                src = d_w1[u, c * 128:(c + 1) * 128, :]
                dst = d_w1pre[u, c * 128:(c + 1) * 128, :]
                P.dma("gpsimd", lambda e, o=dst, s=src: e.dma_start(o, s),
                      writes=[f"w1pre_{u}_{c}"], tag=f"wpre1_{u}_{c}")
            else:
                src = d_w2[u, c * 256:(c + 1) * 256, :]
                dst = d_w2pre[u, c * 256:(c + 1) * 256, :]
                P.dma("gpsimd", lambda e, o=dst, s=src: e.dma_start(o, s),
                      writes=[f"w2pre_{u}_{c}"], tag=f"wpre2_{u}_{c}")

    nfc = 3 * D // 512
    nqf = D // 512
    for fc in range(nfc):
        wbuf = f"wq{fc % 2}"
        wb_ap = view(wbuf, BF16)
        for st in range(nst):
            bank = 4 + (st % 4)
            for dt in range(ndt):
                mm(pb[bank],
                   hT_bf[:, dt * S + st * 128: dt * S + st * 128 + 128],
                   wb_ap[:, dt * 512:(dt + 1) * 512],
                   start=(dt == 0), stop=(dt == ndt - 1),
                   reads=[f"hT_{dt}_{st}", f"wqbuf{fc % 2}"],
                   writes=[f"pbk{bank}"], tag=f"qkv{fc}_{st}")
            if fc < nqf:
                rope_evac(bank, "q_nat", st, fc % nqf, "ctq", "stq")
            elif fc < 2 * nqf:
                rope_evac(bank, "k_nat", st, fc % nqf, "ctk", "stk")
            else:
                half = fc % nqf
                dst = view("v_nat", BF16)
                dst = dst[:, st * 1040 + half * 8 * 65:
                          st * 1040 + (half + 1) * 8 * 65]
                dst = dst.rearrange("p (h c) -> p h c", c=65)[:, :, 0:64]
                src_ps = pb[bank].rearrange("p (h c) -> p h c", c=64)
                P.add("scalar", lambda e, d=dst, s=src_ps: e.copy(d, s),
                      reads=[f"pbk{bank}"], writes=[f"v_nat_{st}"])
            if fc == 2:
                norm_tile("q_nat", "qn_acc", st)
            elif fc == 4:
                norm_tile("k_nat", "kn_acc", st)
            if st % 4 == 3:
                pump_prepass(1)
        if fc + 2 < nfc:
            wq_dma(fc + 2)

    dump("dbg_qn", view("q_nat", BF16), [f"q_nat_{st}" for st in range(nst)])
    dump("dbg_kn", view("k_nat", BF16), [f"k_nat_{st}" for st in range(nst)])
    dump("dbg_vn", view("v_nat", BF16), [f"v_nat_{st}" for st in range(nst)])
    free("hT", [f"hT_{dt}_{st}" for dt in range(ndt) for st in range(nst)])
    for nm in ("rb0", "rb1", "rb2", "rb3"):
        free(nm, [f"B:{nm}"])
    for nm in ("ctq", "stq", "ctk", "stk"):
        free(nm, [f"B:{nm}"])
    free("wq0", ["B:wq0", "wqbuf0"]); free("wq1", ["B:wq1", "wqbuf1"])

    c_qn, c_kn = col("qn1"), col("kn1")
    P.add("vector",
          lambda e, o=c_qn, i=view("qn_acc")[:, 0:16]: e.reduce_max(o, i, axis=AX.X),
          reads=["B:qn_acc"], writes=["c_qn1"])
    P.add("vector",
          lambda e, o=c_kn, i=view("kn_acc")[:, 0:16]: e.reduce_max(o, i, axis=AX.X),
          reads=["B:kn_acc"], writes=["c_kn1"])
    transpose(pb[0][0:1, 0:128], c_qn, id_f_ap,
              reads=["c_qn1", "B:id_f"], writes=["pbk0"])
    c_mq, c_mk, c_mm2, c_m, c_nm = (col("mq"), col("mk"), col("mm2"), col("m"),
                                    col("nm"))
    P.add("vector", lambda e, o=c_mq: e.reduce_max(o[0:1, :], pb[0][0:1, 0:128],
                                                   axis=AX.X),
          reads=["pbk0"], writes=["c_mq"])
    transpose(pb[1][0:1, 0:128], c_kn, id_f_ap,
              reads=["c_kn1", "B:id_f"], writes=["pbk1"])
    P.add("vector", lambda e, o=c_mk: e.reduce_max(o[0:1, :], pb[1][0:1, 0:128],
                                                   axis=AX.X),
          reads=["pbk1"], writes=["c_mk"])
    P.add("vector",
          lambda e, o=c_mm2, a=c_mq, b=c_mk: e.tensor_mul(o[0:1, :], a[0:1, :],
                                                          b[0:1, :]),
          reads=["c_mq", "c_mk"], writes=["c_mm2"])
    P.add("scalar",
          lambda e, o=c_m, i=c_mm2: e.activation(o[0:1, :], i[0:1, :], AF.Sqrt),
          reads=["c_mm2"], writes=["c_m"])
    P.add("vector",
          lambda e, o=c_nm, i=c_m: e.tensor_scalar(o[0:1, :], i[0:1, :], -1.0,
                                                   None, op0=ALU.mult),
          reads=["c_m"], writes=["c_nm"])
    mm(pb[2][:, 0:1], onesf[0:1, 0:128], c_nm[0:1, :], True, True,
       reads=["c_nm", "B:ones_f"], writes=["pbk2"], tag="negM_bc")
    P.add("vector", lambda e, o=negM_ap: e.tensor_copy(o, pb[2][:, 0:1]),
          reads=["pbk2"], writes=wreg("negM"))
    dump("dbg_negM", negM_ap, ["B:negM"])
    free("qn_acc", ["B:qn_acc"]); free("kn_acc", ["B:kn_acc"])
    free("n_tmp", ["R:ntmp"])

    alloc("qT", ndt * S * 2, [f"qT_{dt}_{st}" for dt in range(ndt) for st in range(nst)])
    qT_bf = view("qT", BF16)
    for st in range(nst):
        for dt in range(ndt):
            bank = dt % 4
            transpose(pbf16(bank),
                      view("q_nat", BF16)[:, st * 1024 + dt * 128:
                                          st * 1024 + (dt + 1) * 128],
                      id_bf_ap, reads=[f"q_nat_{st}", "B:id_bf"],
                      writes=[f"pbk{bank}"])
            copy_on("scalar",
                    qT_bf[:, dt * S + st * 128: dt * S + st * 128 + 128],
                    pbf16(bank), reads=[f"pbk{bank}"], writes=[f"qT_{dt}_{st}"])
    free("q_nat", [f"q_nat_{st}" for st in range(nst)])
    alloc("kT", ndt * S * 2, [f"kT_{dt}_{st}" for dt in range(ndt) for st in range(nst)])
    kT_bf = view("kT", BF16)
    for st in range(nst):
        for dt in range(ndt):
            bank = 4 + dt % 4
            transpose(pbf16(bank),
                      view("k_nat", BF16)[:, st * 1024 + dt * 128:
                                          st * 1024 + (dt + 1) * 128],
                      id_bf_ap, reads=[f"k_nat_{st}", "B:id_bf"],
                      writes=[f"pbk{bank}"])
            copy_on("scalar",
                    kT_bf[:, dt * S + st * 128: dt * S + st * 128 + 128],
                    pbf16(bank), reads=[f"pbk{bank}"], writes=[f"kT_{dt}_{st}"])
    free("k_nat", [f"k_nat_{st}" for st in range(nst)])

    alloc("oT", ndt * S * 2, [f"oT_{hp}_{ic}_{p}" for hp in range(ndt) for ic in range(nic) for p in (0, 1)])
    alloc("slabA", njt * 512 * 2, [f"slabA_{jt}" for jt in range(njt)])
    alloc("slabB", njt * 512 * 2, [f"slabB_{jt}" for jt in range(njt)])
    alloc("recbuf", 1024 * 4, ["rec0", "rec1"])
    alloc("bcscr", 1024 * 4, ["bcs0", "bcs1"])
    alloc("ostg", 1024 * 2, ["ostg0", "ostg1"])
    oT_bf = view("oT", BF16)
    slab_ap = {"slabA": view("slabA", BF16), "slabB": view("slabB", BF16)}
    v_bf = view("v_nat", BF16)
    rec_ap = view("recbuf", F32, cols=1024)
    bcs_ap = view("bcscr", F32, cols=1024)
    ostg_ap = view("ostg", BF16, cols=1024)

    def emit_spair(hp, ic, jt):
        for (hh, r0, bank, slab) in ((2 * hp, 0, jt % 2, "slabA"),
                                     (2 * hp + 1, 64, 2 + jt % 2, "slabB")):
            lhs = kT_bf[r0:r0 + 64, hp * S + jt * 128: hp * S + (jt + 1) * 128]
            rhs = qT_bf[r0:r0 + 64, hp * S + ic * 512: hp * S + (ic + 1) * 512]
            rds = [f"kT_{hp}_{jt}"] + [f"qT_{hp}_{4 * ic + q}" for q in range(4)]
            mm(pb[bank], lhs, rhs, True, True, reads=rds,
               writes=[f"pbk{bank}"], tp=(r0, 0), tag=f"S{hh}_{ic}_{jt}")
            dst = slab_ap[slab][:, jt * 512:(jt + 1) * 512]
            P.add("scalar",
                  lambda e, o=dst, bk=bank, nm=negM_ap: e.activation(
                      o, pb[bk], AF.Exp, bias=nm, scale=1.0),
                  reads=[f"pbk{bank}", "B:negM"],
                  writes=[f"{slab}_{jt}"], tag=f"exp{hh}_{ic}_{jt}")

    def emit_ovstep(hp, ic, jt):
        for (hh, slab, bank) in ((2 * hp, "slabA", 4), (2 * hp + 1, "slabB", 5)):
            vl = v_bf[:, jt * 1040 + hh * 65: jt * 1040 + hh * 65 + 65]
            sl = slab_ap[slab][:, jt * 512:(jt + 1) * 512]
            mm(pb[bank][0:65, :], vl, sl, jt == 0, jt == njt - 1,
               reads=[f"v_nat_{jt}", f"{slab}_{jt}"], writes=[f"pbk{bank}"],
               tp=(0, 0), tag=f"ov{hh}_{ic}_{jt}")

    def emit_chunk(hp, ic, hook=None):
        LOOK = 2
        for jt in range(njt):
            emit_spair(hp, ic, jt)
            if jt == 0 and hook is not None:
                hook()
            if jt >= LOOK:
                emit_ovstep(hp, ic, jt - LOOK)
        for jt in range(njt - LOOK, njt):
            emit_ovstep(hp, ic, jt)

    def emit_norm(hp, ic):
        P.add("vector",
              lambda e, o=rec_ap: e.reciprocal(o[64:65, 0:512], pb[4][64:65, :]),
              reads=["pbk4"], writes=["rec0"])
        P.add("vector",
              lambda e, o=rec_ap: e.reciprocal(o[64:65, 512:1024], pb[5][64:65, :]),
              reads=["pbk5"], writes=["rec1"])
        mm(pb[7][0:64, :], onesf[64:65, 0:64], rec_ap[64:65, 0:512], True, True,
           reads=["rec0", "B:ones_f"], writes=["pbk7"],
           tp=(64, 0), tag=f"bc{2 * hp}_{ic}")
        P.add("vector",
              lambda e, o=bcs_ap: e.tensor_copy(o[0:64, 0:512], pb[7][0:64, :]),
              reads=["pbk7"], writes=["bcs0"])
        dst = oT_bf[0:64, hp * S + ic * 512: hp * S + (ic + 1) * 512]
        P.add("vector",
              lambda e, o=dst, c=bcs_ap: e.tensor_mul(
                  o, pb[4][0:64, :], c[0:64, 0:512]),
              reads=["pbk4", "bcs0"],
              writes=[f"oT_{hp}_{ic}_0"], tag=f"onrm{2 * hp}_{ic}")
        mm(pb[6][0:64, :], onesf[64:65, 0:64], rec_ap[64:65, 512:1024], True, True,
           reads=["rec1", "B:ones_f"], writes=["pbk6"],
           tp=(64, 0), tag=f"bc{2 * hp + 1}_{ic}")
        P.add("vector",
              lambda e, o=bcs_ap: e.tensor_copy(o[0:64, 512:1024], pb[6][0:64, :]),
              reads=["pbk6"], writes=["bcs1"])
        par = ic % 2
        stg = ostg_ap[0:64, par * 512:(par + 1) * 512]
        P.add("vector",
              lambda e, o=stg, c=bcs_ap: e.tensor_mul(
                  o, pb[5][0:64, :], c[0:64, 512:1024]),
              reads=["pbk5", "bcs1"], writes=[f"ostg{par}"],
              tag=f"onrm{2 * hp + 1}_{ic}")
        dsto = oT_bf[64:128, hp * S + ic * 512: hp * S + (ic + 1) * 512]
        P.dma("sync", lambda e, o=dsto, i=stg: e.dma_start(o, i),
              reads=[f"ostg{par}"], writes=[f"oT_{hp}_{ic}_1"],
              tag=f"oshift{2 * hp + 1}_{ic}")

    pending = []
    for hp in range(NH // 2):
        for ic in range(nic):
            hook = None
            if pending:
                prev = pending.pop(0)
                hook = (lambda pv: (lambda: emit_norm(*pv)))(prev)
            emit_chunk(hp, ic, hook)
            if dbg and hp == 0 and ic == 0:
                dump("dbg_slab", slab_ap["slabA"],
                     [f"slabA_{jt}" for jt in range(njt)])
            pending.append((hp, ic))
            pump_prepass(4)
        if hp == NH // 2 - 1:
            free("kT", [f"kT_{dt}_{st}" for dt in range(ndt) for st in range(nst)])
            alloc("woutT", ndt * 1024 * 2)
            wout_dst = view("woutT", BF16).rearrange("p (d f) -> p d f", f=1024)
            P.dma("gpsimd",
                  lambda e, o=wout_dst: e.dma_start(
                      o, d_woutT.rearrange("(d p) f -> p d f", p=128)),
                  writes=wreg("woutT"))
    while pending:
        emit_norm(*pending.pop(0))
    pump_prepass(len(_pre_chunks))

    dump("dbg_oT", oT_bf, [f"oT_{hp}_{ic}_{p}" for hp in range(ndt)
                           for ic in range(nic) for p in (0, 1)])
    free("slabA", [f"slabA_{jt}" for jt in range(njt)])
    free("slabB", [f"slabB_{jt}" for jt in range(njt)])
    free("v_nat", [f"v_nat_{st}" for st in range(nst)])
    free("qT", [f"qT_{dt}_{st}" for dt in range(ndt) for st in range(nst)])
    free("recbuf", ["rec0", "rec1"])
    free("bcscr", ["bcs0", "bcs1"])
    free("ostg", ["ostg0", "ostg1"])

    alloc("bufA0", 4096); alloc("bufA1", 4096)
    alloc("bufA2", 4096); alloc("bufA3", 4096)
    alloc("bufB2", 4096); alloc("bufB3", 4096)
    alloc("htile2", 2048); alloc("htile3", 2048)
    alloc("bout", 4096)
    bout_ap = view("bout")[0:1, 0:D]
    P.dma("sync", lambda e, o=bout_ap: e.dma_start(o, d_bout), writes=wreg("bout"))
    alloc("flatT", ndt * S * 2, [f"flatT_{dt}_{st}" for dt in range(ndt) for st in range(nst)])
    flat_bf = view("flatT", BF16)
    wout_bf = view("woutT", BF16)

    h0z = view("htile0", BF16)
    P.add("vector", lambda e, o=h0z: e.memset(o, 0.0), writes=["B:htile0"],
          tag="flatb_z")
    P.dma("sync",
          lambda e, i=h0z: e.dma_start(d_flatb[S:S + 128, :], i),
          reads=["B:htile0"], writes=["flatb_trash"], tag="flatb_zw")

    def p4_front(st):
        par = st % 4
        a, bB, ht, sqn = f"bufA{par}", f"bufB{par}", f"htile{par}", "sq0"
        a_ap, b_ap, h_ap = view(a), view(bB), view(ht, BF16)
        P.dma("sync",
              lambda e, o=a_ap, s=st: e.dma_start(o, d_x[s * 128:(s + 1) * 128, :]),
              writes=wreg(a), tag=f"xre{st}")
        for dc in range(ndc):
            bank = (2 * st + dc) % 4
            for dt in range(ndt):
                mm(pb[bank],
                   oT_bf[:, dt * S + st * 128: dt * S + st * 128 + 128],
                   wout_bf[:, dt * 1024 + dc * 512: dt * 1024 + (dc + 1) * 512],
                   start=(dt == 0), stop=False,
                   reads=[f"oT_{dt}_{st // 4}_{p}" for p in (0, 1)] + ["B:woutT"],
                   writes=[f"pbk{bank}"], tag=f"aopj{st}_{dc}")
            mm(pb[bank], onesf[0:1, 0:128], bout_ap[:, dc * 512:(dc + 1) * 512],
               False, True, reads=["B:bout", "B:ones_f"], writes=[f"pbk{bank}"],
               tag=f"aob{st}_{dc}")
            copy_on("scalar", b_ap[:, dc * 512:(dc + 1) * 512], pb[bank],
                    reads=[f"pbk{bank}"], writes=[f"B:{bB}"])
        cm, cr, mr = ln_stats(b_ap, [f"B:{bB}"], f"p3{par}_", sqn)
        P.add("vector",
              lambda e, o=b_ap, m=cm, r=cr: e.tensor_scalar(
                  o, o, m, r, op0=ALU.subtract, op1=ALU.mult),
              reads=[f"B:{bB}"] + mr, writes=[f"B:{bB}"])
        P.add("gpsimd", lambda e, o=b_ap, i=a_ap: e.tensor_add(o, o, i),
              reads=[f"B:{bB}", f"B:{a}"], writes=[f"B:{bB}"])
        P.dma("sync",
              lambda e, i=b_ap, s=st: e.dma_start(d_xsp[s * 128:(s + 1) * 128, :], i),
              reads=[f"B:{bB}"], writes=[f"xsp_{st}"], tag=f"spill{st}")
        cm4, cr4, mr4 = ln_stats(b_ap, [f"B:{bB}"], f"p4{par}_", sqn)
        P.add("vector",
              lambda e, i=b_ap, o=h_ap, m=cm4, r=cr4: e.tensor_scalar(
                  o, i, m, r, op0=ALU.subtract, op1=ALU.mult),
              reads=[f"B:{bB}"] + mr4, writes=wreg(ht))
        P.dma("sync",
              lambda e, i=h_ap, s=st: e.dma_start(
                  d_flatb[s * 128:(s + 1) * 128, :], i),
              reads=[f"B:{ht}"], writes=[f"flatb_{st}"], tag=f"flath{st}")

    def p4_back(st):
        par = st % 4
        ht = f"htile{par}"
        h_ap = view(ht, BF16)
        for dt in range(ndt):
            bank = 4 + dt % 4
            transpose(pbf16(bank), h_ap[:, dt * 128:(dt + 1) * 128], id_bf_ap,
                      reads=[f"B:{ht}", "B:id_bf"], writes=[f"pbk{bank}"],
                      tag=f"ftp{st}_{dt}")
            copy_on("scalar",
                    flat_bf[:, dt * S + st * 128: dt * S + st * 128 + 128],
                    pbf16(bank), reads=[f"pbk{bank}"], writes=[f"flatT_{dt}_{st}"])

    for st in range(nst):
        p4_front(st)
        if st > 0:
            p4_back(st - 1)
    p4_back(nst - 1)

    free("oT", [f"oT_{hp}_{ic}_{p}" for hp in range(ndt) for ic in range(nic)
                for p in (0, 1)])
    free("woutT", ["B:woutT"])
    free("bout", ["B:bout"])

    alloc("w_sb", nst * NE * 4, [f"w_{tt}" for tt in range(nst)])
    alloc("m_sb", nst * NE * 4, [f"m_{tt}" for tt in range(nst)])
    alloc("rbq", NE * 4)
    alloc("lgt", NE * 4); alloc("prb", NE * 4); alloc("msk", NE * 4)
    w_ap = view("w_sb")
    m_ap = view("m_sb")
    rbq_ap = view("rbq")[:, 0:NE]
    lgt_ap = view("lgt")[:, 0:NE]
    prb_ap = view("prb")[:, 0:NE]
    msk_ap = view("msk")[:, 0:NE]
    rwq_bf = view("rwq", BF16)
    P.dma("sync", lambda e, o=rbq_ap: e.dma_start(o, d_rbq), writes=wreg("rbq"))
    c_nmx, c_rs, c_m1p, c_m2p = col("nmx"), col("rs"), col("m1p"), col("m2p")
    c_rrs = col("rrs")
    for tt in range(nst):
        bank = tt % 2
        for dt in range(ndt):
            mm(pb[bank][:, 0:128],
               flat_bf[:, dt * S + tt * 128: dt * S + tt * 128 + 128],
               rwq_bf[:, dt * 128:(dt + 1) * 128],
               start=(dt == 0), stop=(dt == ndt - 1),
               reads=[f"flatT_{dt}_{tt}", "B:rwq"], writes=[f"pbk{bank}"],
               tag=f"rtr{tt}")
        P.add("vector",
              lambda e, bk=bank, o=lgt_ap, b=rbq_ap: e.tensor_add(
                  o, pb[bk][:, 0:NE], b),
              reads=[f"pbk{bank}", "B:rbq"], writes=["B:lgt"])
        P.add("vector", lambda e, o=c_nmx, i=lgt_ap: e.reduce_max(
            o, i, axis=AX.X, negate=True),
              reads=["B:lgt"], writes=["c_nmx"])
        P.add("scalar",
              lambda e, o=prb_ap, i=lgt_ap, b=c_nmx: e.activation(
                  o, i, AF.Exp, bias=b),
              reads=["B:lgt", "c_nmx"], writes=["B:prb"])
        P.add("vector", lambda e, o=c_rs, i=prb_ap: e.reduce_sum(o, i, axis=AX.X),
              reads=["B:prb"], writes=["c_rs"])
        P.add("vector", lambda e, o=c_rrs, i=c_rs: e.reciprocal(o, i),
              reads=["c_rs"], writes=["c_rrs"])
        P.add("vector",
              lambda e, o=prb_ap, s=c_rrs: e.tensor_scalar(o, o, s, None,
                                                           op0=ALU.mult),
              reads=["B:prb", "c_rrs"], writes=["B:prb"])
        P.add("vector", lambda e, o=c_m1p, i=prb_ap: e.reduce_max(o, i, axis=AX.X),
              reads=["B:prb"], writes=["c_m1p"])
        P.add("vector",
              lambda e, o=msk_ap, i=prb_ap, m=c_m1p: e.tensor_scalar(
                  o, i, m, -2.0, op0=ALU.is_equal, op1=ALU.mult),
              reads=["B:prb", "c_m1p"], writes=["B:msk"])
        P.add("vector", lambda e, o=msk_ap, i=prb_ap: e.tensor_add(o, o, i),
              reads=["B:msk", "B:prb"], writes=["B:msk"])
        P.add("vector", lambda e, o=c_m2p, i=msk_ap: e.reduce_max(o, i, axis=AX.X),
              reads=["B:msk"], writes=["c_m2p"])
        mdst = m_ap[:, tt * NE:(tt + 1) * NE]
        P.add("vector",
              lambda e, o=mdst, i=prb_ap, m=c_m2p: e.tensor_scalar(
                  o, i, m, None, op0=ALU.is_ge),
              reads=["B:prb", "c_m2p"], writes=[f"m_{tt}"])
        wdst = w_ap[:, tt * NE:(tt + 1) * NE]
        P.add("vector",
              lambda e, o=wdst, i=prb_ap, m=mdst: e.tensor_mul(o, i, m),
              reads=["B:prb", f"m_{tt}"], writes=[f"w_{tt}"])
    dump("dbg_flatT", flat_bf, [f"flatT_{dt}_{st}" for dt in range(ndt)
                                for st in range(nst)])
    dump("dbg_w", w_ap, [f"w_{tt}" for tt in range(nst)])
    free("rwq", ["B:rwq"])
    free("lgt", ["B:lgt"]); free("prb", ["B:prb"]); free("msk", ["B:msk"])
    free("rbq", ["B:rbq"])

    BIGS = 2.0e6
    alloc("tri_sb", 512)
    alloc("iota_sb", nst * 4)
    alloc("erow", 16 * 4)
    alloc("srt", 64 * 4)
    alloc("sig", 16 * 4)
    alloc("psf", nst * NE * 4)
    alloc("bas", nst * NE * 4)
    alloc("bas3", nst * NE * 4)
    alloc("bsc", NE * 4)
    alloc("irep", 512)
    alloc("pbuf", 512)
    alloc("cw", 16 * 4)
    alloc("stg", 16 * 4)
    alloc("sent", (NTILE + 1) * 2 * 4)
    tri_ap = view("tri_sb", F32, cols=128)
    irep_ap = view("irep", F32, cols=128)
    pbuf_ap = view("pbuf", F32, cols=128)
    cw_ap = view("cw", F32, cols=16)
    psf_ap = view("psf", F32, cols=nst * NE)
    bas_i = view("bas", I32, cols=nst * NE)
    bas3_i = view("bas3", I32, cols=nst * NE)
    bsc_ap = view("bsc", F32, cols=NE)
    iota_ap = view("iota_sb", F32, cols=nst)
    erow_ap = view("erow", F32, cols=16)
    srt_ap = view("srt", F32, cols=64)
    sig_i = view("sig", I32, cols=16)
    stg_ap = view("stg", F32, cols=16)
    sent_ap = view("sent", F32, cols=(NTILE + 1) * 2)
    P.dma("sync", lambda e, o=tri_ap: e.dma_start(o, d_tri), writes=wreg("tri_sb"))
    P.dma("sync", lambda e, o=irep_ap: e.dma_start(o, d_irep), writes=wreg("irep"))
    P.dma("sync", lambda e, o=iota_ap: e.dma_start(o, d_itok), writes=wreg("iota_sb"))
    P.dma("sync", lambda e, o=erow_ap[0:1, 0:NE]: e.dma_start(o, d_eidx),
          writes=wreg("erow"))
    P.dma("sync", lambda e, o=erow_ap[0:1, 8:8 + NE]: e.dma_start(o, d_etag),
          writes=["B:erow"])
    P.add("vector", lambda e, o=sent_ap: e.memset(o, 1.0e5), writes=wreg("sent"))
    P.dma("scalar",
          lambda e, o=d_tokw.rearrange("(r p) c -> p r c", p=128),
          i=sent_ap.rearrange("p (r c) -> p r c", c=2): e.dma_start(o, i),
          reads=["B:sent"],
          writes=["tokw"] + [f"tokw_e{ee}" for ee in range(NE)], tag="tokw_init")

    for tt in range(nst):
        mm(pb[6][0:1, 0:NE], onesf[:, 0:1], m_ap[:, tt * NE:(tt + 1) * NE],
           start=(tt == 0), stop=(tt == nst - 1),
           reads=[f"m_{tt}", "B:ones_f"], writes=["pbk6"], tag=f"lds{tt}")
    LD, OH, TMP, XOF, MK, SGF = 0, 8, 16, 24, 40, 48
    P.add("vector", lambda e, o=srt_ap[0:1, LD:LD + NE]: e.tensor_copy(
        o, pb[6][0:1, 0:NE]), reads=["pbk6"], writes=["srt"])
    P.add("vector",
          lambda e, o=srt_ap[0:1, LD:LD + NE], t=erow_ap[0:1, 8:8 + NE]:
          e.tensor_add(o, o, t),
          reads=["srt", "B:erow"], writes=["srt"])
    for k in range(NE):
        P.add("vector",
              lambda e, o=srt_ap[0:1, MK:MK + 1], i=srt_ap[0:1, LD:LD + NE]:
              e.reduce_max(o, i, axis=AX.X),
              reads=["srt"], writes=["srt"])
        P.add("vector",
              lambda e, o=srt_ap[0:1, OH:OH + NE], i=srt_ap[0:1, LD:LD + NE],
              m=srt_ap[0:1, MK:MK + 1]: e.tensor_scalar(
                  o, i, m, None, op0=ALU.is_equal),
              reads=["srt"], writes=["srt"])
        P.add("vector",
              lambda e, o=srt_ap[0:1, TMP:TMP + NE], a=srt_ap[0:1, OH:OH + NE],
              b=erow_ap[0:1, 0:NE]: e.tensor_mul(o, a, b),
              reads=["srt", "B:erow"], writes=["srt"])
        P.add("vector",
              lambda e, o=srt_ap[0:1, SGF + k:SGF + k + 1],
              i=srt_ap[0:1, TMP:TMP + NE]: e.reduce_sum(o, i, axis=AX.X),
              reads=["srt"], writes=["srt"])
        if k == 0:
            P.add("vector",
                  lambda e, o=srt_ap[0:1, XOF:XOF + NE],
                  a=srt_ap[0:1, OH:OH + NE], u=float(UOFF[0]): e.tensor_scalar(
                      o, a, u, None, op0=ALU.mult),
                  reads=["srt"], writes=["srt"])
        else:
            P.add("vector",
                  lambda e, o=srt_ap[0:1, XOF:XOF + NE],
                  a=srt_ap[0:1, OH:OH + NE], u=float(UOFF[k]):
                  e.scalar_tensor_tensor(o, a, u, o, op0=ALU.mult, op1=ALU.add),
                  reads=["srt"], writes=["srt"])
        P.add("vector",
              lambda e, o=srt_ap[0:1, LD:LD + NE], a=srt_ap[0:1, OH:OH + NE]:
              e.scalar_tensor_tensor(o, a, -1.0e9, o, op0=ALU.mult, op1=ALU.add),
              reads=["srt"], writes=["srt"])
    P.add("vector",
          lambda e, o=sig_i[0:1, 0:NE], i=srt_ap[0:1, SGF:SGF + NE]:
          e.tensor_copy(o, i),
          reads=["srt"], writes=["sig"])

    for tt in range(nst):
        mm(pb[7][:, 0:NE], onesf[0:1, 0:128], srt_ap[0:1, XOF:XOF + NE],
           start=True, stop=False, reads=["srt", "B:ones_f"], writes=["pbk7"],
           tag=f"sb{tt}")
        for tp_ in range(tt):
            mm(pb[7][:, 0:NE], onesf, m_ap[:, tp_ * NE:(tp_ + 1) * NE],
               start=False, stop=False,
               reads=[f"m_{tp_}", "B:ones_f"], writes=["pbk7"], tag=f"sc{tt}_{tp_}")
        mm(pb[7][:, NE:2 * NE], tri_ap, m_ap[:, tt * NE:(tt + 1) * NE],
           start=False, stop=True,
           reads=[f"m_{tt}", "B:tri_sb"], writes=["pbk7"], tag=f"st{tt}")
        P.add("vector",
              lambda e, o=bsc_ap[0:1, 0:NE]: e.tensor_scalar(
                  o, pb[7][0:1, 0:NE], 8.0, None, op0=ALU.mult),
              reads=["pbk7"], writes=["B:bsc"])
        P.add("vector",
              lambda e, o=bas_i[0:1, tt * NE:(tt + 1) * NE],
              i=bsc_ap[0:1, 0:NE]: e.tensor_copy(o, i),
              reads=["B:bsc"], writes=[f"bas_{tt}"])
        P.add("vector",
              lambda e, o=bsc_ap[0:1, 0:NE]: e.tensor_scalar(
                  o, pb[7][0:1, 0:NE], 2048.0, None, op0=ALU.mult),
              reads=["pbk7", f"bas_{tt}"], writes=["B:bsc"])
        P.add("vector",
              lambda e, o=bas3_i[0:1, tt * NE:(tt + 1) * NE],
              i=bsc_ap[0:1, 0:NE]: e.tensor_copy(o, i),
              reads=["B:bsc"], writes=[f"bas_{tt}"])
        pslice = psf_ap[:, tt * NE:(tt + 1) * NE]
        P.add("vector",
              lambda e, o=pslice, m=m_ap[:, tt * NE:(tt + 1) * NE]:
              e.scalar_tensor_tensor(o, m, 1000.0, pb[7][:, NE:2 * NE],
                                     op0=ALU.mult, op1=ALU.add),
              reads=[f"m_{tt}", "pbk7"], writes=[f"psf_{tt}"])
        P.add("vector",
              lambda e, o=pslice: e.tensor_scalar(o, o, -1000.0, None,
                                                  op0=ALU.add),
              reads=[f"psf_{tt}"], writes=[f"psf_{tt}"])
        stg3 = stg_ap.rearrange("p (e c) -> p e c", c=2)
        P.add("vector",
              lambda e, o=stg3[:, :, 0:1],
              i=iota_ap[:, tt:tt + 1].unsqueeze(1).broadcast_to([128, NE, 1]):
              e.tensor_copy(o, i),
              reads=["B:iota_sb"], writes=["B:stg"])
        P.add("vector",
              lambda e, o=stg3[:, :, 1:2],
              i=w_ap[:, tt * NE:(tt + 1) * NE].unsqueeze(2): e.tensor_copy(o, i),
              reads=[f"w_{tt}"], writes=["B:stg"])
        for ee in range(NE):
            P.add("vector",
                  lambda e, o=pbuf_ap, m=psf_ap[:, tt * NE + ee:tt * NE + ee + 1]:
                  e.tensor_scalar(o, irep_ap, m, None, op0=ALU.is_equal),
                  reads=[f"psf_{tt}", "B:irep"], writes=["B:pbuf"],
                  tag=f"pb{tt}_{ee}")
            mm(pb[6][:, 2 * ee:2 * ee + 2], pbuf_ap,
               stg_ap[:, 2 * ee:2 * ee + 2],
               start=(ee == 0), stop=(ee == NE - 1),
               reads=["B:pbuf", "B:stg"], writes=["pbk6"], tag=f"cm{tt}_{ee}")
        P.add("vector", lambda e, o=cw_ap: e.tensor_copy(o, pb[6][:, 0:16]),
              reads=["pbk6"], writes=["B:cw"])
        for ee in range(NE):
            def bwfn(e_, sem, _tt=tt, _ee=ee,
                     _src=cw_ap[:, 2 * ee:2 * ee + 2].bitcast(U8)):
                with e_.register() as r:
                    e_.load(r, bas_i[0:1, _tt * NE + _ee:_tt * NE + _ee + 1])
                    sv = e_.compute_val(r, donate=True)
                    dst = bass.AP(d_tokw8.tensor, sv, [[8, 128], [1, 8]])
                    e_.dma_start(dst, _src).then_inc(sem, 16)
            P.dma_selfinc("gpsimd", bwfn,
                          reads=[f"bas_{tt}", "B:cw", f"tokw_e{ee}"],
                          writes=[f"tokw_e{ee}"], tag=f"bw{tt}_{ee}")

    alloc("h1", 8 * S * 2, [f"h1_{h}_{t}" for h in range(8) for t in range(nic)])
    alloc("rw1a", ndt * 512 * 2, ["rw1buf0"]); alloc("rw1b", ndt * 512 * 2, ["rw1buf1"])
    alloc("rw1c", ndt * 512 * 2, ["rw1buf2"])
    alloc("rw2b", 8 * 1024 * 2, ["rw2buf"])
    alloc("b1t0", 16 * 4, ["b1buf0"]); alloc("b1t1", 16 * 4, ["b1buf1"])
    alloc("b2r0", 4096, ["b2buf0"]); alloc("b2r1", 4096, ["b2buf1"])
    h1_bf = view("h1", BF16)
    rw1_names = ["rw1a", "rw1b", "rw1c"]
    rw1_aps = [view(n, BF16) for n in rw1_names]
    rw2_ap = view("rw2b", BF16)
    b1_aps = [view("b1t0")[:, 0:nht], view("b1t1")[:, 0:nht]]
    b2_aps = [view("b2r0")[0:1, 0:D], view("b2r1")[0:1, 0:D]]
    nq_per_unit = HF // 512

    def rw1_dma(u, qi):
        slot = qi % 3
        nm = rw1_names[slot]
        dst = rw1_aps[slot].rearrange("p (d f) -> p d f", f=512)
        src = d_w1pre[u, :, qi * 512:(qi + 1) * 512].rearrange(
            "(d p) f -> p d f", p=128)
        P.dma("scalar", lambda e, o=dst, s=src: e.dma_start(o, s),
              reads=[f"w1pre_{u}_{c}" for c in range(8)],
              writes=wreg(nm, [f"rw1buf{slot}"]), tag=f"rw1_{u}_{qi}")

    def rw2_dma(u, Hh):
        dst = rw2_ap.rearrange("p (t f) -> p t f", f=1024)
        src = d_w2pre[u, Hh * (HF // 2):(Hh + 1) * (HF // 2), :].rearrange(
            "(t p) f -> p t f", p=128)
        P.dma("scalar", lambda e, o=dst, s=src: e.dma_start(o, s),
              reads=[f"w2pre_{u}_{c}" for c in range(8)],
              writes=wreg("rw2b", ["rw2buf"]), tag=f"rw2_{u}_{Hh}")

    alloc("mx0", 4096); alloc("mx1", 4096)
    rw1_dma(0, 0); rw1_dma(0, 1)
    P.dma("sync", lambda e, o=b1_aps[0]: e.dma_start(o, d_b1[0]),
          writes=wreg("b1t0", ["b1buf0"]), tag="b1_0")
    P.dma("sync", lambda e, o=b2_aps[0]: e.dma_start(o, d_b2[0:1, :]),
          writes=wreg("b2r0", ["b2buf0"]), tag="b2_0")
    for Hh in range(2):
        rw2_dma(0, Hh)
        for ht8 in range(8):
            gq = (Hh * 8 + ht8) // 4
            qslot = gq % 3
            if ht8 % 4 == 0:
                nxt = Hh * 2 + ht8 // 4 + 2
                if nxt < nq_per_unit:
                    rw1_dma(0, nxt)
            qrow = ((Hh * 8 + ht8) % 4) * 128
            for tc in range(nic):
                bank = tc % 4
                for dt in range(ndt):
                    mm(pb[bank],
                       rw1_aps[qslot][:, dt * 512 + qrow: dt * 512 + qrow + 128],
                       flat_bf[:, dt * S + tc * 512: dt * S + (tc + 1) * 512],
                       start=(dt == 0), stop=(dt == ndt - 1),
                       reads=[f"rw1buf{qslot}"] +
                             [f"flatT_{dt}_{4 * tc + q}" for q in range(4)],
                       writes=[f"pbk{bank}"], tag=f"h1s_{Hh}_{ht8}_{tc}")
                gdst = h1_bf[:, ht8 * S + tc * 512: ht8 * S + (tc + 1) * 512]
                gbias = b1_aps[0][:, Hh * 8 + ht8: Hh * 8 + ht8 + 1]
                P.add("scalar",
                      lambda e, o=gdst, bk=bank, b=gbias: e.activation(
                          o, pb[bk], AF.Gelu, bias=b),
                      reads=[f"pbk{bank}", "b1buf0"],
                      writes=[f"h1_{ht8}_{tc}"], tag=f"gls_{Hh}_{ht8}_{tc}")
        for tt in range(nst):
            par = tt % 2
            bB = f"bufB{par}"
            b_ap = view(bB)
            mxr = view(f"mx{par}")
            if Hh == 1:
                P.dma("sync",
                      lambda e, o=mxr, s=tt: e.dma_start(
                          o, d_mixh[s * 128:(s + 1) * 128, :]),
                      reads=[f"mixh_{tt}"], writes=wreg(f"mx{par}"),
                      tag=f"shr{tt}")
            for dc in range(ndc):
                bank = 4 + (2 * tt + dc) % 4
                for ht8 in range(8):
                    last = (ht8 == 7) and (Hh == 0)
                    mm(pb[bank],
                       h1_bf[:, ht8 * S + tt * 128: ht8 * S + tt * 128 + 128],
                       rw2_ap[:, ht8 * 1024 + dc * 512: ht8 * 1024 + (dc + 1) * 512],
                       start=(ht8 == 0), stop=last,
                       reads=[f"h1_{ht8}_{tt // 4}", "rw2buf"],
                       writes=[f"pbk{bank}"], tag=f"o2s_{Hh}_{tt}_{dc}")
                if Hh == 1:
                    mm(pb[bank], onesf[0:1, 0:128],
                       b2_aps[0][:, dc * 512:(dc + 1) * 512],
                       False, True, reads=["b2buf0", "B:ones_f"],
                       writes=[f"pbk{bank}"], tag=f"o2sb_{tt}_{dc}")
                if Hh == 0:
                    P.add("vector",
                          lambda e, bk=bank, o=b_ap[:, dc * 512:(dc + 1) * 512]:
                          e.tensor_copy(o, pb[bk]),
                          reads=[f"pbk{bank}"], writes=[f"B:{bB}"],
                          tag=f"shc_{tt}_{dc}")
                else:
                    P.add("vector",
                          lambda e, bk=bank, o=b_ap[:, dc * 512:(dc + 1) * 512],
                          m_=mxr[:, dc * 512:(dc + 1) * 512]:
                          e.tensor_add(o, pb[bk], m_),
                          reads=[f"pbk{bank}", f"B:mx{par}"], writes=[f"B:{bB}"],
                          tag=f"sha_{tt}_{dc}")
            P.dma("sync",
                  lambda e, i=b_ap, s=tt: e.dma_start(
                      d_mixh[s * 128:(s + 1) * 128, :], i),
                  reads=[f"B:{bB}"], writes=[f"mixh_{tt}"], tag=f"shw{Hh}_{tt}")

    free("h1", [f"h1_{h}_{t}" for h in range(8) for t in range(nic)])
    free("rw1a", ["B:rw1a", "rw1buf0"])
    free("rw1b", ["B:rw1b", "rw1buf1"])
    free("rw1c", ["B:rw1c", "rw1buf2"])
    free("rw2b", ["B:rw2b", "rw2buf"])
    free("flatT", [f"flatT_{dt}_{st}" for dt in range(ndt) for st in range(nst)])
    free("w_sb", [f"w_{tt}" for tt in range(nst)])
    free("m_sb", [f"m_{tt}" for tt in range(nst)])
    free("tri_sb", ["B:tri_sb"]); free("erow", ["B:erow"])
    free("stg", ["B:stg"]); free("sent", ["B:sent"])
    free("bufA2", ["B:bufA2"]); free("bufA3", ["B:bufA3"])
    free("bufB2", ["B:bufB2"]); free("bufB3", ["B:bufB3"])
    free("htile2", ["B:htile2"]); free("htile3", ["B:htile3"])

    alloc("tkw", NTILE * 2 * 4)
    alloc("tokf", NTILE * 4)
    alloc("toki", NTILE * 4)
    alloc("wcolb", NTILE * 4)
    tkw_ap = view("tkw", F32, cols=NTILE * 2)
    toki_ap = view("toki", I32, cols=NTILE)
    wcol_ap = view("wcolb", F32, cols=NTILE)
    P.dma("sync",
          lambda e, o=tkw_ap.rearrange("p (r c) -> p r c", c=2),
          i=d_tokw[0:NSLOT, :].rearrange("(r p) c -> p r c", p=128):
          e.dma_start(o, i),
          reads=["tokw"] + [f"tokw_e{ee}" for ee in range(NE)],
          writes=wreg("tkw"), tag="tokw_rd")
    tokf_ap = view("tokf", F32, cols=NTILE)
    P.add("vector",
          lambda e, o=tokf_ap.unsqueeze(2),
          i=tkw_ap.rearrange("p (r c) -> p r c", c=2)[:, :, 0:1]:
          e.tensor_copy(o, i), reads=["B:tkw"], writes=["B:tokf"])
    P.add("vector",
          lambda e, o=tokf_ap, s=float(S): e.tensor_scalar_min(o, o, s),
          reads=["B:tokf"], writes=["B:tokf"])
    P.add("vector",
          lambda e, o=toki_ap, i=tokf_ap: e.tensor_copy(o, i),
          reads=["B:tokf"], writes=["B:toki"])
    P.add("vector",
          lambda e, o=wcol_ap.unsqueeze(2),
          i=tkw_ap.rearrange("p (r c) -> p r c", c=2)[:, :, 1:2]:
          e.tensor_copy(o, i), reads=["B:tkw"], writes=["B:wcolb"])

    CMAX = CAP[0]
    alloc("gatT", 8 * CMAX * 2, [f"gatT_l{i}" for i in range(TK[0])])
    alloc("rw1f0", 8 * HF * 2, [f"rw1f0_d{d}" for d in range(8)])
    alloc("rw1f1", 8 * HF * 2, [f"rw1f1_d{d}" for d in range(8)])
    alloc("rw2f", nht * 1024 * 2, [f"rw2f_c{c}" for c in range(8)])
    free("bufB0", ["B:bufB0"]); free("bufB1", ["B:bufB1"])
    alloc("h1sg", 16 * 512 * 2, ["h1sg"])
    alloc("gati0", 1024 * 2, ["gati0"]); alloc("gati1", 1024 * 2, ["gati1"])
    alloc("gati2", 1024 * 2, ["gati2"])
    alloc("scl0", 2048, ["scl0"]); alloc("scl1", 2048, ["scl1"])
    gatT_ap = view("gatT", BF16)
    rw1f_aps = [view("rw1f0", BF16), view("rw1f1", BF16)]
    rw2f_ap = view("rw2f", BF16)
    h1sg_ap = view("h1sg", BF16)
    gati_aps = [view("gati0", BF16, cols=1024), view("gati1", BF16, cols=1024),
                view("gati2", BF16, cols=1024)]
    scl_aps = [view("scl0", BF16, cols=1024), view("scl1", BF16, cols=1024)]

    def seldma(kk, dst_ap, proto_ap, unit_stride, writes, tag, reads=()):
        def fn(e_, sem, _st=unit_stride, _p=proto_ap, _d=dst_ap, _kk=kk):
            with e_.register() as r:
                e_.load(r, sig_i[0:1, _kk:_kk + 1])
                e_.reg_add(r, r, 1)
                e_.reg_mul(r, r, _st)
                sv = e_.compute_val(r, donate=True)
                sap = bass.AP(_p.tensor, sv, list(_p.ap))
                e_.dma_start(_d, sap).then_inc(sem, 16)
        P.dma_selfinc("gpsimd", fn, reads=["sig"] + list(reads),
                      writes=writes, tag=tag)

    _w1p = d_w1pre8[0].rearrange("(d p) f -> p d f", p=128)
    _w2p = d_w2pre8[0].rearrange("(t p) f -> p t f", p=128)
    _b1p = d_b1.bitcast(U8)[0]
    _b2p = d_b2.bitcast(U8)[0:1, :]
    pre1r = [f"w1pre_{u}_{c}" for u in range(1, nunits) for c in range(8)]
    pre2r = [f"w2pre_{u}_{c}" for u in range(1, nunits) for c in range(8)]

    def seldma_off(kk, dst_ap, proto_tensor, proto_pat, unit_stride, off,
                   writes, tag, reads=()):
        def fn(e_, sem, _st=unit_stride, _o=off, _d=dst_ap, _kk=kk,
               _pt=proto_tensor, _pp=proto_pat):
            with e_.register() as r:
                e_.load(r, sig_i[0:1, _kk:_kk + 1])
                e_.reg_add(r, r, 1)
                e_.reg_mul(r, r, _st)
                if _o:
                    e_.reg_add(r, r, _o)
                sv = e_.compute_val(r, donate=True)
                sap = bass.AP(_pt, sv, _pp)
                e_.dma_start(_d, sap).then_inc(sem, 16)
        P.dma_selfinc("gpsimd", fn, reads=["sig"] + list(reads),
                      writes=writes, tag=tag)

    def exp_rw1_chunk(kk, dt):
        dst = rw1f_aps[kk % 2][:, dt * HF:(dt + 1) * HF].bitcast(U8)
        w = wreg(f"rw1f{kk % 2}", [f"rw1f{kk % 2}_d{dt}"]) if dt == 0 else             [f"rw1f{kk % 2}_d{dt}"]
        seldma_off(kk, dst, _w1p.tensor, [[2 * HF, 128], [1, 2 * HF]],
                   2 * D * HF, dt * 128 * 2 * HF, writes=w,
                   tag=f"xw1_{kk}_{dt}", reads=pre1r)

    def exp_rw2_chunk(kk, c):
        dst = rw2f_ap[:, c * 2048:(c + 1) * 2048].bitcast(U8).rearrange(
            "p (t f) -> p t f", f=2 * D)
        w = wreg("rw2f", [f"rw2f_c{c}"]) if c == 0 else [f"rw2f_c{c}"]
        seldma_off(kk, dst, _w2p.tensor,
                   [[2 * D, 128], [128 * 2 * D, 2], [1, 2 * D]],
                   2 * HF * D, c * 256 * 2 * D, writes=w,
                   tag=f"xw2_{kk}_{c}", reads=pre2r)

    for dt in range(ndt):
        exp_rw1_chunk(0, dt)
    flatb_regs = [f"flatb_{st}" for st in range(nst)]
    for kk in range(NE):
        Ck, Tk, base = CAP[kk], TK[kk], UOFF[kk] // 128
        rw2q = [lambda c=c: exp_rw2_chunk(kk, c) for c in range(8)]
        fetchq = []
        seldma(kk, b1_aps[kk % 2].bitcast(U8), _b1p, 128 * nht * 4,
               writes=wreg(f"b1t{kk % 2}", [f"b1buf{kk % 2}"]), tag=f"xb1_{kk}")
        seldma(kk, b2_aps[kk % 2].bitcast(U8), _b2p, D * 4,
               writes=wreg(f"b2r{kk % 2}", [f"b2buf{kk % 2}"]), tag=f"xb2_{kk}")
        if kk + 1 < NE:
            fetchq += [lambda dt=dt, nk=kk + 1: exp_rw1_chunk(nk, dt)
                       for dt in range(ndt)]
        groups = [list(range(i, min(i + 4, Tk))) for i in range(0, Tk, 4)]
        prate = -(-len(fetchq) // max(len(groups) - 1, 1))
        first_group = True
        for sg in groups:
            for i in sg:
                g = base + i
                gb = gati_aps[g % 3]
                P.dma("gpsimd",
                      lambda e, o=gb, ix=toki_ap[:, g:g + 1]:
                      e.indirect_dma_start(o, None, d_flatb,
                                           IndirectOffsetOnAxis(ix, 0)),
                      reads=["B:toki"] + flatb_regs, writes=[f"gati{g % 3}"],
                      tag=f"gth{kk}_{i}")
                for dt in range(ndt):
                    bank = dt % 2
                    transpose(pbf16(bank), gb[:, dt * 128:(dt + 1) * 128],
                              id_bf_ap,
                              reads=[f"gati{g % 3}", "B:id_bf"],
                              writes=[f"pbk{bank}"])
                    copy_on("scalar",
                            gatT_ap[:, dt * CMAX + i * 128: dt * CMAX + i * 128 + 128],
                            pbf16(bank), reads=[f"pbk{bank}"],
                            writes=[f"gatT_l{i}"])
            if first_group:
                for f in rw2q:
                    f()
                first_group = False
            else:
                for _ in range(prate):
                    if fetchq:
                        fetchq.pop(0)()
            W = 128 * len(sg)
            i0 = sg[0]
            for ht in range(nht):
                bank = 2 + ht % 2
                for dt in range(ndt):
                    mm(pb[bank][:, 0:W],
                       rw1f_aps[kk % 2][:, dt * HF + ht * 128: dt * HF + ht * 128 + 128],
                       gatT_ap[:, dt * CMAX + i0 * 128: dt * CMAX + i0 * 128 + W],
                       start=(dt == 0), stop=(dt == ndt - 1),
                       reads=[f"rw1f{kk % 2}_d{dt}"] + [f"gatT_l{i}" for i in sg],
                       writes=[f"pbk{bank}"], tag=f"xh1_{kk}_{i0}_{ht}")
                P.add("scalar",
                      lambda e, o=h1sg_ap[:, ht * 512: ht * 512 + W],
                      i=pb[bank][:, 0:W],
                      b=b1_aps[kk % 2][:, ht:ht + 1]: e.activation(
                          o, i, AF.Gelu, bias=b),
                      reads=[f"pbk{bank}", f"b1buf{kk % 2}"],
                      writes=["h1sg"], tag=f"xgl_{kk}_{i0}_{ht}")
            for tl, i in enumerate(sg):
                g = base + i
                sclb = scl_aps[g % 2]
                for dc in range(ndc):
                    bank = 4 + (2 * tl + dc) % 4
                    for ht in range(nht):
                        mm(pb[bank],
                           h1sg_ap[:, ht * 512 + tl * 128: ht * 512 + tl * 128 + 128],
                           rw2f_ap[:, ht * 1024 + dc * 512: ht * 1024 + (dc + 1) * 512],
                           start=(ht == 0), stop=False,
                           reads=["h1sg", f"rw2f_c{ht // 2}"],
                           writes=[f"pbk{bank}"],
                           tag=f"xo2_{kk}_{i}_{dc}")
                    mm(pb[bank], onesf[0:1, 0:128],
                       b2_aps[kk % 2][:, dc * 512:(dc + 1) * 512],
                       False, True, reads=[f"b2buf{kk % 2}", "B:ones_f"],
                       writes=[f"pbk{bank}"], tag=f"xo2b_{kk}_{i}_{dc}")
                    P.add("vector",
                          lambda e, o=sclb[:, dc * 512:(dc + 1) * 512], bk=bank,
                          w=wcol_ap[:, g:g + 1]: e.tensor_scalar(
                              o, pb[bk], w, None, op0=ALU.mult),
                          reads=[f"pbk{bank}", "B:wcolb"],
                          writes=[f"scl{g % 2}"], tag=f"xsc_{kk}_{i}_{dc}")
                P.dma("sync",
                      lambda e, i_=sclb, gg=g: e.dma_start(
                          d_expout[gg * 128:(gg + 1) * 128, :], i_),
                      reads=[f"scl{g % 2}"], writes=[f"expout_{g}"],
                      tag=f"xw_{kk}_{i}")
        while fetchq:
            fetchq.pop(0)()

    free("gatT", [f"gatT_l{i}" for i in range(TK[0])])
    free("rw1f0", [f"rw1f0_d{dt}" for dt in range(8)])
    free("rw1f1", [f"rw1f1_d{dt}" for dt in range(8)])
    free("rw2f", [f"rw2f_c{c}" for c in range(8)])
    free("h1sg", ["h1sg"])
    alloc("bufB0", 4096); alloc("bufB1", 4096)
    free("gati0", ["gati0"]); free("gati1", ["gati1"]); free("gati2", ["gati2"])
    free("b1t0", ["B:b1t0", "b1buf0"]); free("b1t1", ["B:b1t1", "b1buf1"])
    free("b2r0", ["B:b2r0", "b2buf0"]); free("b2r1", ["B:b2r1", "b2buf1"])

    alloc("blk0", 1024 * 2); alloc("blk1", 1024 * 2)
    alloc("blk2", 1024 * 2); alloc("blk3", 1024 * 2)
    alloc("blk4", 1024 * 2); alloc("blk5", 1024 * 2)
    alloc("ptb0", 128 * 2); alloc("ptb1", 128 * 2)
    alloc("pbufb0", 128 * 2); alloc("pbufb1", 128 * 2)
    blk_aps = [view(f"blk{j}", BF16, cols=1024) for j in range(6)]
    ptb_aps = [view("ptb0", BF16, cols=128), view("ptb1", BF16, cols=128)]
    pbufb_aps = [view("pbufb0", BF16, cols=128), view("pbufb1", BF16, cols=128)]
    exr = [f"expout_{g}" for g in range(NTILE)]

    def p6_expand(tt):
        par = tt % 2
        a, bB = f"bufA{par}", f"bufB{par}"
        a_ap, b_ap = view(a), view(bB)
        mx_ap = view(f"mx{par}")
        P.dma("sync",
              lambda e, o=a_ap, s=tt: e.dma_start(o, d_xsp[s * 128:(s + 1) * 128, :]),
              reads=[f"xsp_{tt}"], writes=wreg(a), tag=f"x2re{tt}")
        P.dma("sync",
              lambda e, o=mx_ap, s=tt: e.dma_start(o, d_mixh[s * 128:(s + 1) * 128, :]),
              reads=[f"mixh_{tt}"], writes=wreg(f"mx{par}"), tag=f"mxre{tt}")
        for ee in range(NE):
            bj = (tt * NE + ee) % 6
            pp = ee % 2
            pbufb_ap = pbufb_aps[pp]
            ptb_ap = ptb_aps[pp]
            def blfn(e_, sem, _tt=tt, _ee=ee, _dst=blk_aps[bj].bitcast(U8)):
                with e_.register() as r:
                    e_.load(r, bas3_i[0:1, _tt * NE + _ee:_tt * NE + _ee + 1])
                    sv = e_.compute_val(r, donate=True)
                    sap = bass.AP(d_expout8.tensor, sv, [[2 * D, 128], [1, 2 * D]])
                    e_.dma_start(_dst, sap).then_inc(sem, 16)
            P.dma_selfinc("gpsimd", blfn,
                          reads=[f"bas_{tt}"] + exr, writes=[f"blk{bj}"],
                          tag=f"bl{tt}_{ee}")
            P.add("vector",
                  lambda e, o=pbufb_ap, m=psf_ap[:, tt * NE + ee:tt * NE + ee + 1]:
                  e.tensor_scalar(o, irep_ap, m, None, op0=ALU.is_equal),
                  reads=[f"psf_{tt}", "B:irep"], writes=[f"B:pbufb{pp}"],
                  tag=f"pb6_{tt}_{ee}")
            bank6 = 2 + ee % 2
            transpose(pbf16(bank6), pbufb_ap, id_bf_ap,
                      reads=[f"B:pbufb{pp}", "B:id_bf"], writes=[f"pbk{bank6}"],
                      tag=f"pt6_{tt}_{ee}")
            copy_on("scalar", ptb_ap, pbf16(bank6), reads=[f"pbk{bank6}"],
                    writes=[f"B:ptb{pp}"], tag=f"pe6_{tt}_{ee}")
            for dc in range(ndc):
                bank = 4 + (2 * tt + dc) % 4
                mm(pb[bank], ptb_ap, blk_aps[bj][:, dc * 512:(dc + 1) * 512],
                   start=(ee == 0), stop=(ee == NE - 1),
                   reads=[f"B:ptb{pp}", f"blk{bj}"], writes=[f"pbk{bank}"],
                   tag=f"xp6_{tt}_{ee}_{dc}")

    def p6_finish(tt):
        par = tt % 2
        a, bB = f"bufA{par}", f"bufB{par}"
        a_ap, b_ap = view(a), view(bB)
        mx_ap = view(f"mx{par}")
        for dc in range(ndc):
            bank = 4 + (2 * tt + dc) % 4
            P.add("vector",
                  lambda e, o=b_ap[:, dc * 512:(dc + 1) * 512], bk=bank,
                  m=mx_ap[:, dc * 512:(dc + 1) * 512]: e.tensor_add(o, pb[bk], m),
                  reads=[f"pbk{bank}", f"B:mx{par}"], writes=[f"B:{bB}"],
                  tag=f"mxs6_{tt}_{dc}")
        cm, cr, mr = ln_stats(b_ap, [f"B:{bB}"], f"p5{par}_", "sq0")
        P.add("vector",
              lambda e, o=b_ap, m=cm, r=cr: e.tensor_scalar(
                  o, o, m, r, op0=ALU.subtract, op1=ALU.mult),
              reads=[f"B:{bB}"] + mr, writes=[f"B:{bB}"])
        P.add("vector", lambda e, o=b_ap, i=a_ap: e.tensor_add(o, o, i),
              reads=[f"B:{bB}", f"B:{a}"], writes=[f"B:{bB}"])
        P.dma("sync",
              lambda e, i=b_ap, s=tt: e.dma_start(d_y[s * 128:(s + 1) * 128, :], i),
              reads=[f"B:{bB}"], writes=[f"y_{tt}"], tag=f"yout{tt}")

    for tt in range(nst):
        p6_expand(tt)
        if tt > 0:
            p6_finish(tt - 1)
    p6_finish(nst - 1)

    with nc.Block() as blk:
        P.emit(blk, sems)
    es.close()
    if ret_prog:
        return nc, P
    return nc


_CACHE = {}


def _rope_tables(S, half=32, theta=10000.0, scale=1.0):
    inv = 1.0 / theta ** (np.arange(half, dtype=np.float64) / half)
    fr = np.arange(S, dtype=np.float64)[:, None] * inv[None, :]
    return (np.cos(fr) * scale).astype(np.float32), \
           (np.sin(fr) * scale).astype(np.float32)


def kernel(L0_attn_x, L0_attn_ln_g, L0_attn_ln_b, L0_attn_w_qkv, L0_attn_w_out,
           L0_attn_b_out, L0_ffn_router_w, L0_ffn_router_b, L0_ffn_sw1,
           L0_ffn_sb1, L0_ffn_sw2, L0_ffn_sb2, L0_ffn_rw1, L0_ffn_rb1,
           L0_ffn_rw2, L0_ffn_rb2):
    B, S, D = L0_attn_x.shape
    NE = L0_ffn_rw1.shape[0]
    HF = L0_ffn_sw1.shape[1]
    nunits = NE + 1
    nht = HF // 128

    if "nc" not in _CACHE:
        _CACHE["nc"] = build_program(S=S, D=D, NH=16, HD=64, HF=HF, NE=NE)
    nc = _CACHE["nc"]

    f32 = np.float32
    asc = np.ascontiguousarray
    cosn, sinn = _rope_tables(S, scale=1.0)
    cosq, sinq = _rope_tables(S, scale=0.125)

    wqkvT = asc(np.asarray(L0_attn_w_qkv, f32).T)
    woutT = asc(np.asarray(L0_attn_w_out, f32).T)
    lng_rep = asc(np.broadcast_to(np.asarray(L0_attn_ln_g, f32), (128, D)))
    lnb_rep = asc(np.broadcast_to(np.asarray(L0_attn_ln_b, f32), (128, D)))
    b_out = asc(np.asarray(L0_attn_b_out, f32).reshape(1, D))
    rwq_pad = np.zeros((D, 128), f32)
    rwq_pad[:, :NE] = np.asarray(L0_ffn_router_w, f32)
    rbq_rep = asc(np.broadcast_to(np.asarray(L0_ffn_router_b, f32), (128, NE)))
    ident = np.eye(128, dtype=f32)
    w1_all = asc(np.concatenate(
        [np.asarray(L0_ffn_sw1, f32)[None], np.asarray(L0_ffn_rw1, f32)], axis=0))
    w2_all = asc(np.concatenate(
        [np.asarray(L0_ffn_sw2, f32)[None], np.asarray(L0_ffn_rw2, f32)], axis=0))
    b1_all = asc(np.concatenate(
        [np.asarray(L0_ffn_sb1, f32)[None], np.asarray(L0_ffn_rb1, f32)],
        axis=0).reshape(nunits, nht, 128).transpose(0, 2, 1))
    b2_all = asc(np.concatenate(
        [np.asarray(L0_ffn_sb2, f32)[None], np.asarray(L0_ffn_rb2, f32)], axis=0))

    nst = S // 128
    tri_u = np.triu(np.ones((128, 128), f32), 1)
    iota_tok = (np.arange(128, dtype=f32)[:, None]
                + 128.0 * np.arange(nst, dtype=f32)[None, :])
    iota_tok = asc(iota_tok.astype(f32))
    eidx = np.arange(NE, dtype=f32).reshape(1, NE)
    etag = (np.arange(NE, dtype=f32) / 16.0).reshape(1, NE)
    iota_rep = asc(np.broadcast_to(np.arange(128, dtype=f32)[None, :], (128, 128)))
    shared = {
        "wqkvT": wqkvT, "woutT": woutT, "lng_rep": lng_rep, "lnb_rep": lnb_rep,
        "b_out": b_out, "cosq": cosq, "sinq": sinq, "cosk": cosn, "sink": sinn,
        "ident": ident, "rwq_pad": rwq_pad, "rbq_rep": rbq_rep,
        "w1_all": w1_all, "b1_all": b1_all, "w2_all": w2_all, "b2_all": b2_all,
        "tri_u": tri_u, "iota_tok": iota_tok, "eidx": eidx, "etag": etag,
        "iota_rep": iota_rep,
    }
    x = np.asarray(L0_attn_x, f32)
    in_maps = []
    for c in range(B):
        m = dict(shared)
        m["x"] = asc(x[c])
        in_maps.append(m)

    res = run_bass_kernel_spmd(nc, in_maps, core_ids=list(range(B)))
    out = np.stack([np.asarray(res.results[c]["y"]) for c in range(B)], axis=0)
    return out
